# version 06
# baseline (speedup 1.0000x reference)
import contextlib
import numpy as np
import ml_dtypes
import concourse.bass as bass
import concourse.mybir as mybir
from concourse.bass_utils import run_bass_kernel_spmd

F32 = mybir.dt.float32
BF16 = mybir.dt.bfloat16
AF = mybir.ActivationFunctionType
ALU = mybir.AluOpType
AX = mybir.AxisListType

B, S, D = 4, 4096, 2048
H, HD, KVH = 16, 128, 4
NB = S // 128
NOB = 16
NIC = D // 128
EPS_LN = 1e-5
EPS_RMS = 1.1920929e-7
C23 = 1.5 * 2.0 ** 23
CDELTA = 1.5 * 2.0 ** 52 - 1.5 * 2.0 ** 23
SQ128 = float(np.sqrt(128.0))
TWO_PI = 2.0 * np.pi
INV2PI = 1.0 / TWO_PI
RC1 = 6.28125
RC2 = float(np.float64(TWO_PI) - RC1)
PI_CLAMP = 3.1415925
LN1E4 = float(np.log(10000.0))

ENGS = ("tensor", "vector", "scalar", "sync", "gpsimd")
DMA_SEMS = {"sync": ["Yd%d" % i for i in range(6)], "scalar": ["Cd0"], "gpsimd": ["Gd0"]}
ALL_DMA_SEMS = [s for v in DMA_SEMS.values() for s in v]
ENG_SEM = {"tensor": "T", "vector": "V", "scalar": "C", "sync": "Y", "gpsimd": "G"}


def own_blocks(h):
    return sorted([2 * i + h for i in range(8)] + [16 + 2 * i + (1 - h) for i in range(8)])


class Sched:
    def __init__(self):
        self.items = {e: [] for e in ENGS}
        self.cnt = {s: 0 for s in list(ENG_SEM.values()) + ALL_DMA_SEMS}
        self.rr = {e: 0 for e in DMA_SEMS}

    def wait(self, eng, deps):
        for d in deps:
            if d is not None:
                self.items[eng].append(("w", d[0], d[1]))

    def emit(self, eng, kind, fn, deps):
        st = self.items[eng]
        for d in deps:
            if d is not None:
                st.append(("w", d[0], d[1]))
        if kind == "dma":
            sl = DMA_SEMS[eng]
            sem = sl[self.rr[eng] % len(sl)]
            self.rr[eng] += 1
            if self.cnt[sem] > 0:
                st.append(("w", sem, self.cnt[sem]))
            st.append((kind, fn, sem))
            self.cnt[sem] += 16
            return (sem, self.cnt[sem])
        st.append((kind, fn))
        sem = ENG_SEM[eng]
        self.cnt[sem] += 1
        return (sem, self.cnt[sem])

    def barrier(self):
        snap = dict(self.cnt)
        for e in ENGS:
            for s, v in snap.items():
                if v > 0:
                    self.items[e].append(("w", s, v))


class Buf:
    __slots__ = ("w", "rs")

    def __init__(self):
        self.w = None
        self.rs = []


S_ = None


def _emit(eng, kind, fn, r, w):
    deps = [b.w for b in r]
    for b in w:
        deps.append(b.w)
        deps.extend(b.rs)
    tok = S_.emit(eng, kind, fn, deps)
    for b in r:
        b.rs.append(tok)
    for b in w:
        b.w = tok
        b.rs = []
    return tok


def op(eng, fn, r=(), w=()):
    return _emit(eng, "op", fn, r, w)


def opg(eng, fns, r=(), w=()):
    return _emit(eng, "opg", fns, r, w)


def dma(eng, fn, r=(), w=()):
    return _emit(eng, "dma", fn, r, w)


def build_nc():
    global S_
    S_ = Sched()
    nc = bass.Bass("TRN2", target_bir_lowering=False)

    def din(name, shape, dt=F32):
        return nc.dram_tensor(name, list(shape), dt, kind="ExternalInput").ap()

    x_full = din("x_full", [S, D])
    x_own = din("x_own", [NOB * 128, D])
    wqkv_T = din("wqkv_T", [D, 3072])
    wqkv_R = din("wqkv_R", [3072, D])
    wo_T = din("wo_T", [D, D])
    wo_R = din("wo_R", [D, D])
    ffn_T = din("ffn_T", [D, D])
    ffn_R = din("ffn_R", [D, D])
    gains_h = din("gains", [1, H])
    idnb_h = din("idnb", [128, 128], BF16)
    idnf_h = din("idnf", [128, 128], F32)
    ones_r = din("ones_r", [1, 128])
    ones_c = din("ones_c", [128, 1])
    tpos_h = din("tpos", [128, NB])
    tqpos_h = din("tqpos", [128, NOB])
    ifreq_h = din("ifreq", [1, 64])
    iidx_h = din("iidx", [1, 64])
    masks_h = din("masks", [32, 128, 128], BF16)

    out_h = nc.dram_tensor("out_shard", [NOB * 128, D], F32, kind="ExternalOutput").ap()
    sc_qt = nc.dram_tensor("sc_qt", [H, 128, NOB * 128], BF16, kind="Internal").ap()
    sc_g = nc.dram_tensor("sc_g", [NOB, 128, D], BF16, kind="Internal").ap()
    sc_woq = nc.dram_tensor("sc_woq", [NIC, 128, D], BF16, kind="Internal").ap()
    sc_xq = nc.dram_tensor("sc_xq", [NOB, NIC, 128, 128], BF16, kind="Internal").ap()

    es = contextlib.ExitStack()

    AB = es.enter_context(nc.sbuf_tensor("AB", [128, 72832], BF16))
    AFR = es.enter_context(nc.sbuf_tensor("AFR", [128, 15360], F32))

    def vb(arena, c0, n, p0=0, p1=128):
        return arena[p0:p1, c0:c0 + n], Buf()

    ktf, ktf_b = vb(AB, 0, KVH * S)
    vaug, vaug_b = vb(AB, 16384, KVH * NB * 129)
    wkvq, wkvq_b = vb(AB, 32896, NIC * 1024)
    xqs0, xqs0_b = vb(AB, 32896, NIC * 128)
    xqs1, xqs1_b = vb(AB, 34944, NIC * 128)
    gbf, gbf_b = vb(AB, 32896, 4 * D)
    wqg, wqg_b = vb(AB, 41088, NIC * 512)
    xq, xq_b = vb(AB, 49280, D)
    xqt_t, xqt_t_b = vb(AB, 51328, NIC * 128)
    kbf, kbf_b = vb(AB, 53376, 512)
    qt_t, qt_t_b = vb(AB, 49280, H * 512)
    yq, yq_b = vb(AB, 57472, D)
    e0, e0_b = vb(AB, 59520, 512)
    e1, e1_b = vb(AB, 60032, 512)
    yqt, yqt_b = vb(AB, 60544, 4 * NIC * 128)
    masksb, masksb_b = vb(AB, 68736, 32 * 128)

    wld0, wld0_b = vb(AFR, 0, 2048)
    wld1, wld1_b = vb(AFR, 2048, 2048)
    wtt0, wtt0_b = vb(AFR, 4096, 512)
    wtt1, wtt1_b = vb(AFR, 4608, 512)
    xt0, xt0_b = vb(AFR, 0, 2048)
    xt1, xt1_b = vb(AFR, 2048, 2048)
    xn, xn_b = vb(AFR, 5120, 2048)
    tq, tq_b = vb(AFR, 7168, 2048)
    kwork, kwork_b = vb(AFR, 9216, 512)
    kwork2, kwork2_b = vb(AFR, 9728, 512)
    wskv_rep, wskv_rep_b = vb(AFR, 10240, 1024)
    cosT, cosT_b = vb(AFR, 11264, NB * 64)
    sinT, sinT_b = vb(AFR, 13312, NB * 64)
    cosQ, cosQ_b = vb(AFR, 11264, NOB * 64)
    sinQ, sinQ_b = vb(AFR, 12288, NOB * 64)
    wsqd, wsqd_b = vb(AFR, 13312, 512)
    y_tm, y_tm_b = vb(AFR, 0, 4 * D)
    wso_rep, wso_rep_b = vb(AFR, 9216, D)
    sdqt, sdqt_b = vb(AFR, 11264, 512)
    xrt, xrt_b = vb(AFR, 11776, 512)
    outt, outt_b = vb(AFR, 12288, 512)
    gft, gft_b = vb(AFR, 12800, 512)
    tqT, tqT_b = vb(AFR, 13312, 2048)
    rowstg1, rowstg1_b = vb(AFR, 5120, 512, 0, 1)
    rowstgi1, rowstgi1_b = vb(AFR, 5632, 512, 0, 1)
    rowstg3, rowstg3_b = vb(AFR, 10240, 512, 0, 1)
    rowstgi3, rowstgi3_b = vb(AFR, 10752, 512, 0, 1)

    def sbf(name, p, f, dt=F32):
        return es.enter_context(nc.sbuf_tensor(name, [p, f], dt)), Buf()

    adeq, adeq_b = sbf("adeq", 128, NB)
    adeqo, adeqo_b = sbf("adeqo", 128, NOB)
    afo, afo_b = sbf("afo", 128, NOB)
    kk, kk_b = sbf("kk", 128, 1)
    ce, ce_b = sbf("ce", 128, 1)
    idn, idn_b = sbf("idn", 128, 128, BF16)
    idnf, idnf_b = sbf("idnf_s", 128, 128)
    onr, onr_b = sbf("onr", 1, 128)
    onc, onc_b = sbf("onc", 128, 1)
    ifr, ifr_b = sbf("ifr", 128, 64)
    mneg, mneg_b = sbf("mneg", 128, H)
    gn, gn_b = sbf("gn", 1, H)
    gnb, gnb_b = sbf("gnb", 128, H)
    wscol, wscol_b = sbf("wscol", 128, 24)
    colA, colA_b = sbf("colA", 128, 1)
    colB, colB_b = sbf("colB", 128, 1)
    colC, colC_b = sbf("colC", 128, 1)
    colS, colS_b = sbf("colS", 128, 4)
    adqy, adqy_b = sbf("adqy", 128, 4)
    sins, sins_b = sbf("sins", 128, 128)
    ang, ang_b = sbf("ang", 128, 64)
    ang2, ang2_b = sbf("ang2", 128, 64)
    angk, angk_b = sbf("angk", 128, 64)
    angt, angt_b = sbf("angt", 128, 64)
    ifh, ifh_b = sbf("ifh", 1, 64)
    ifd, ifd_b = sbf("ifd", 1, 64)
    tmpr, tmpr_b = sbf("tmpr", 128, 64)
    tmp2, tmp2_b = sbf("tmp2", 1, 16)
    tmp11, tmp11_b = sbf("tmp11", 1, 1)
    tposb, tposb_b = sbf("tposb", 128, NB)
    tqposb, tqposb_b = sbf("tqposb", 128, NOB)
    zcol, zcol_b = sbf("zcol", 128, 4 * H)
    zinv, zinv_b = sbf("zinv", 128, 4 * H)
    wsf_c, wsf_c_b = sbf("wsf_c", 128, 1)
    wsfd, wsfd_b = sbf("wsfd", 128, 1)

    ps, ps_b = [], []
    for i in range(8):
        ps.append(es.enter_context(nc.psum_tensor("ps%d" % i, [128, 512], F32)))
        ps_b.append(Buf())

    with es:
        dma("sync", lambda e: e.dma_start(out=idn[:], in_=idnb_h[:]), w=[idn_b])
        dma("sync", lambda e: e.dma_start(out=idnf[:], in_=idnf_h[:]), w=[idnf_b])
        dma("sync", lambda e: e.dma_start(out=onr[:], in_=ones_r[:]), w=[onr_b])
        dma("sync", lambda e: e.dma_start(out=onc[:], in_=ones_c[:]), w=[onc_b])
        dma("sync", lambda e: e.dma_start(out=gn[:], in_=gains_h[:]), w=[gn_b])
        dma("sync", lambda e: e.dma_start(out=ifh[:], in_=ifreq_h[:]), w=[ifh_b])
        dma("sync", lambda e: e.dma_start(out=ifd[:], in_=iidx_h[:]), w=[ifd_b])
        dma("sync", lambda e: e.dma_start(out=tposb[:], in_=tpos_h[:]), w=[tposb_b])
        dma("sync", lambda e: e.dma_start(out=tqposb[:], in_=tqpos_h[:]), w=[tqposb_b])

        op("vector", lambda v: v.memset(kk[:], 1.0), w=[kk_b])
        op("vector", lambda v: v.tensor_scalar(kk[:], kk[:], 2.0 ** -25, None, ALU.add), r=[kk_b], w=[kk_b])
        op("vector", lambda v: v.tensor_scalar(kk[:], kk[:], -1.0, None, ALU.add), r=[kk_b], w=[kk_b])
        op("vector", lambda v: v.tensor_scalar(kk[:], kk[:], 2.0 ** 25, None, ALU.mult), r=[kk_b], w=[kk_b])
        op("vector", lambda v: v.tensor_scalar(ce[:], kk[:], CDELTA, C23, ALU.mult, ALU.add), r=[kk_b], w=[ce_b])

        op("scalar", lambda c: c.activation(ifd[:], ifd[:], AF.Exp, scale=-LN1E4 / 128.0), r=[ifd_b], w=[ifd_b])
        op("vector", lambda v: v.tensor_tensor(ifd[:], ifd[:], ifh[:], ALU.subtract), r=[ifd_b, ifh_b], w=[ifd_b])
        op("tensor", lambda t: t.matmul(ps[6][:, 0:64], onr[:], ifh[:], start=True, stop=True), r=[onr_b, ifh_b], w=[ps_b[6]])
        op("vector", lambda v: v.tensor_copy(ifr[:], ps[6][:, 0:64]), r=[ps_b[6]], w=[ifr_b])
        op("tensor", lambda t: t.matmul(ps[6][:, 0:64], onr[:], ifd[:], start=True, stop=True), r=[onr_b, ifd_b], w=[ps_b[6]])
        op("vector", lambda v: v.tensor_copy(tmpr[:], ps[6][:, 0:64]), r=[ps_b[6]], w=[tmpr_b])
        op("vector", lambda v: v.tensor_scalar(tmpr[:], tmpr[:], kk[:], None, ALU.mult), r=[tmpr_b, kk_b], w=[tmpr_b])
        op("vector", lambda v: v.tensor_tensor(ifr[:], ifr[:], tmpr[:], ALU.add), r=[ifr_b, tmpr_b], w=[ifr_b])

        op("scalar", lambda c: c.activation(tmp2[:], gn[:], AF.Abs), r=[gn_b], w=[tmp2_b])
        op("tensor", lambda t: t.matmul(ps[6][:, 0:16], onr[:], tmp2[:], start=True, stop=True), r=[onr_b, tmp2_b], w=[ps_b[6]])
        op("vector", lambda v: v.tensor_scalar(mneg[:], ps[6][:, 0:16], -11.35, -2.0, ALU.mult, ALU.add), r=[ps_b[6]], w=[mneg_b])
        op("tensor", lambda t: t.matmul(ps[6][:, 0:16], onr[:], gn[:], start=True, stop=True), r=[onr_b, gn_b], w=[ps_b[6]])
        op("vector", lambda v: v.tensor_scalar(gnb[:], ps[6][:, 0:16], 1.0 / SQ128, None, ALU.mult), r=[ps_b[6]], w=[gnb_b])

        op("vector", lambda v: v.memset(vaug[:], 1.0), w=[vaug_b])

        def rope_tables(n, tcol_t, tcol_bb, cos_t, cos_bb, sin_t, sin_bb):
            for blk in range(n):
                cs = slice(blk * 64, blk * 64 + 64)
                op("vector", lambda v, blk=blk: v.tensor_scalar(
                    ang[:], ifr[:], tcol_t[:, blk:blk + 1], None, ALU.mult), r=[ifr_b, tcol_bb], w=[ang_b])
                for tgt, tgtb, shift in ((sin_t, sin_bb, 0.0), (cos_t, cos_bb, np.pi / 2)):
                    if shift:
                        op("vector", lambda v: v.tensor_scalar(ang2[:], ang[:], shift, None, ALU.add), r=[ang_b], w=[ang2_b])
                        src, srcb = ang2, ang2_b
                    else:
                        src, srcb = ang, ang_b
                    op("vector", lambda v, src=src: v.tensor_scalar(angk[:], src[:], INV2PI, None, ALU.mult), r=[srcb], w=[angk_b])
                    op("vector", lambda v: v.tensor_scalar(angk[:], angk[:], ce[:], None, ALU.add), r=[angk_b, ce_b], w=[angk_b])
                    op("vector", lambda v: v.tensor_scalar(angk[:], angk[:], ce[:], None, ALU.subtract), r=[angk_b, ce_b], w=[angk_b])
                    op("vector", lambda v: v.tensor_scalar(angt[:], angk[:], RC1, None, ALU.mult), r=[angk_b], w=[angt_b])
                    op("vector", lambda v, src=src: v.tensor_tensor(angt[:], src[:], angt[:], ALU.subtract), r=[srcb, angt_b], w=[angt_b])
                    op("vector", lambda v: v.tensor_scalar(angk[:], angk[:], RC2, None, ALU.mult), r=[angk_b], w=[angk_b])
                    op("vector", lambda v: v.tensor_tensor(angt[:], angt[:], angk[:], ALU.subtract), r=[angt_b, angk_b], w=[angt_b])
                    op("vector", lambda v: v.tensor_scalar(angt[:], angt[:], PI_CLAMP, -PI_CLAMP, ALU.min, ALU.max), r=[angt_b], w=[angt_b])
                    op("scalar", lambda c, tgt=tgt, cs=cs: c.activation(tgt[:, cs], angt[:], AF.Sin), r=[angt_b], w=[tgtb])

        def rowscale_pass(src_h, nchunks):
            for c in range(nchunks):
                wb, wbb = (wld0, wld0_b) if c % 2 == 0 else (wld1, wld1_b)
                dma("sync", lambda e, c=c, wb=wb: e.dma_start(
                    out=wb[:], in_=src_h[c * 128:(c + 1) * 128, :]), w=[wbb])
                op("vector", lambda v, wb=wb: v.reduce_max(colA[:], wb[:], axis=AX.X, apply_absolute_value=True), r=[wbb], w=[colA_b])
                op("vector", lambda v, c=c: v.tensor_scalar(wscol[:, c:c + 1], colA[:], 1.0 / 127.0, 1e-5, ALU.mult, ALU.max), r=[colA_b], w=[wscol_b])

        def build_row(chunks, rstg, rstg_b, rstgi, rstgi_b):
            for i, c in enumerate(chunks):
                op("tensor", lambda t, c=c: t.matmul(ps[7][0:1, 0:128], wscol[:, c:c + 1], idnf[:], start=True, stop=True), r=[wscol_b, idnf_b], w=[ps_b[7]])
                op("vector", lambda v, i=i, rstg=rstg: v.tensor_copy(rstg[:, i * 128:(i + 1) * 128], ps[7][0:1, 0:128]), r=[ps_b[7]], w=[rstg_b])
            op("vector", lambda v, rstg=rstg, rstgi=rstgi: v.reciprocal(rstgi[:], rstg[:]), r=[rstg_b], w=[rstgi_b])

        def rep_to_ps(bank, row_t, row_bb):
            op("tensor", lambda t: t.matmul(ps[bank][:], onr[:], row_t[:], start=True, stop=True),
               r=[onr_b, row_bb], w=[ps_b[bank]])

        def quant_w_tile(srcT_h, col0, ic, dst_cs, repbank, buf, bufb, clip1=False, scalcol=None, scalcol_b=None):
            dma("sync", lambda e: e.dma_start(out=buf[:], in_=srcT_h[ic * 128:(ic + 1) * 128, col0:col0 + 512]), w=[bufb])
            if scalcol is not None:
                op("vector", lambda v: v.tensor_scalar(buf[:], buf[:], scalcol[:], None, ALU.mult), r=[bufb, scalcol_b], w=[bufb])
            else:
                op("vector", lambda v: v.tensor_tensor(buf[:], buf[:], ps[repbank][:], ALU.mult), r=[bufb, ps_b[repbank]], w=[bufb])
            op("vector", lambda v: v.tensor_scalar(buf[:], buf[:], ce[:], None, ALU.add), r=[bufb, ce_b], w=[bufb])
            if clip1:
                op("vector", lambda v: v.tensor_scalar(buf[:], buf[:], ce[:], None, ALU.subtract), r=[bufb, ce_b], w=[bufb])
                op("vector", lambda v: v.tensor_scalar(wqg[:, dst_cs], buf[:], 1.0, -1.0, ALU.min, ALU.max), r=[bufb], w=[wqg_b])
            else:
                op("vector", lambda v: v.tensor_scalar(wqg[:, dst_cs], buf[:], ce[:], None, ALU.subtract), r=[bufb, ce_b], w=[wqg_b])

        def ln_quant(src_ap, idx, adeq_t, adeq_bb, af_t, af_bb, xin, xin_b):
            dma("sync", lambda e: e.dma_start(out=xin[:], in_=src_ap), w=[xin_b])
            op("vector", lambda v: v.reduce_sum(colA[:], xin[:], axis=AX.X), r=[xin_b], w=[colA_b])
            op("vector", lambda v: v.tensor_scalar(colA[:], colA[:], 1.0 / D, None, ALU.mult), r=[colA_b], w=[colA_b])
            op("vector", lambda v: v.tensor_scalar(xn[:], xin[:], colA[:], None, ALU.subtract), r=[xin_b, colA_b], w=[xn_b])
            op("scalar", lambda c: c.activation(tq[:], xn[:], AF.Square), r=[xn_b], w=[tq_b])
            op("vector", lambda v: v.reduce_sum(colA[:], tq[:], axis=AX.X), r=[tq_b], w=[colA_b])
            op("vector", lambda v: v.tensor_scalar(colA[:], colA[:], 1.0 / D, EPS_LN, ALU.mult, ALU.add), r=[colA_b], w=[colA_b])
            op("scalar", lambda c: c.sqrt(colA[:], colA[:]), r=[colA_b], w=[colA_b])
            op("vector", lambda v: v.reciprocal(colA[:], colA[:]), r=[colA_b], w=[colA_b])
            op("vector", lambda v: v.tensor_scalar(xn[:], xn[:], colA[:], None, ALU.mult), r=[xn_b, colA_b], w=[xn_b])
            op("vector", lambda v: v.reduce_max(colA[:], xn[:], axis=AX.X, apply_absolute_value=True), r=[xn_b], w=[colA_b])
            if af_t is not None:
                op("vector", lambda v: v.tensor_scalar(af_t[:, idx:idx + 1], colA[:], 1e-8, None, ALU.max), r=[colA_b], w=[af_bb])
            op("vector", lambda v: v.tensor_scalar(colB[:], colA[:], 1.0 / 127.0, 1e-5, ALU.mult, ALU.max), r=[colA_b], w=[colB_b])
            op("vector", lambda v: v.tensor_copy(adeq_t[:, idx:idx + 1], colB[:]), r=[colB_b], w=[adeq_bb])
            op("vector", lambda v: v.reciprocal(colB[:], colB[:]), r=[colB_b], w=[colB_b])
            op("vector", lambda v: v.tensor_scalar(tq[:], xn[:], colB[:], None, ALU.mult), r=[xn_b, colB_b], w=[tq_b])
            op("vector", lambda v: v.tensor_scalar(tq[:], tq[:], ce[:], None, ALU.add), r=[tq_b, ce_b], w=[tq_b])
            op("vector", lambda v: v.tensor_scalar(xq[:], tq[:], ce[:], None, ALU.subtract), r=[tq_b, ce_b], w=[xq_b])

        def transpose16(src_t, src_bb, dst_ap_fn, dst_bb, bank0=0, bank1=1):
            for g4 in range(4):
                bank = bank0 if g4 % 2 == 0 else bank1
                opg("tensor", [
                    (lambda t, c=g4 * 4 + i, i=i, bank=bank: t.matmul(
                        ps[bank][:, i * 128:(i + 1) * 128], src_t[:, c * 128:(c + 1) * 128], idn[:],
                        start=(i == 0), stop=(i == 3))) for i in range(4)],
                    r=[src_bb, idn_b], w=[ps_b[bank]])
                op("vector", lambda v, g4=g4, bank=bank: v.tensor_copy(
                    dst_ap_fn(g4), ps[bank][:].rearrange("p (c t) -> p c t", c=4)), r=[ps_b[bank]], w=[dst_bb])

        def rope_tm(work_t, work_bb, nheads, cos_t, cos_bb, sin_t, sin_bb, blk, w2, w2b):
            cs = slice(blk * 64, blk * 64 + 64)
            nb_ = nheads * 128
            op("vector", lambda v: v.tensor_copy(sins[:, 0:64], sin_t[:, cs]), r=[sin_bb], w=[sins_b])
            op("vector", lambda v: v.tensor_scalar(sins[:, 64:128], sin_t[:, cs], -1.0, None, ALU.mult), r=[sin_bb], w=[sins_b])
            wv = lambda t_, a, b: t_[:, 0:nb_].rearrange("p (h x d) -> p h x d", h=nheads, x=2)[:, :, a:b, :]
            op("vector", lambda v: v.tensor_copy(wv(w2, 0, 1), wv(work_t, 1, 2)), r=[work_bb], w=[w2b])
            op("vector", lambda v: v.tensor_copy(wv(w2, 1, 2), wv(work_t, 0, 1)), r=[work_bb], w=[w2b])
            cosv = cos_t[:, cs].unsqueeze(1).unsqueeze(2).broadcast_to((128, nheads, 2, 64))
            sinv = sins[:, 0:128].unsqueeze(1).broadcast_to((128, nheads, 128))
            op("vector", lambda v: v.tensor_tensor(
                work_t[:, 0:nb_].rearrange("p (h x d) -> p h x d", h=nheads, x=2),
                work_t[:, 0:nb_].rearrange("p (h x d) -> p h x d", h=nheads, x=2), cosv, ALU.mult),
                r=[work_bb, cos_bb], w=[work_bb])
            op("vector", lambda v: v.tensor_tensor(
                w2[:, 0:nb_].rearrange("p (h d) -> p h d", h=nheads),
                w2[:, 0:nb_].rearrange("p (h d) -> p h d", h=nheads), sinv, ALU.mult),
                r=[w2b, sins_b], w=[w2b])
            op("vector", lambda v: v.tensor_tensor(work_t[:, 0:nb_], work_t[:, 0:nb_], w2[:, 0:nb_], ALU.add),
               r=[work_bb, w2b], w=[work_bb])

        rope_tables(NB, tposb, tposb_b, cosT, cosT_b, sinT, sinT_b)
        rowscale_pass(wqkv_R, 24)
        for g in range(2):
            build_row(range(16 + g * 4, 16 + (g + 1) * 4), rowstg1, rowstg1_b, rowstgi1, rowstgi1_b)
            rep_to_ps(5, rowstg1, rowstg1_b)
            op("vector", lambda v, g=g: v.tensor_copy(wskv_rep[:, g * 512:(g + 1) * 512], ps[5][:]), r=[ps_b[5]], w=[wskv_rep_b])
        for og in range(2):
            col0 = 2048 + og * 512
            build_row(range(16 + og * 4, 16 + (og + 1) * 4), rowstg1, rowstg1_b, rowstgi1, rowstgi1_b)
            rep_to_ps(5, rowstgi1, rowstgi1_b)
            for ic in range(NIC):
                buf, bufb = (wtt0, wtt0_b) if ic % 2 == 0 else (wtt1, wtt1_b)
                dma("sync", lambda e, ic=ic, col0=col0, buf=buf: e.dma_start(
                    out=buf[:], in_=wqkv_T[ic * 128:(ic + 1) * 128, col0:col0 + 512]), w=[bufb])
                op("vector", lambda v, buf=buf: v.tensor_tensor(buf[:], buf[:], ps[5][:], ALU.mult), r=[bufb, ps_b[5]], w=[bufb])
                op("vector", lambda v, buf=buf: v.tensor_scalar(buf[:], buf[:], ce[:], None, ALU.add), r=[bufb, ce_b], w=[bufb])
                op("vector", lambda v, ic=ic, og=og, buf=buf: v.tensor_scalar(
                    wkvq[:, ic * 1024 + og * 512: ic * 1024 + (og + 1) * 512], buf[:], ce[:], None, ALU.subtract),
                    r=[bufb, ce_b], w=[wkvq_b])
        S_.barrier()

        for blk in range(NB):
            xin, xin_b = (xt0, xt0_b) if blk % 2 == 0 else (xt1, xt1_b)
            ln_quant(x_full[blk * 128:(blk + 1) * 128, :], blk, adeq, adeq_b, None, None, xin, xin_b)
            transpose16(xq, xq_b, lambda g4: xqt_t[:, g4 * 512:(g4 + 1) * 512].rearrange("p (c t) -> p c t", c=4), xqt_t_b, 0, 1)
            for og in range(2):
                bank = 2 + og
                opg("tensor", [
                    (lambda t, ic=ic, og=og, bank=bank: t.matmul(
                        ps[bank][:], xqt_t[:, ic * 128:(ic + 1) * 128],
                        wkvq[:, ic * 1024 + og * 512: ic * 1024 + (og + 1) * 512],
                        start=(ic == 0), stop=(ic == NIC - 1))) for ic in range(NIC)],
                    r=[xqt_t_b, wkvq_b], w=[ps_b[bank]])
                op("vector", lambda v, bank=bank, blk=blk: v.tensor_scalar(
                    kwork[:], ps[bank][:], adeq[:, blk:blk + 1], None, ALU.mult), r=[ps_b[bank], adeq_b], w=[kwork_b])
                op("vector", lambda v, og=og: v.tensor_tensor(
                    kwork[:], kwork[:], wskv_rep[:, og * 512:(og + 1) * 512], ALU.mult), r=[kwork_b, wskv_rep_b], w=[kwork_b])
                if og == 0:
                    op("scalar", lambda c: c.activation(kwork2[:], kwork[:], AF.Square), r=[kwork_b], w=[kwork2_b])
                    for kv in range(KVH):
                        op("vector", lambda v, kv=kv: v.reduce_sum(colS[:, kv:kv + 1], kwork2[:, kv * 128:(kv + 1) * 128], axis=AX.X), r=[kwork2_b], w=[colS_b])
                    op("vector", lambda v: v.tensor_scalar(colS[:], colS[:], 1.0 / 128.0, EPS_RMS, ALU.mult, ALU.add), r=[colS_b], w=[colS_b])
                    op("scalar", lambda c: c.sqrt(colS[:], colS[:]), r=[colS_b], w=[colS_b])
                    op("vector", lambda v: v.reciprocal(colS[:], colS[:]), r=[colS_b], w=[colS_b])
                    for kv in range(KVH):
                        op("vector", lambda v, kv=kv: v.tensor_scalar(
                            kwork[:, kv * 128:(kv + 1) * 128], kwork[:, kv * 128:(kv + 1) * 128],
                            colS[:, kv:kv + 1], None, ALU.mult), r=[kwork_b, colS_b], w=[kwork_b])
                    rope_tm(kwork, kwork_b, KVH, cosT, cosT_b, sinT, sinT_b, blk, kwork2, kwork2_b)
                    op("vector", lambda v: v.tensor_copy(kbf[:], kwork[:]), r=[kwork_b], w=[kbf_b])
                    opg("tensor", [
                        (lambda t, kv=kv: t.matmul(
                            ps[4][:, kv * 128:(kv + 1) * 128], kbf[:, kv * 128:(kv + 1) * 128], idn[:],
                            start=(kv == 0), stop=(kv == 3))) for kv in range(KVH)],
                        r=[kbf_b, idn_b], w=[ps_b[4]])
                    op("vector", lambda v, blk=blk: v.tensor_copy(
                        ktf.rearrange("p (kv s) -> p kv s", kv=KVH)[:, :, blk * 128:(blk + 1) * 128],
                        ps[4][:].rearrange("p (kv r) -> p kv r", kv=KVH)), r=[ps_b[4]], w=[ktf_b])
                else:
                    op("vector", lambda v, blk=blk: v.tensor_copy(
                        vaug.rearrange("p (kv nb c) -> p kv nb c", kv=KVH, nb=NB)[:, :, blk:blk + 1, 0:128],
                        kwork[:].rearrange("p (kv o d) -> p kv o d", kv=KVH, o=1)), r=[kwork_b], w=[vaug_b])
        S_.barrier()

        rope_tables(NOB, tqposb, tqposb_b, cosQ, cosQ_b, sinQ, sinQ_b)
        for ot in range(NOB):
            xin, xin_b = (xt0, xt0_b) if ot % 2 == 0 else (xt1, xt1_b)
            ln_quant(x_own[ot * 128:(ot + 1) * 128, :], ot, adeqo, adeqo_b, afo, afo_b, xin, xin_b)
            transpose16(xq, xq_b, lambda g4: xqt_t[:, g4 * 512:(g4 + 1) * 512].rearrange("p (c t) -> p c t", c=4), xqt_t_b, 0, 1)
            dma("sync", lambda e, ot=ot: e.dma_start(
                out=sc_xq[ot].rearrange("ic p t -> p ic t"),
                in_=xqt_t[:].rearrange("p (ic t) -> p ic t", ic=NIC)), r=[xqt_t_b])
        for og in range(4):
            build_row(range(og * 4, (og + 1) * 4), rowstg3, rowstg3_b, rowstgi3, rowstgi3_b)
            rep_to_ps(5, rowstg3, rowstg3_b)
            op("vector", lambda v: v.tensor_copy(wsqd[:], ps[5][:]), r=[ps_b[5]], w=[wsqd_b])
            rep_to_ps(5, rowstgi3, rowstgi3_b)
            for ic in range(NIC):
                buf, bufb = (wtt0, wtt0_b) if ic % 2 == 0 else (wtt1, wtt1_b)
                quant_w_tile(wqkv_T, og * 512, ic, slice(ic * 512, (ic + 1) * 512), 5, buf, bufb)
            def q_chain(ot):
                bank = 2 + ot % 2
                xs, xs_b = (xqs0, xqs0_b) if ot % 2 == 0 else (xqs1, xqs1_b)
                dma("sync", lambda e, ot=ot, xs=xs: e.dma_start(
                    out=xs[:].rearrange("p (ic t) -> p ic t", ic=NIC),
                    in_=sc_xq[ot].rearrange("ic p t -> p ic t")), w=[xs_b])
                opg("tensor", [
                    (lambda t, ic=ic, bank=bank, xs=xs: t.matmul(
                        ps[bank][:], xs[:, ic * 128:(ic + 1) * 128],
                        wqg[:, ic * 512:(ic + 1) * 512],
                        start=(ic == 0), stop=(ic == NIC - 1))) for ic in range(NIC)],
                    r=[xs_b, wqg_b], w=[ps_b[bank]])

            q_chain(0)
            for ot in range(NOB):
                bank = 2 + ot % 2
                if ot + 1 < NOB:
                    q_chain(ot + 1)
                op("vector", lambda v, bank=bank, ot=ot: v.tensor_scalar(
                    kwork[:], ps[bank][:], adeqo[:, ot:ot + 1], None, ALU.mult), r=[ps_b[bank], adeqo_b], w=[kwork_b])
                op("vector", lambda v: v.tensor_tensor(kwork[:], kwork[:], wsqd[:], ALU.mult), r=[kwork_b, wsqd_b], w=[kwork_b])
                op("scalar", lambda c: c.activation(kwork2[:], kwork[:], AF.Square), r=[kwork_b], w=[kwork2_b])
                for i in range(4):
                    op("vector", lambda v, i=i: v.reduce_sum(colS[:, i:i + 1], kwork2[:, i * 128:(i + 1) * 128], axis=AX.X), r=[kwork2_b], w=[colS_b])
                op("vector", lambda v: v.tensor_scalar(colS[:], colS[:], 1.0 / 128.0, EPS_RMS, ALU.mult, ALU.add), r=[colS_b], w=[colS_b])
                op("scalar", lambda c: c.sqrt(colS[:], colS[:]), r=[colS_b], w=[colS_b])
                op("vector", lambda v: v.reciprocal(colS[:], colS[:]), r=[colS_b], w=[colS_b])
                op("vector", lambda v, og=og: v.tensor_tensor(colS[:], colS[:], gnb[:, og * 4:(og + 1) * 4], ALU.mult), r=[colS_b, gnb_b], w=[colS_b])
                for i in range(4):
                    op("vector", lambda v, i=i: v.tensor_scalar(
                        kwork[:, i * 128:(i + 1) * 128], kwork[:, i * 128:(i + 1) * 128],
                        colS[:, i:i + 1], None, ALU.mult), r=[kwork_b, colS_b], w=[kwork_b])
                rope_tm(kwork, kwork_b, 4, cosQ, cosQ_b, sinQ, sinQ_b, ot, kwork2, kwork2_b)
                op("vector", lambda v: v.tensor_copy(kbf[:], kwork[:]), r=[kwork_b], w=[kbf_b])
                opg("tensor", [
                    (lambda t, i=i: t.matmul(
                        ps[4][:, i * 128:(i + 1) * 128], kbf[:, i * 128:(i + 1) * 128], idn[:],
                        start=(i == 0), stop=(i == 3))) for i in range(4)],
                    r=[kbf_b, idn_b], w=[ps_b[4]])
                op("vector", lambda v: v.tensor_copy(kbf[:], ps[4][:]), r=[ps_b[4]], w=[kbf_b])
                dma("sync", lambda e, og=og, ot=ot: e.dma_start(
                    out=sc_qt[og * 4:(og + 1) * 4, :, ot * 128:(ot + 1) * 128].rearrange("h p t -> p h t"),
                    in_=kbf[:].rearrange("p (h t) -> p h t", h=4)), r=[kbf_b])
        S_.barrier()

        op("vector", lambda v: v.memset(colB[:], 0.0), w=[colB_b])
        for c in range(NIC):
            wb, wbb = (wld0, wld0_b) if c % 2 == 0 else (wld1, wld1_b)
            dma("sync", lambda e, c=c, wb=wb: e.dma_start(out=wb[:], in_=ffn_R[c * 128:(c + 1) * 128, :]), w=[wbb])
            op("vector", lambda v, wb=wb: v.reduce_sum(colA[:], wb[:], axis=AX.X, apply_absolute_value=True), r=[wbb], w=[colA_b])
            op("vector", lambda v: v.tensor_tensor(colB[:], colB[:], colA[:], ALU.add), r=[colB_b, colA_b], w=[colB_b])
        op("tensor", lambda t: t.matmul(ps[7][0:1, 0:1], colB[:], onc[:], start=True, stop=True), r=[colB_b, onc_b], w=[ps_b[7]])
        op("vector", lambda v: v.tensor_copy(tmp11[:], ps[7][0:1, 0:1]), r=[ps_b[7]], w=[tmp11_b])
        op("vector", lambda v: v.tensor_scalar(tmp11[:], tmp11[:], 1.0 / (D * D), 1e-6, ALU.mult, ALU.max), r=[tmp11_b], w=[tmp11_b])
        op("tensor", lambda t: t.matmul(ps[7][:, 0:1], onr[:], tmp11[:], start=True, stop=True), r=[onr_b, tmp11_b], w=[ps_b[7]])
        op("vector", lambda v: v.tensor_scalar(wsfd[:], ps[7][:, 0:1], 1.0 / 127.0, None, ALU.mult), r=[ps_b[7]], w=[wsfd_b])
        op("vector", lambda v: v.reciprocal(wsf_c[:], ps[7][:, 0:1]), r=[ps_b[7]], w=[wsf_c_b])
        for og in range(4):
            for ic in range(NIC):
                buf, bufb = (wtt0, wtt0_b) if ic % 2 == 0 else (wtt1, wtt1_b)
                quant_w_tile(ffn_T, og * 512, ic, slice(ic * 512, (ic + 1) * 512), None, buf, bufb,
                             clip1=True, scalcol=wsf_c, scalcol_b=wsf_c_b)
            def f_chain(ot):
                bank = 2 + ot % 2
                xs, xs_b = (xqs0, xqs0_b) if ot % 2 == 0 else (xqs1, xqs1_b)
                dma("sync", lambda e, ot=ot, xs=xs: e.dma_start(
                    out=xs[:].rearrange("p (ic t) -> p ic t", ic=NIC),
                    in_=sc_xq[ot].rearrange("ic p t -> p ic t")), w=[xs_b])
                opg("tensor", [
                    (lambda t, ic=ic, bank=bank, xs=xs: t.matmul(
                        ps[bank][:], xs[:, ic * 128:(ic + 1) * 128],
                        wqg[:, ic * 512:(ic + 1) * 512],
                        start=(ic == 0), stop=(ic == NIC - 1))) for ic in range(NIC)],
                    r=[xs_b, wqg_b], w=[ps_b[bank]])

            f_chain(0)
            for ot in range(NOB):
                bank = 2 + ot % 2
                if ot + 1 < NOB:
                    f_chain(ot + 1)
                op("vector", lambda v, ot=ot: v.tensor_tensor(colC[:], afo[:, ot:ot + 1], wsfd[:], ALU.mult), r=[afo_b, wsfd_b], w=[colC_b])
                op("scalar", lambda c, bank=bank: c.activation(kbf[:], ps[bank][:], AF.Silu, scale=colC[:]), r=[ps_b[bank], colC_b], w=[kbf_b])
                dma("sync", lambda e, ot=ot, og=og: e.dma_start(
                    out=sc_g[ot, :, og * 512:(og + 1) * 512], in_=kbf[:]), r=[kbf_b])
        S_.barrier()

        rowscale_pass(wo_R, NIC)
        for og in range(4):
            build_row(range(og * 4, (og + 1) * 4), rowstg1, rowstg1_b, rowstgi1, rowstgi1_b)
            rep_to_ps(5, rowstgi1, rowstgi1_b)
            for ic in range(NIC):
                buf, bufb = (wtt0, wtt0_b) if ic % 2 == 0 else (wtt1, wtt1_b)
                quant_w_tile(wo_T, og * 512, ic, slice(ic * 512, (ic + 1) * 512), 5, buf, bufb)
            dma("sync", lambda e, og=og: e.dma_start(
                out=sc_woq[:, :, og * 512:(og + 1) * 512].rearrange("ic p d -> p ic d"),
                in_=wqg[:].rearrange("p (ic d) -> p ic d", ic=NIC)), r=[wqg_b])
            rep_to_ps(5, rowstg1, rowstg1_b)
            op("vector", lambda v, og=og: v.tensor_copy(wso_rep[:, og * 512:(og + 1) * 512], ps[5][:]), r=[ps_b[5]], w=[wso_rep_b])
        S_.barrier()

        dma("sync", lambda e: e.dma_start(
            out=masksb.rearrange("p (m q) -> p m q", m=32), in_=masks_h.rearrange("m p q -> p m q")), w=[masksb_b])
        for qt in range(4):
            KMAX = 8 * qt + 8
            dma("sync", lambda e, qt=qt: e.dma_start(
                out=qt_t[:].rearrange("p (h t) -> p h t", h=H),
                in_=sc_qt[:, :, qt * 512:(qt + 1) * 512].rearrange("h p t -> p h t")), w=[qt_t_b])
            for h in range(H):
                kv = h // 4

                def st_mm(kc, kv=kv, h=h):
                    stb = 4 + (kc % 2)
                    op("tensor", lambda t, kc=kc, kv=kv, h=h, stb=stb: t.matmul(
                        ps[stb][:], ktf[:, kv * S + kc * 128: kv * S + (kc + 1) * 128],
                        qt_t[:, h * 512:(h + 1) * 512], start=True, stop=True),
                        r=[ktf_b, qt_t_b], w=[ps_b[stb]])

                st_mm(0)
                for kc in range(KMAX):
                    stb = 4 + (kc % 2)
                    if kc + 1 < KMAX:
                        st_mm(kc + 1)
                    e_t, e_b = (e0, e0_b) if kc % 2 == 0 else (e1, e1_b)
                    op("scalar", lambda c, h=h, stb=stb, e_t=e_t: c.activation(
                        e_t[:], ps[stb][:], AF.Exp, bias=mneg[:, h:h + 1]), r=[ps_b[stb], mneg_b], w=[e_b])
                    mmfns = []
                    wbanks = []
                    for j in range(4):
                        dd = kc - (8 * qt + 2 * j)
                        if dd > 1:
                            continue
                        if dd >= 0:
                            m_idx = (qt * 4 + j) * 2 + dd
                            op("vector", lambda v, j=j, m_idx=m_idx, e_t=e_t: v.tensor_tensor(
                                e_t[:, j * 128:(j + 1) * 128], e_t[:, j * 128:(j + 1) * 128],
                                masksb[:, m_idx * 128:(m_idx + 1) * 128], ALU.mult), r=[masksb_b, e_b], w=[e_b])
                        last_kc = min(8 * qt + 2 * j + 1, KMAX - 1)
                        mmfns.append(lambda t, j=j, kc=kc, kv=kv, e_t=e_t, last_kc=last_kc: t.matmul(
                            ps[j][:, 0:129], e_t[:, j * 128:(j + 1) * 128],
                            vaug[:, (kv * NB + kc) * 129: (kv * NB + kc + 1) * 129],
                            start=(kc == 0), stop=(kc == last_kc)))
                        wbanks.append(ps_b[j])
                    opg("tensor", mmfns, r=[e_b, vaug_b], w=wbanks)
                for j in range(4):
                    op("vector", lambda v, j=j, h=h: v.tensor_copy(
                        y_tm[:, j * D + h * 128: j * D + (h + 1) * 128], ps[j][:, 0:128]), r=[ps_b[j]], w=[y_tm_b])
                    op("vector", lambda v, j=j, h=h: v.tensor_copy(
                        zcol[:, j * H + h: j * H + h + 1], ps[j][:, 128:129]), r=[ps_b[j]], w=[zcol_b])
            for j in range(4):
                op("vector", lambda v, j=j: v.reciprocal(zinv[:, j * H:(j + 1) * H], zcol[:, j * H:(j + 1) * H]), r=[zcol_b], w=[zinv_b])
                for h in range(H):
                    op("vector", lambda v, j=j, h=h: v.tensor_scalar(
                        y_tm[:, j * D + h * 128: j * D + (h + 1) * 128],
                        y_tm[:, j * D + h * 128: j * D + (h + 1) * 128],
                        zinv[:, j * H + h: j * H + h + 1], None, ALU.mult), r=[zinv_b, y_tm_b], w=[y_tm_b])
                op("vector", lambda v, j=j: v.reduce_max(colA[:], y_tm[:, j * D:(j + 1) * D], axis=AX.X, apply_absolute_value=True), r=[y_tm_b], w=[colA_b])
                op("vector", lambda v: v.tensor_scalar(colB[:], colA[:], 1.0 / 127.0, 1e-5, ALU.mult, ALU.max), r=[colA_b], w=[colB_b])
                op("vector", lambda v, j=j: v.tensor_copy(adqy[:, j:j + 1], colB[:]), r=[colB_b], w=[adqy_b])
                op("vector", lambda v: v.reciprocal(colB[:], colB[:]), r=[colB_b], w=[colB_b])
                op("vector", lambda v, j=j: v.tensor_scalar(tqT[:], y_tm[:, j * D:(j + 1) * D], colB[:], None, ALU.mult), r=[y_tm_b, colB_b], w=[tqT_b])
                op("vector", lambda v: v.tensor_scalar(tqT[:], tqT[:], ce[:], None, ALU.add), r=[tqT_b, ce_b], w=[tqT_b])
                op("vector", lambda v: v.tensor_scalar(yq[:], tqT[:], ce[:], None, ALU.subtract), r=[tqT_b, ce_b], w=[yq_b])
                transpose16(yq, yq_b, lambda g4, j=j: yqt[:, j * D + g4 * 512: j * D + (g4 + 1) * 512].rearrange("p (c t) -> p c t", c=4), yqt_b, 6, 7)
            for j in range(4):
                ot = qt * 4 + j
                dma("sync", lambda e, ot=ot, j=j: e.dma_start(
                    out=gbf[:, j * D:(j + 1) * D], in_=sc_g[ot, :, :]), w=[gbf_b])
            for og in range(4):
                dma("sync", lambda e, og=og: e.dma_start(
                    out=wqg[:].rearrange("p (ic d) -> p ic d", ic=NIC),
                    in_=sc_woq[:, :, og * 512:(og + 1) * 512].rearrange("ic p d -> p ic d")), w=[wqg_b])
                for j in range(4):
                    ot = qt * 4 + j
                    bank = 6 + (j % 2)
                    opg("tensor", [
                        (lambda t, j=j, ic=ic, bank=bank: t.matmul(
                            ps[bank][:], yqt[:, j * D + ic * 128: j * D + (ic + 1) * 128],
                            wqg[:, ic * 512:(ic + 1) * 512],
                            start=(ic == 0), stop=(ic == NIC - 1))) for ic in range(NIC)],
                        r=[yqt_b, wqg_b], w=[ps_b[bank]])
                    op("vector", lambda v, j=j, bank=bank: v.tensor_scalar(
                        sdqt[:], ps[bank][:], adqy[:, j:j + 1], None, ALU.mult), r=[ps_b[bank], adqy_b], w=[sdqt_b])
                    op("vector", lambda v, og=og: v.tensor_tensor(
                        sdqt[:], sdqt[:], wso_rep[:, og * 512:(og + 1) * 512], ALU.mult), r=[wso_rep_b, sdqt_b], w=[sdqt_b])
                    dma("sync", lambda e, ot=ot, og=og: e.dma_start(
                        out=xrt[:], in_=x_own[ot * 128:(ot + 1) * 128, og * 512:(og + 1) * 512]), w=[xrt_b])
                    op("vector", lambda v, j=j, og=og: v.tensor_copy(
                        gft[:], gbf[:, j * D + og * 512: j * D + (og + 1) * 512]), r=[gbf_b], w=[gft_b])
                    op("vector", lambda v: v.tensor_tensor(outt[:], sdqt[:], gft[:], ALU.mult), r=[sdqt_b, gft_b], w=[outt_b])
                    op("vector", lambda v: v.tensor_tensor(outt[:], outt[:], xrt[:], ALU.add), r=[outt_b, xrt_b], w=[outt_b])
                    dma("sync", lambda e, ot=ot, og=og: e.dma_start(
                        out=out_h[ot * 128:(ot + 1) * 128, og * 512:(og + 1) * 512], in_=outt[:]), r=[outt_b])

        S_.wait("sync", [(s, S_.cnt[s]) for s in ALL_DMA_SEMS if S_.cnt[s] > 0])

        sems = {}
        for name in list(ENG_SEM.values()) + ALL_DMA_SEMS:
            sems[name] = es.enter_context(nc.semaphore(name))
        block = es.enter_context(nc.Block())

        def mk(engname):
            items = S_.items[engname]
            osem = ENG_SEM[engname]

            def body(proxy):
                waited = {}
                for it in items:
                    if it[0] == "w":
                        _, s, v = it
                        if waited.get(s, -1) >= v:
                            continue
                        proxy.wait_ge(sems[s], v)
                        waited[s] = v
                    elif it[0] == "op":
                        it[1](proxy).then_inc(sems[osem], 1)
                    elif it[0] == "opg":
                        for f in it[1][:-1]:
                            f(proxy)
                        it[1][-1](proxy).then_inc(sems[osem], 1)
                    else:
                        it[1](proxy).then_inc(sems[it[2]], 16)
            return body

        block.tensor(mk("tensor"))
        block.vector(mk("vector"))
        block.scalar(mk("scalar"))
        block.sync(mk("sync"))
        block.gpsimd(mk("gpsimd"))

    return nc


def kernel(L0_attn_x, L0_attn_w_q, L0_attn_w_k, L0_attn_w_v, L0_attn_w_o,
           L0_attn_q_gain, L0_ffn_weight):
    x = np.ascontiguousarray(np.asarray(L0_attn_x, dtype=np.float32))
    wq = np.asarray(L0_attn_w_q, dtype=np.float32)
    wk = np.asarray(L0_attn_w_k, dtype=np.float32)
    wv = np.asarray(L0_attn_w_v, dtype=np.float32)
    wo = np.asarray(L0_attn_w_o, dtype=np.float32)
    gain = np.asarray(L0_attn_q_gain, dtype=np.float32)
    wf = np.asarray(L0_ffn_weight, dtype=np.float32)

    wqkv_R = np.ascontiguousarray(np.concatenate([wq, wk, wv], axis=0))
    wqkv_T = np.ascontiguousarray(wqkv_R.T)
    wo_T = np.ascontiguousarray(wo.T)
    ffn_T = np.ascontiguousarray(wf.T)

    idnb = np.eye(128, dtype=ml_dtypes.bfloat16)
    idnf = np.eye(128, dtype=np.float32)
    ones_r = np.ones((1, 128), dtype=np.float32)
    ones_c = np.ones((128, 1), dtype=np.float32)
    iidx = np.arange(0, 128, 2, dtype=np.float32).reshape(1, 64)
    ifreq = (1.0 / (10000.0 ** (np.arange(0, 128, 2, dtype=np.float64) / 128.0))).astype(np.float32).reshape(1, 64)
    tpos = np.ascontiguousarray(np.arange(S, dtype=np.float32).reshape(NB, 128).T)

    in_maps = []
    block_maps = []
    for c in range(8):
        b, h = c // 2, c % 2
        obs = own_blocks(h)
        block_maps.append(obs)
        xb = np.ascontiguousarray(x[b])
        x_own = np.ascontiguousarray(xb.reshape(NB, 128, D)[obs].reshape(NOB * 128, D))
        tq_own = np.asarray([g * 128 + i for g in obs for i in range(128)], dtype=np.float32)
        tqpos = np.ascontiguousarray(tq_own.reshape(NOB, 128).T)
        masks = np.zeros((32, 128, 128), dtype=ml_dtypes.bfloat16)
        tri = np.triu(np.ones((128, 128), dtype=np.float32))
        for t in range(4):
            for j in range(4):
                g = obs[t * 4 + j]
                for slot in range(2):
                    kc = 8 * t + 2 * j + slot
                    if kc < g:
                        m = np.ones((128, 128), dtype=np.float32)
                    elif kc == g:
                        m = tri
                    else:
                        m = np.zeros((128, 128), dtype=np.float32)
                    masks[(t * 4 + j) * 2 + slot] = m.astype(ml_dtypes.bfloat16)
        in_maps.append({
            "x_full": xb, "x_own": x_own,
            "wqkv_T": wqkv_T, "wqkv_R": wqkv_R, "wo_T": wo_T, "wo_R": wo,
            "ffn_T": ffn_T, "ffn_R": wf,
            "gains": gain.reshape(1, H), "idnb": idnb, "idnf": idnf,
            "ones_r": ones_r, "ones_c": ones_c,
            "tpos": tpos, "tqpos": tqpos, "ifreq": ifreq, "iidx": iidx,
            "masks": masks,
        })

    nc = build_nc()
    res = run_bass_kernel_spmd(nc, in_maps, core_ids=list(range(8)))

    out = np.empty((B, S, D), dtype=np.float32)
    for c in range(8):
        b = c // 2
        shard = np.asarray(res.results[c]["out_shard"]).reshape(NOB, 128, D)
        for i, g in enumerate(block_maps[c]):
            out[b, g * 128:(g + 1) * 128, :] = shard[i]
    return out


# version 07
# speedup vs baseline: 1.1479x; 1.1479x over previous
import contextlib
import numpy as np
import ml_dtypes
import concourse.bass as bass
import concourse.mybir as mybir
from concourse.bass_utils import run_bass_kernel_spmd

F32 = mybir.dt.float32
BF16 = mybir.dt.bfloat16
AF = mybir.ActivationFunctionType
ALU = mybir.AluOpType
AX = mybir.AxisListType

B, S, D = 4, 4096, 2048
H, HD, KVH = 16, 128, 4
NB = S // 128
NOB = 16
NIC = D // 128
EPS_LN = 1e-5
EPS_RMS = 1.1920929e-7
C23 = 1.5 * 2.0 ** 23
CDELTA = 1.5 * 2.0 ** 52 - 1.5 * 2.0 ** 23
SQ128 = float(np.sqrt(128.0))
TWO_PI = 2.0 * np.pi
INV2PI = 1.0 / TWO_PI
RC1 = 6.28125
RC2 = float(np.float64(TWO_PI) - RC1)
RC2A = float(np.float32(RC2))
RC2B = float(np.float64(RC2) - np.float64(np.float32(RC2)))
EXSC = float(-np.log(10000.0) / 128.0)
EXSCA = float(np.float32(EXSC))
EXSCB = float(np.float64(EXSC) - np.float64(np.float32(EXSC)))
PI_CLAMP = 3.1415925
LN1E4 = float(np.log(10000.0))

ENGS = ("tensor", "vector", "scalar", "sync", "gpsimd")
DMA_SEMS = {"sync": ["Yd%d" % i for i in range(6)], "scalar": ["Cd0"], "gpsimd": ["Gd0"]}
ALL_DMA_SEMS = [s for v in DMA_SEMS.values() for s in v]
ENG_SEM = {"tensor": "T", "vector": "V", "scalar": "C", "sync": "Y", "gpsimd": "G"}


def own_blocks(h):
    return sorted([2 * i + h for i in range(8)] + [16 + 2 * i + (1 - h) for i in range(8)])


class Sched:
    def __init__(self):
        self.items = {e: [] for e in ENGS}
        self.cnt = {s: 0 for s in list(ENG_SEM.values()) + ALL_DMA_SEMS}
        self.rr = {e: 0 for e in DMA_SEMS}

    def wait(self, eng, deps):
        for d in deps:
            if d is not None:
                self.items[eng].append(("w", d[0], d[1]))

    def emit(self, eng, kind, fn, deps):
        st = self.items[eng]
        for d in deps:
            if d is not None:
                st.append(("w", d[0], d[1]))
        if kind == "dma":
            sl = DMA_SEMS[eng]
            sem = sl[self.rr[eng] % len(sl)]
            self.rr[eng] += 1
            if self.cnt[sem] > 0:
                st.append(("w", sem, self.cnt[sem]))
            st.append((kind, fn, sem))
            self.cnt[sem] += 16
            return (sem, self.cnt[sem])
        st.append((kind, fn))
        sem = ENG_SEM[eng]
        self.cnt[sem] += 1
        return (sem, self.cnt[sem])

    def barrier(self):
        snap = dict(self.cnt)
        for e in ENGS:
            for s, v in snap.items():
                if v > 0:
                    self.items[e].append(("w", s, v))


class Buf:
    __slots__ = ("w", "rs")

    def __init__(self):
        self.w = None
        self.rs = []


S_ = None


def _emit(eng, kind, fn, r, w):
    deps = [b.w for b in r]
    for b in w:
        deps.append(b.w)
        deps.extend(b.rs)
    tok = S_.emit(eng, kind, fn, deps)
    for b in r:
        b.rs.append(tok)
    for b in w:
        b.w = tok
        b.rs = []
    return tok


def op(eng, fn, r=(), w=()):
    return _emit(eng, "op", fn, r, w)


def opg(eng, fns, r=(), w=()):
    return _emit(eng, "opg", fns, r, w)


def dma(eng, fn, r=(), w=()):
    return _emit(eng, "dma", fn, r, w)


def build_nc():
    global S_
    S_ = Sched()
    nc = bass.Bass("TRN2", target_bir_lowering=False)

    def din(name, shape, dt=F32):
        return nc.dram_tensor(name, list(shape), dt, kind="ExternalInput").ap()

    x_full = din("x_full", [S, D])
    x_own = din("x_own", [NOB * 128, D])
    wqkv_T = din("wqkv_T", [D, 3072])
    wqkv_R = din("wqkv_R", [3072, D])
    wo_T = din("wo_T", [D, D])
    wo_R = din("wo_R", [D, D])
    ffn_T = din("ffn_T", [D, D])
    ffn_R = din("ffn_R", [D, D])
    gains_h = din("gains", [1, H])
    idnb_h = din("idnb", [128, 128], BF16)
    idnf_h = din("idnf", [128, 128], F32)
    ones_r = din("ones_r", [1, 128])
    ones_c = din("ones_c", [128, 1])
    tpos_h = din("tpos", [128, NB])
    tqpos_h = din("tqpos", [128, NOB])
    ifreq_h = din("ifreq", [1, 64])
    iidx_h = din("iidx", [1, 64])
    masks_h = din("masks", [32, 128, 128], BF16)

    out_h = nc.dram_tensor("out_shard", [NOB * 128, D], F32, kind="ExternalOutput").ap()
    sc_qt = nc.dram_tensor("sc_qt", [H, 128, NOB * 128], BF16, kind="Internal").ap()
    sc_g = nc.dram_tensor("sc_g", [NOB, 128, D], BF16, kind="Internal").ap()
    sc_woq = nc.dram_tensor("sc_woq", [NIC, 128, D], BF16, kind="Internal").ap()
    sc_xq = nc.dram_tensor("sc_xq", [NOB, NIC, 128, 128], BF16, kind="Internal").ap()

    es = contextlib.ExitStack()

    AB = es.enter_context(nc.sbuf_tensor("AB", [128, 72832], BF16))
    AFR = es.enter_context(nc.sbuf_tensor("AFR", [128, 15360], F32))

    def vb(arena, c0, n, p0=0, p1=128):
        return arena[p0:p1, c0:c0 + n], Buf()

    ktf, ktf_b = vb(AB, 0, KVH * S)
    vaug, vaug_b = vb(AB, 16384, KVH * NB * 129)
    wkvq, wkvq_b = vb(AB, 32896, NIC * 1024)
    xqs0, xqs0_b = vb(AB, 32896, NIC * 128)
    xqs1, xqs1_b = vb(AB, 34944, NIC * 128)
    gbf, gbf_b = vb(AB, 32896, 4 * D)
    wqg, wqg_b = vb(AB, 41088, NIC * 512)
    xq, xq_b = vb(AB, 49280, D)
    xqt_t, xqt_t_b = vb(AB, 51328, NIC * 128)
    kbf, kbf_b = vb(AB, 53376, 512)
    qt_t, qt_t_b = vb(AB, 49280, H * 512)
    yq, yq_b = vb(AB, 57472, D)
    e0, e0_b = vb(AB, 59520, 512)
    e1, e1_b = vb(AB, 60032, 512)
    yqt, yqt_b = vb(AB, 60544, 4 * NIC * 128)
    masksb, masksb_b = vb(AB, 68736, 32 * 128)

    wld0, wld0_b = vb(AFR, 0, 2048)
    wld1, wld1_b = vb(AFR, 2048, 2048)
    wtt0, wtt0_b = vb(AFR, 4096, 512)
    wtt1, wtt1_b = vb(AFR, 4608, 512)
    xt0, xt0_b = vb(AFR, 0, 2048)
    xt1, xt1_b = vb(AFR, 2048, 2048)
    xn, xn_b = vb(AFR, 5120, 2048)
    tq, tq_b = vb(AFR, 7168, 2048)
    kwork, kwork_b = vb(AFR, 9216, 512)
    kwork2, kwork2_b = vb(AFR, 9728, 512)
    wskv_rep, wskv_rep_b = vb(AFR, 10240, 1024)
    cosT, cosT_b = vb(AFR, 11264, NB * 64)
    sinT, sinT_b = vb(AFR, 13312, NB * 64)
    cosQ, cosQ_b = vb(AFR, 11264, NOB * 64)
    sinQ, sinQ_b = vb(AFR, 12288, NOB * 64)
    wsqd, wsqd_b = vb(AFR, 13312, 512)
    y_tm, y_tm_b = vb(AFR, 0, 4 * D)
    wso_rep, wso_rep_b = vb(AFR, 9216, D)
    sdqt, sdqt_b = vb(AFR, 11264, 512)
    xrt, xrt_b = vb(AFR, 11776, 512)
    outt, outt_b = vb(AFR, 12288, 512)
    gft, gft_b = vb(AFR, 12800, 512)
    tqT, tqT_b = vb(AFR, 13312, 2048)
    rowstg1, rowstg1_b = vb(AFR, 5120, 512, 0, 1)
    rowstgi1, rowstgi1_b = vb(AFR, 5632, 512, 0, 1)
    rowstg3, rowstg3_b = vb(AFR, 10240, 512, 0, 1)
    rowstgi3, rowstgi3_b = vb(AFR, 10752, 512, 0, 1)

    def sbf(name, p, f, dt=F32):
        return es.enter_context(nc.sbuf_tensor(name, [p, f], dt)), Buf()

    adeq, adeq_b = sbf("adeq", 128, NB)
    adeqo, adeqo_b = sbf("adeqo", 128, NOB)
    afo, afo_b = sbf("afo", 128, NOB)
    kk, kk_b = sbf("kk", 128, 1)
    ce, ce_b = sbf("ce", 128, 1)
    idn, idn_b = sbf("idn", 128, 128, BF16)
    idnf, idnf_b = sbf("idnf_s", 128, 128)
    onr, onr_b = sbf("onr", 1, 128)
    onc, onc_b = sbf("onc", 128, 1)
    ifr, ifr_b = sbf("ifr", 128, 64)
    mneg, mneg_b = sbf("mneg", 128, H)
    gn, gn_b = sbf("gn", 1, H)
    gnb, gnb_b = sbf("gnb", 128, H)
    wscol, wscol_b = sbf("wscol", 128, 24)
    colA, colA_b = sbf("colA", 128, 1)
    colB, colB_b = sbf("colB", 128, 1)
    colC, colC_b = sbf("colC", 128, 1)
    colS, colS_b = sbf("colS", 128, 4)
    adqy, adqy_b = sbf("adqy", 128, 4)
    sins, sins_b = sbf("sins", 128, 128)
    ang, ang_b = sbf("ang", 128, 64)
    ang2, ang2_b = sbf("ang2", 128, 64)
    angk, angk_b = sbf("angk", 128, 64)
    angt, angt_b = sbf("angt", 128, 64)
    ifh, ifh_b = sbf("ifh", 1, 64)
    ifd, ifd_b = sbf("ifd", 1, 64)
    tmpr, tmpr_b = sbf("tmpr", 128, 64)
    tmp2, tmp2_b = sbf("tmp2", 1, 16)
    tmp11, tmp11_b = sbf("tmp11", 1, 1)
    tposb, tposb_b = sbf("tposb", 128, NB)
    tqposb, tqposb_b = sbf("tqposb", 128, NOB)
    zcol, zcol_b = sbf("zcol", 128, 4 * H)
    zinv, zinv_b = sbf("zinv", 128, 4 * H)
    r127, r127_b = sbf("r127", 128, 1)
    rs128, rs128_b = sbf("rs128", 128, 1)
    wsf_c, wsf_c_b = sbf("wsf_c", 128, 1)
    wsfd, wsfd_b = sbf("wsfd", 128, 1)

    ps, ps_b = [], []
    for i in range(8):
        ps.append(es.enter_context(nc.psum_tensor("ps%d" % i, [128, 512], F32)))
        ps_b.append(Buf())

    with es:
        dma("sync", lambda e: e.dma_start(out=idn[:], in_=idnb_h[:]), w=[idn_b])
        dma("sync", lambda e: e.dma_start(out=idnf[:], in_=idnf_h[:]), w=[idnf_b])
        dma("sync", lambda e: e.dma_start(out=onr[:], in_=ones_r[:]), w=[onr_b])
        dma("sync", lambda e: e.dma_start(out=onc[:], in_=ones_c[:]), w=[onc_b])
        dma("sync", lambda e: e.dma_start(out=gn[:], in_=gains_h[:]), w=[gn_b])
        dma("sync", lambda e: e.dma_start(out=ifh[:], in_=ifreq_h[:]), w=[ifh_b])
        dma("sync", lambda e: e.dma_start(out=ifd[:], in_=iidx_h[:]), w=[ifd_b])
        dma("sync", lambda e: e.dma_start(out=tposb[:], in_=tpos_h[:]), w=[tposb_b])
        dma("sync", lambda e: e.dma_start(out=tqposb[:], in_=tqpos_h[:]), w=[tqposb_b])

        op("vector", lambda v: v.memset(kk[:], 1.0), w=[kk_b])
        op("vector", lambda v: v.tensor_scalar(kk[:], kk[:], 2.0 ** -25, None, ALU.add), r=[kk_b], w=[kk_b])
        op("vector", lambda v: v.tensor_scalar(kk[:], kk[:], -1.0, None, ALU.add), r=[kk_b], w=[kk_b])
        op("vector", lambda v: v.tensor_scalar(kk[:], kk[:], 2.0 ** 25, None, ALU.mult), r=[kk_b], w=[kk_b])
        op("vector", lambda v: v.tensor_scalar(ce[:], kk[:], CDELTA, C23, ALU.mult, ALU.add), r=[kk_b], w=[ce_b])

        op("vector", lambda v: v.memset(r127[:], 127.0), w=[r127_b])
        op("vector", lambda v: v.reciprocal(r127[:], r127[:]), r=[r127_b], w=[r127_b])
        op("vector", lambda v: v.memset(rs128[:], 128.0), w=[rs128_b])
        op("scalar", lambda c: c.sqrt(rs128[:], rs128[:]), r=[rs128_b], w=[rs128_b])
        op("vector", lambda v: v.reciprocal(rs128[:], rs128[:]), r=[rs128_b], w=[rs128_b])

        op("vector", lambda v: v.tensor_scalar(angk[0:1, :], ifd[:], EXSCB, None, ALU.mult), r=[ifd_b], w=[angk_b])
        op("vector", lambda v: v.tensor_scalar(ifd[:], ifd[:], EXSCA, None, ALU.mult), r=[ifd_b], w=[ifd_b])
        op("vector", lambda v: v.tensor_tensor(ifd[:], ifd[:], angk[0:1, :], ALU.add), r=[ifd_b, angk_b], w=[ifd_b])
        op("scalar", lambda c: c.activation(ifd[:], ifd[:], AF.Exp), r=[ifd_b], w=[ifd_b])
        op("vector", lambda v: v.tensor_tensor(ifd[:], ifd[:], ifh[:], ALU.subtract), r=[ifd_b, ifh_b], w=[ifd_b])
        op("tensor", lambda t: t.matmul(ps[6][:, 0:64], onr[:], ifh[:], start=True, stop=True), r=[onr_b, ifh_b], w=[ps_b[6]])
        op("vector", lambda v: v.tensor_copy(ifr[:], ps[6][:, 0:64]), r=[ps_b[6]], w=[ifr_b])
        op("tensor", lambda t: t.matmul(ps[6][:, 0:64], onr[:], ifd[:], start=True, stop=True), r=[onr_b, ifd_b], w=[ps_b[6]])
        op("vector", lambda v: v.tensor_copy(tmpr[:], ps[6][:, 0:64]), r=[ps_b[6]], w=[tmpr_b])
        op("vector", lambda v: v.tensor_scalar(tmpr[:], tmpr[:], kk[:], None, ALU.mult), r=[tmpr_b, kk_b], w=[tmpr_b])
        op("vector", lambda v: v.tensor_tensor(ifr[:], ifr[:], tmpr[:], ALU.add), r=[ifr_b, tmpr_b], w=[ifr_b])

        op("scalar", lambda c: c.activation(tmp2[:], gn[:], AF.Abs), r=[gn_b], w=[tmp2_b])
        op("tensor", lambda t: t.matmul(ps[6][:, 0:16], onr[:], tmp2[:], start=True, stop=True), r=[onr_b, tmp2_b], w=[ps_b[6]])
        op("vector", lambda v: v.tensor_scalar(mneg[:], ps[6][:, 0:16], -11.35, -2.0, ALU.mult, ALU.add), r=[ps_b[6]], w=[mneg_b])
        op("tensor", lambda t: t.matmul(ps[6][:, 0:16], onr[:], gn[:], start=True, stop=True), r=[onr_b, gn_b], w=[ps_b[6]])
        op("vector", lambda v: v.tensor_scalar(gnb[:], ps[6][:, 0:16], rs128[:], None, ALU.mult), r=[ps_b[6], rs128_b], w=[gnb_b])

        op("vector", lambda v: v.memset(vaug[:], 1.0), w=[vaug_b])

        def rope_tables(n, tcol_t, tcol_bb, cos_t, cos_bb, sin_t, sin_bb):
            for blk in range(n):
                cs = slice(blk * 64, blk * 64 + 64)
                op("vector", lambda v, blk=blk: v.tensor_scalar(
                    ang[:], ifr[:], tcol_t[:, blk:blk + 1], None, ALU.mult), r=[ifr_b, tcol_bb], w=[ang_b])
                for tgt, tgtb, shift in ((sin_t, sin_bb, 0.0), (cos_t, cos_bb, np.pi / 2)):
                    if shift:
                        op("vector", lambda v: v.tensor_scalar(ang2[:], ang[:], shift, None, ALU.add), r=[ang_b], w=[ang2_b])
                        src, srcb = ang2, ang2_b
                    else:
                        src, srcb = ang, ang_b
                    op("vector", lambda v, src=src: v.tensor_scalar(angk[:], src[:], INV2PI, None, ALU.mult), r=[srcb], w=[angk_b])
                    op("vector", lambda v: v.tensor_scalar(angk[:], angk[:], ce[:], None, ALU.add), r=[angk_b, ce_b], w=[angk_b])
                    op("vector", lambda v: v.tensor_scalar(angk[:], angk[:], ce[:], None, ALU.subtract), r=[angk_b, ce_b], w=[angk_b])
                    op("vector", lambda v: v.tensor_scalar(angt[:], angk[:], RC1, None, ALU.mult), r=[angk_b], w=[angt_b])
                    op("vector", lambda v, src=src: v.tensor_tensor(angt[:], src[:], angt[:], ALU.subtract), r=[srcb, angt_b], w=[angt_b])
                    op("vector", lambda v: v.tensor_scalar(tmpr[:], angk[:], RC2B, None, ALU.mult), r=[angk_b], w=[tmpr_b])
                    op("vector", lambda v: v.tensor_scalar(angk[:], angk[:], RC2A, None, ALU.mult), r=[angk_b], w=[angk_b])
                    op("vector", lambda v: v.tensor_tensor(angt[:], angt[:], angk[:], ALU.subtract), r=[angt_b, angk_b], w=[angt_b])
                    op("vector", lambda v: v.tensor_tensor(angt[:], angt[:], tmpr[:], ALU.subtract), r=[angt_b, tmpr_b], w=[angt_b])
                    op("vector", lambda v: v.tensor_scalar(angt[:], angt[:], PI_CLAMP, -PI_CLAMP, ALU.min, ALU.max), r=[angt_b], w=[angt_b])
                    op("scalar", lambda c, tgt=tgt, cs=cs: c.activation(tgt[:, cs], angt[:], AF.Sin), r=[angt_b], w=[tgtb])

        def rowscale_pass(src_h, nchunks):
            for c in range(nchunks):
                wb, wbb = (wld0, wld0_b) if c % 2 == 0 else (wld1, wld1_b)
                dma("sync", lambda e, c=c, wb=wb: e.dma_start(
                    out=wb[:], in_=src_h[c * 128:(c + 1) * 128, :]), w=[wbb])
                op("vector", lambda v, wb=wb: v.reduce_max(colA[:], wb[:], axis=AX.X, apply_absolute_value=True), r=[wbb], w=[colA_b])
                op("vector", lambda v, c=c: v.tensor_scalar(wscol[:, c:c + 1], colA[:], r127[:], 1e-5, ALU.mult, ALU.max), r=[colA_b, r127_b], w=[wscol_b])

        def build_row(chunks, rstg, rstg_b, rstgi, rstgi_b):
            for i, c in enumerate(chunks):
                op("tensor", lambda t, c=c: t.matmul(ps[7][0:1, 0:128], wscol[:, c:c + 1], idnf[:], start=True, stop=True), r=[wscol_b, idnf_b], w=[ps_b[7]])
                op("vector", lambda v, i=i, rstg=rstg: v.tensor_copy(rstg[:, i * 128:(i + 1) * 128], ps[7][0:1, 0:128]), r=[ps_b[7]], w=[rstg_b])
            op("vector", lambda v, rstg=rstg, rstgi=rstgi: v.reciprocal(rstgi[:], rstg[:]), r=[rstg_b], w=[rstgi_b])

        def rep_to_ps(bank, row_t, row_bb):
            op("tensor", lambda t: t.matmul(ps[bank][:], onr[:], row_t[:], start=True, stop=True),
               r=[onr_b, row_bb], w=[ps_b[bank]])

        def quant_w_tile(srcT_h, col0, ic, dst_cs, repbank, buf, bufb, clip1=False, scalcol=None, scalcol_b=None):
            dma("sync", lambda e: e.dma_start(out=buf[:], in_=srcT_h[ic * 128:(ic + 1) * 128, col0:col0 + 512]), w=[bufb])
            if scalcol is not None:
                op("vector", lambda v: v.tensor_scalar(buf[:], buf[:], scalcol[:], None, ALU.mult), r=[bufb, scalcol_b], w=[bufb])
            else:
                op("vector", lambda v: v.tensor_tensor(buf[:], buf[:], ps[repbank][:], ALU.mult), r=[bufb, ps_b[repbank]], w=[bufb])
            op("vector", lambda v: v.tensor_scalar(buf[:], buf[:], ce[:], None, ALU.add), r=[bufb, ce_b], w=[bufb])
            if clip1:
                op("vector", lambda v: v.tensor_scalar(buf[:], buf[:], ce[:], None, ALU.subtract), r=[bufb, ce_b], w=[bufb])
                op("vector", lambda v: v.tensor_scalar(wqg[:, dst_cs], buf[:], 1.0, -1.0, ALU.min, ALU.max), r=[bufb], w=[wqg_b])
            else:
                op("vector", lambda v: v.tensor_scalar(wqg[:, dst_cs], buf[:], ce[:], None, ALU.subtract), r=[bufb, ce_b], w=[wqg_b])

        def ln_quant(src_ap, idx, adeq_t, adeq_bb, af_t, af_bb, xin, xin_b):
            dma("sync", lambda e: e.dma_start(out=xin[:], in_=src_ap), w=[xin_b])
            op("vector", lambda v: v.reduce_sum(colA[:], xin[:], axis=AX.X), r=[xin_b], w=[colA_b])
            op("vector", lambda v: v.tensor_scalar(colA[:], colA[:], 1.0 / D, None, ALU.mult), r=[colA_b], w=[colA_b])
            op("vector", lambda v: v.tensor_scalar(xn[:], xin[:], colA[:], None, ALU.subtract), r=[xin_b, colA_b], w=[xn_b])
            op("scalar", lambda c: c.activation(tq[:], xn[:], AF.Square), r=[xn_b], w=[tq_b])
            op("vector", lambda v: v.reduce_sum(colA[:], tq[:], axis=AX.X), r=[tq_b], w=[colA_b])
            op("vector", lambda v: v.tensor_scalar(colA[:], colA[:], 1.0 / D, EPS_LN, ALU.mult, ALU.add), r=[colA_b], w=[colA_b])
            op("scalar", lambda c: c.sqrt(colA[:], colA[:]), r=[colA_b], w=[colA_b])
            op("vector", lambda v: v.reciprocal(colA[:], colA[:]), r=[colA_b], w=[colA_b])
            op("vector", lambda v: v.tensor_scalar(xn[:], xn[:], colA[:], None, ALU.mult), r=[xn_b, colA_b], w=[xn_b])
            op("vector", lambda v: v.reduce_max(colA[:], xn[:], axis=AX.X, apply_absolute_value=True), r=[xn_b], w=[colA_b])
            if af_t is not None:
                op("vector", lambda v: v.tensor_scalar(af_t[:, idx:idx + 1], colA[:], 1e-8, None, ALU.max), r=[colA_b], w=[af_bb])
            op("vector", lambda v: v.tensor_scalar(colB[:], colA[:], r127[:], 1e-5, ALU.mult, ALU.max), r=[colA_b, r127_b], w=[colB_b])
            op("vector", lambda v: v.tensor_copy(adeq_t[:, idx:idx + 1], colB[:]), r=[colB_b], w=[adeq_bb])
            op("vector", lambda v: v.reciprocal(colB[:], colB[:]), r=[colB_b], w=[colB_b])
            op("vector", lambda v: v.tensor_scalar(tq[:], xn[:], colB[:], None, ALU.mult), r=[xn_b, colB_b], w=[tq_b])
            op("vector", lambda v: v.tensor_scalar(tq[:], tq[:], ce[:], None, ALU.add), r=[tq_b, ce_b], w=[tq_b])
            op("vector", lambda v: v.tensor_scalar(xq[:], tq[:], ce[:], None, ALU.subtract), r=[tq_b, ce_b], w=[xq_b])

        def transpose16(src_t, src_bb, dst_ap_fn, dst_bb, bank0=0, bank1=1):
            for g4 in range(4):
                bank = bank0 if g4 % 2 == 0 else bank1
                opg("tensor", [
                    (lambda t, c=g4 * 4 + i, i=i, bank=bank: t.matmul(
                        ps[bank][:, i * 128:(i + 1) * 128], src_t[:, c * 128:(c + 1) * 128], idn[:],
                        start=(i == 0), stop=(i == 3))) for i in range(4)],
                    r=[src_bb, idn_b], w=[ps_b[bank]])
                op("vector", lambda v, g4=g4, bank=bank: v.tensor_copy(
                    dst_ap_fn(g4), ps[bank][:].rearrange("p (c t) -> p c t", c=4)), r=[ps_b[bank]], w=[dst_bb])

        def rope_tm(work_t, work_bb, nheads, cos_t, cos_bb, sin_t, sin_bb, blk, w2, w2b):
            cs = slice(blk * 64, blk * 64 + 64)
            nb_ = nheads * 128
            op("vector", lambda v: v.tensor_copy(sins[:, 0:64], sin_t[:, cs]), r=[sin_bb], w=[sins_b])
            op("vector", lambda v: v.tensor_scalar(sins[:, 64:128], sin_t[:, cs], -1.0, None, ALU.mult), r=[sin_bb], w=[sins_b])
            wv = lambda t_, a, b: t_[:, 0:nb_].rearrange("p (h x d) -> p h x d", h=nheads, x=2)[:, :, a:b, :]
            op("vector", lambda v: v.tensor_copy(wv(w2, 0, 1), wv(work_t, 1, 2)), r=[work_bb], w=[w2b])
            op("vector", lambda v: v.tensor_copy(wv(w2, 1, 2), wv(work_t, 0, 1)), r=[work_bb], w=[w2b])
            cosv = cos_t[:, cs].unsqueeze(1).unsqueeze(2).broadcast_to((128, nheads, 2, 64))
            sinv = sins[:, 0:128].unsqueeze(1).broadcast_to((128, nheads, 128))
            op("vector", lambda v: v.tensor_tensor(
                work_t[:, 0:nb_].rearrange("p (h x d) -> p h x d", h=nheads, x=2),
                work_t[:, 0:nb_].rearrange("p (h x d) -> p h x d", h=nheads, x=2), cosv, ALU.mult),
                r=[work_bb, cos_bb], w=[work_bb])
            op("vector", lambda v: v.tensor_tensor(
                w2[:, 0:nb_].rearrange("p (h d) -> p h d", h=nheads),
                w2[:, 0:nb_].rearrange("p (h d) -> p h d", h=nheads), sinv, ALU.mult),
                r=[w2b, sins_b], w=[w2b])
            op("vector", lambda v: v.tensor_tensor(work_t[:, 0:nb_], work_t[:, 0:nb_], w2[:, 0:nb_], ALU.add),
               r=[work_bb, w2b], w=[work_bb])

        rope_tables(NB, tposb, tposb_b, cosT, cosT_b, sinT, sinT_b)
        rowscale_pass(wqkv_R, 24)
        for g in range(2):
            build_row(range(16 + g * 4, 16 + (g + 1) * 4), rowstg1, rowstg1_b, rowstgi1, rowstgi1_b)
            rep_to_ps(5, rowstg1, rowstg1_b)
            op("vector", lambda v, g=g: v.tensor_copy(wskv_rep[:, g * 512:(g + 1) * 512], ps[5][:]), r=[ps_b[5]], w=[wskv_rep_b])
        for og in range(2):
            col0 = 2048 + og * 512
            build_row(range(16 + og * 4, 16 + (og + 1) * 4), rowstg1, rowstg1_b, rowstgi1, rowstgi1_b)
            rep_to_ps(5, rowstgi1, rowstgi1_b)
            for ic in range(NIC):
                buf, bufb = (wtt0, wtt0_b) if ic % 2 == 0 else (wtt1, wtt1_b)
                dma("sync", lambda e, ic=ic, col0=col0, buf=buf: e.dma_start(
                    out=buf[:], in_=wqkv_T[ic * 128:(ic + 1) * 128, col0:col0 + 512]), w=[bufb])
                op("vector", lambda v, buf=buf: v.tensor_tensor(buf[:], buf[:], ps[5][:], ALU.mult), r=[bufb, ps_b[5]], w=[bufb])
                op("vector", lambda v, buf=buf: v.tensor_scalar(buf[:], buf[:], ce[:], None, ALU.add), r=[bufb, ce_b], w=[bufb])
                op("vector", lambda v, ic=ic, og=og, buf=buf: v.tensor_scalar(
                    wkvq[:, ic * 1024 + og * 512: ic * 1024 + (og + 1) * 512], buf[:], ce[:], None, ALU.subtract),
                    r=[bufb, ce_b], w=[wkvq_b])
        S_.barrier()

        for blk in range(NB):
            xin, xin_b = (xt0, xt0_b) if blk % 2 == 0 else (xt1, xt1_b)
            ln_quant(x_full[blk * 128:(blk + 1) * 128, :], blk, adeq, adeq_b, None, None, xin, xin_b)
            transpose16(xq, xq_b, lambda g4: xqt_t[:, g4 * 512:(g4 + 1) * 512].rearrange("p (c t) -> p c t", c=4), xqt_t_b, 0, 1)
            for og in range(2):
                bank = 2 + og
                opg("tensor", [
                    (lambda t, ic=ic, og=og, bank=bank: t.matmul(
                        ps[bank][:], xqt_t[:, ic * 128:(ic + 1) * 128],
                        wkvq[:, ic * 1024 + og * 512: ic * 1024 + (og + 1) * 512],
                        start=(ic == 0), stop=(ic == NIC - 1))) for ic in range(NIC)],
                    r=[xqt_t_b, wkvq_b], w=[ps_b[bank]])
                op("vector", lambda v, bank=bank, blk=blk: v.tensor_scalar(
                    kwork[:], ps[bank][:], adeq[:, blk:blk + 1], None, ALU.mult), r=[ps_b[bank], adeq_b], w=[kwork_b])
                op("vector", lambda v, og=og: v.tensor_tensor(
                    kwork[:], kwork[:], wskv_rep[:, og * 512:(og + 1) * 512], ALU.mult), r=[kwork_b, wskv_rep_b], w=[kwork_b])
                if og == 0:
                    op("scalar", lambda c: c.activation(kwork2[:], kwork[:], AF.Square), r=[kwork_b], w=[kwork2_b])
                    for kv in range(KVH):
                        op("vector", lambda v, kv=kv: v.reduce_sum(colS[:, kv:kv + 1], kwork2[:, kv * 128:(kv + 1) * 128], axis=AX.X), r=[kwork2_b], w=[colS_b])
                    op("vector", lambda v: v.tensor_scalar(colS[:], colS[:], 1.0 / 128.0, EPS_RMS, ALU.mult, ALU.add), r=[colS_b], w=[colS_b])
                    op("scalar", lambda c: c.sqrt(colS[:], colS[:]), r=[colS_b], w=[colS_b])
                    op("vector", lambda v: v.reciprocal(colS[:], colS[:]), r=[colS_b], w=[colS_b])
                    for kv in range(KVH):
                        op("vector", lambda v, kv=kv: v.tensor_scalar(
                            kwork[:, kv * 128:(kv + 1) * 128], kwork[:, kv * 128:(kv + 1) * 128],
                            colS[:, kv:kv + 1], None, ALU.mult), r=[kwork_b, colS_b], w=[kwork_b])
                    rope_tm(kwork, kwork_b, KVH, cosT, cosT_b, sinT, sinT_b, blk, kwork2, kwork2_b)
                    op("vector", lambda v: v.tensor_copy(kbf[:], kwork[:]), r=[kwork_b], w=[kbf_b])
                    opg("tensor", [
                        (lambda t, kv=kv: t.matmul(
                            ps[4][:, kv * 128:(kv + 1) * 128], kbf[:, kv * 128:(kv + 1) * 128], idn[:],
                            start=(kv == 0), stop=(kv == 3))) for kv in range(KVH)],
                        r=[kbf_b, idn_b], w=[ps_b[4]])
                    op("vector", lambda v, blk=blk: v.tensor_copy(
                        ktf.rearrange("p (kv s) -> p kv s", kv=KVH)[:, :, blk * 128:(blk + 1) * 128],
                        ps[4][:].rearrange("p (kv r) -> p kv r", kv=KVH)), r=[ps_b[4]], w=[ktf_b])
                else:
                    op("vector", lambda v, blk=blk: v.tensor_copy(
                        vaug.rearrange("p (kv nb c) -> p kv nb c", kv=KVH, nb=NB)[:, :, blk:blk + 1, 0:128],
                        kwork[:].rearrange("p (kv o d) -> p kv o d", kv=KVH, o=1)), r=[kwork_b], w=[vaug_b])
        S_.barrier()

        rope_tables(NOB, tqposb, tqposb_b, cosQ, cosQ_b, sinQ, sinQ_b)
        for ot in range(NOB):
            xin, xin_b = (xt0, xt0_b) if ot % 2 == 0 else (xt1, xt1_b)
            ln_quant(x_own[ot * 128:(ot + 1) * 128, :], ot, adeqo, adeqo_b, afo, afo_b, xin, xin_b)
            transpose16(xq, xq_b, lambda g4: xqt_t[:, g4 * 512:(g4 + 1) * 512].rearrange("p (c t) -> p c t", c=4), xqt_t_b, 0, 1)
            dma("sync", lambda e, ot=ot: e.dma_start(
                out=sc_xq[ot].rearrange("ic p t -> p ic t"),
                in_=xqt_t[:].rearrange("p (ic t) -> p ic t", ic=NIC)), r=[xqt_t_b])
        for og in range(4):
            build_row(range(og * 4, (og + 1) * 4), rowstg3, rowstg3_b, rowstgi3, rowstgi3_b)
            rep_to_ps(5, rowstg3, rowstg3_b)
            op("vector", lambda v: v.tensor_copy(wsqd[:], ps[5][:]), r=[ps_b[5]], w=[wsqd_b])
            rep_to_ps(5, rowstgi3, rowstgi3_b)
            for ic in range(NIC):
                buf, bufb = (wtt0, wtt0_b) if ic % 2 == 0 else (wtt1, wtt1_b)
                quant_w_tile(wqkv_T, og * 512, ic, slice(ic * 512, (ic + 1) * 512), 5, buf, bufb)
            def q_chain(ot):
                bank = 2 + ot % 2
                xs, xs_b = (xqs0, xqs0_b) if ot % 2 == 0 else (xqs1, xqs1_b)
                dma("sync", lambda e, ot=ot, xs=xs: e.dma_start(
                    out=xs[:].rearrange("p (ic t) -> p ic t", ic=NIC),
                    in_=sc_xq[ot].rearrange("ic p t -> p ic t")), w=[xs_b])
                opg("tensor", [
                    (lambda t, ic=ic, bank=bank, xs=xs: t.matmul(
                        ps[bank][:], xs[:, ic * 128:(ic + 1) * 128],
                        wqg[:, ic * 512:(ic + 1) * 512],
                        start=(ic == 0), stop=(ic == NIC - 1))) for ic in range(NIC)],
                    r=[xs_b, wqg_b], w=[ps_b[bank]])

            q_chain(0)
            for ot in range(NOB):
                bank = 2 + ot % 2
                if ot + 1 < NOB:
                    q_chain(ot + 1)
                op("vector", lambda v, bank=bank, ot=ot: v.tensor_scalar(
                    kwork[:], ps[bank][:], adeqo[:, ot:ot + 1], None, ALU.mult), r=[ps_b[bank], adeqo_b], w=[kwork_b])
                op("vector", lambda v: v.tensor_tensor(kwork[:], kwork[:], wsqd[:], ALU.mult), r=[kwork_b, wsqd_b], w=[kwork_b])
                op("scalar", lambda c: c.activation(kwork2[:], kwork[:], AF.Square), r=[kwork_b], w=[kwork2_b])
                for i in range(4):
                    op("vector", lambda v, i=i: v.reduce_sum(colS[:, i:i + 1], kwork2[:, i * 128:(i + 1) * 128], axis=AX.X), r=[kwork2_b], w=[colS_b])
                op("vector", lambda v: v.tensor_scalar(colS[:], colS[:], 1.0 / 128.0, EPS_RMS, ALU.mult, ALU.add), r=[colS_b], w=[colS_b])
                op("scalar", lambda c: c.sqrt(colS[:], colS[:]), r=[colS_b], w=[colS_b])
                op("vector", lambda v: v.reciprocal(colS[:], colS[:]), r=[colS_b], w=[colS_b])
                op("vector", lambda v, og=og: v.tensor_tensor(colS[:], colS[:], gnb[:, og * 4:(og + 1) * 4], ALU.mult), r=[colS_b, gnb_b], w=[colS_b])
                for i in range(4):
                    op("vector", lambda v, i=i: v.tensor_scalar(
                        kwork[:, i * 128:(i + 1) * 128], kwork[:, i * 128:(i + 1) * 128],
                        colS[:, i:i + 1], None, ALU.mult), r=[kwork_b, colS_b], w=[kwork_b])
                rope_tm(kwork, kwork_b, 4, cosQ, cosQ_b, sinQ, sinQ_b, ot, kwork2, kwork2_b)
                op("vector", lambda v: v.tensor_copy(kbf[:], kwork[:]), r=[kwork_b], w=[kbf_b])
                opg("tensor", [
                    (lambda t, i=i: t.matmul(
                        ps[4][:, i * 128:(i + 1) * 128], kbf[:, i * 128:(i + 1) * 128], idn[:],
                        start=(i == 0), stop=(i == 3))) for i in range(4)],
                    r=[kbf_b, idn_b], w=[ps_b[4]])
                op("vector", lambda v: v.tensor_copy(kbf[:], ps[4][:]), r=[ps_b[4]], w=[kbf_b])
                dma("sync", lambda e, og=og, ot=ot: e.dma_start(
                    out=sc_qt[og * 4:(og + 1) * 4, :, ot * 128:(ot + 1) * 128].rearrange("h p t -> p h t"),
                    in_=kbf[:].rearrange("p (h t) -> p h t", h=4)), r=[kbf_b])
        S_.barrier()

        op("vector", lambda v: v.memset(colB[:], 0.0), w=[colB_b])
        for c in range(NIC):
            wb, wbb = (wld0, wld0_b) if c % 2 == 0 else (wld1, wld1_b)
            dma("sync", lambda e, c=c, wb=wb: e.dma_start(out=wb[:], in_=ffn_R[c * 128:(c + 1) * 128, :]), w=[wbb])
            op("vector", lambda v, wb=wb: v.reduce_sum(colA[:], wb[:], axis=AX.X, apply_absolute_value=True), r=[wbb], w=[colA_b])
            op("vector", lambda v: v.tensor_tensor(colB[:], colB[:], colA[:], ALU.add), r=[colB_b, colA_b], w=[colB_b])
        op("tensor", lambda t: t.matmul(ps[7][0:1, 0:1], colB[:], onc[:], start=True, stop=True), r=[colB_b, onc_b], w=[ps_b[7]])
        op("vector", lambda v: v.tensor_copy(tmp11[:], ps[7][0:1, 0:1]), r=[ps_b[7]], w=[tmp11_b])
        op("vector", lambda v: v.tensor_scalar(tmp11[:], tmp11[:], 1.0 / (D * D), 1e-6, ALU.mult, ALU.max), r=[tmp11_b], w=[tmp11_b])
        op("tensor", lambda t: t.matmul(ps[7][:, 0:1], onr[:], tmp11[:], start=True, stop=True), r=[onr_b, tmp11_b], w=[ps_b[7]])
        op("vector", lambda v: v.tensor_scalar(wsfd[:], ps[7][:, 0:1], r127[:], None, ALU.mult), r=[ps_b[7], r127_b], w=[wsfd_b])
        op("vector", lambda v: v.reciprocal(wsf_c[:], ps[7][:, 0:1]), r=[ps_b[7]], w=[wsf_c_b])
        for og in range(4):
            for ic in range(NIC):
                buf, bufb = (wtt0, wtt0_b) if ic % 2 == 0 else (wtt1, wtt1_b)
                quant_w_tile(ffn_T, og * 512, ic, slice(ic * 512, (ic + 1) * 512), None, buf, bufb,
                             clip1=True, scalcol=wsf_c, scalcol_b=wsf_c_b)
            def f_chain(ot):
                bank = 2 + ot % 2
                xs, xs_b = (xqs0, xqs0_b) if ot % 2 == 0 else (xqs1, xqs1_b)
                dma("sync", lambda e, ot=ot, xs=xs: e.dma_start(
                    out=xs[:].rearrange("p (ic t) -> p ic t", ic=NIC),
                    in_=sc_xq[ot].rearrange("ic p t -> p ic t")), w=[xs_b])
                opg("tensor", [
                    (lambda t, ic=ic, bank=bank, xs=xs: t.matmul(
                        ps[bank][:], xs[:, ic * 128:(ic + 1) * 128],
                        wqg[:, ic * 512:(ic + 1) * 512],
                        start=(ic == 0), stop=(ic == NIC - 1))) for ic in range(NIC)],
                    r=[xs_b, wqg_b], w=[ps_b[bank]])

            f_chain(0)
            for ot in range(NOB):
                bank = 2 + ot % 2
                if ot + 1 < NOB:
                    f_chain(ot + 1)
                op("vector", lambda v, ot=ot: v.tensor_tensor(colC[:], afo[:, ot:ot + 1], wsfd[:], ALU.mult), r=[afo_b, wsfd_b], w=[colC_b])
                op("scalar", lambda c, bank=bank: c.activation(kbf[:], ps[bank][:], AF.Silu, scale=colC[:]), r=[ps_b[bank], colC_b], w=[kbf_b])
                dma("sync", lambda e, ot=ot, og=og: e.dma_start(
                    out=sc_g[ot, :, og * 512:(og + 1) * 512], in_=kbf[:]), r=[kbf_b])
        S_.barrier()

        rowscale_pass(wo_R, NIC)
        for og in range(4):
            build_row(range(og * 4, (og + 1) * 4), rowstg1, rowstg1_b, rowstgi1, rowstgi1_b)
            rep_to_ps(5, rowstgi1, rowstgi1_b)
            for ic in range(NIC):
                buf, bufb = (wtt0, wtt0_b) if ic % 2 == 0 else (wtt1, wtt1_b)
                quant_w_tile(wo_T, og * 512, ic, slice(ic * 512, (ic + 1) * 512), 5, buf, bufb)
            dma("sync", lambda e, og=og: e.dma_start(
                out=sc_woq[:, :, og * 512:(og + 1) * 512].rearrange("ic p d -> p ic d"),
                in_=wqg[:].rearrange("p (ic d) -> p ic d", ic=NIC)), r=[wqg_b])
            rep_to_ps(5, rowstg1, rowstg1_b)
            op("vector", lambda v, og=og: v.tensor_copy(wso_rep[:, og * 512:(og + 1) * 512], ps[5][:]), r=[ps_b[5]], w=[wso_rep_b])
        S_.barrier()

        dma("sync", lambda e: e.dma_start(
            out=masksb.rearrange("p (m q) -> p m q", m=32), in_=masks_h.rearrange("m p q -> p m q")), w=[masksb_b])
        for qt in range(4):
            KMAX = 8 * qt + 8
            dma("sync", lambda e, qt=qt: e.dma_start(
                out=qt_t[:].rearrange("p (h t) -> p h t", h=H),
                in_=sc_qt[:, :, qt * 512:(qt + 1) * 512].rearrange("h p t -> p h t")), w=[qt_t_b])
            for h in range(H):
                kv = h // 4

                def st_mm(kc, kv=kv, h=h):
                    stb = 4 + (kc % 2)
                    op("tensor", lambda t, kc=kc, kv=kv, h=h, stb=stb: t.matmul(
                        ps[stb][:], ktf[:, kv * S + kc * 128: kv * S + (kc + 1) * 128],
                        qt_t[:, h * 512:(h + 1) * 512], start=True, stop=True),
                        r=[ktf_b, qt_t_b], w=[ps_b[stb]])

                st_mm(0)
                for kc in range(KMAX):
                    stb = 4 + (kc % 2)
                    if kc + 1 < KMAX:
                        st_mm(kc + 1)
                    e_t, e_b = (e0, e0_b) if kc % 2 == 0 else (e1, e1_b)
                    op("scalar", lambda c, h=h, stb=stb, e_t=e_t: c.activation(
                        e_t[:], ps[stb][:], AF.Exp, bias=mneg[:, h:h + 1]), r=[ps_b[stb], mneg_b], w=[e_b])
                    mmfns = []
                    wbanks = []
                    for j in range(4):
                        dd = kc - (8 * qt + 2 * j)
                        if dd > 1:
                            continue
                        if dd >= 0:
                            m_idx = (qt * 4 + j) * 2 + dd
                            op("vector", lambda v, j=j, m_idx=m_idx, e_t=e_t: v.tensor_tensor(
                                e_t[:, j * 128:(j + 1) * 128], e_t[:, j * 128:(j + 1) * 128],
                                masksb[:, m_idx * 128:(m_idx + 1) * 128], ALU.mult), r=[masksb_b, e_b], w=[e_b])
                        last_kc = min(8 * qt + 2 * j + 1, KMAX - 1)
                        mmfns.append(lambda t, j=j, kc=kc, kv=kv, e_t=e_t, last_kc=last_kc: t.matmul(
                            ps[j][:, 0:129], e_t[:, j * 128:(j + 1) * 128],
                            vaug[:, (kv * NB + kc) * 129: (kv * NB + kc + 1) * 129],
                            start=(kc == 0), stop=(kc == last_kc)))
                        wbanks.append(ps_b[j])
                    opg("tensor", mmfns, r=[e_b, vaug_b], w=wbanks)
                for j in range(4):
                    op("vector", lambda v, j=j, h=h: v.tensor_copy(
                        y_tm[:, j * D + h * 128: j * D + (h + 1) * 128], ps[j][:, 0:128]), r=[ps_b[j]], w=[y_tm_b])
                    op("vector", lambda v, j=j, h=h: v.tensor_copy(
                        zcol[:, j * H + h: j * H + h + 1], ps[j][:, 128:129]), r=[ps_b[j]], w=[zcol_b])
            for j in range(4):
                op("vector", lambda v, j=j: v.reciprocal(zinv[:, j * H:(j + 1) * H], zcol[:, j * H:(j + 1) * H]), r=[zcol_b], w=[zinv_b])
                for h in range(H):
                    op("vector", lambda v, j=j, h=h: v.tensor_scalar(
                        y_tm[:, j * D + h * 128: j * D + (h + 1) * 128],
                        y_tm[:, j * D + h * 128: j * D + (h + 1) * 128],
                        zinv[:, j * H + h: j * H + h + 1], None, ALU.mult), r=[zinv_b, y_tm_b], w=[y_tm_b])
                op("vector", lambda v, j=j: v.reduce_max(colA[:], y_tm[:, j * D:(j + 1) * D], axis=AX.X, apply_absolute_value=True), r=[y_tm_b], w=[colA_b])
                op("vector", lambda v: v.tensor_scalar(colB[:], colA[:], r127[:], 1e-5, ALU.mult, ALU.max), r=[colA_b, r127_b], w=[colB_b])
                op("vector", lambda v, j=j: v.tensor_copy(adqy[:, j:j + 1], colB[:]), r=[colB_b], w=[adqy_b])
                op("vector", lambda v: v.reciprocal(colB[:], colB[:]), r=[colB_b], w=[colB_b])
                op("vector", lambda v, j=j: v.tensor_scalar(tqT[:], y_tm[:, j * D:(j + 1) * D], colB[:], None, ALU.mult), r=[y_tm_b, colB_b], w=[tqT_b])
                op("vector", lambda v: v.tensor_scalar(tqT[:], tqT[:], ce[:], None, ALU.add), r=[tqT_b, ce_b], w=[tqT_b])
                op("vector", lambda v: v.tensor_scalar(yq[:], tqT[:], ce[:], None, ALU.subtract), r=[tqT_b, ce_b], w=[yq_b])
                transpose16(yq, yq_b, lambda g4, j=j: yqt[:, j * D + g4 * 512: j * D + (g4 + 1) * 512].rearrange("p (c t) -> p c t", c=4), yqt_b, 6, 7)
            for j in range(4):
                ot = qt * 4 + j
                dma("sync", lambda e, ot=ot, j=j: e.dma_start(
                    out=gbf[:, j * D:(j + 1) * D], in_=sc_g[ot, :, :]), w=[gbf_b])
            for og in range(4):
                dma("sync", lambda e, og=og: e.dma_start(
                    out=wqg[:].rearrange("p (ic d) -> p ic d", ic=NIC),
                    in_=sc_woq[:, :, og * 512:(og + 1) * 512].rearrange("ic p d -> p ic d")), w=[wqg_b])
                for j in range(4):
                    ot = qt * 4 + j
                    bank = 6 + (j % 2)
                    opg("tensor", [
                        (lambda t, j=j, ic=ic, bank=bank: t.matmul(
                            ps[bank][:], yqt[:, j * D + ic * 128: j * D + (ic + 1) * 128],
                            wqg[:, ic * 512:(ic + 1) * 512],
                            start=(ic == 0), stop=(ic == NIC - 1))) for ic in range(NIC)],
                        r=[yqt_b, wqg_b], w=[ps_b[bank]])
                    op("vector", lambda v, j=j, bank=bank: v.tensor_scalar(
                        sdqt[:], ps[bank][:], adqy[:, j:j + 1], None, ALU.mult), r=[ps_b[bank], adqy_b], w=[sdqt_b])
                    op("vector", lambda v, og=og: v.tensor_tensor(
                        sdqt[:], sdqt[:], wso_rep[:, og * 512:(og + 1) * 512], ALU.mult), r=[wso_rep_b, sdqt_b], w=[sdqt_b])
                    dma("sync", lambda e, ot=ot, og=og: e.dma_start(
                        out=xrt[:], in_=x_own[ot * 128:(ot + 1) * 128, og * 512:(og + 1) * 512]), w=[xrt_b])
                    op("vector", lambda v, j=j, og=og: v.tensor_copy(
                        gft[:], gbf[:, j * D + og * 512: j * D + (og + 1) * 512]), r=[gbf_b], w=[gft_b])
                    op("vector", lambda v: v.tensor_tensor(outt[:], sdqt[:], gft[:], ALU.mult), r=[sdqt_b, gft_b], w=[outt_b])
                    op("vector", lambda v: v.tensor_tensor(outt[:], outt[:], xrt[:], ALU.add), r=[outt_b, xrt_b], w=[outt_b])
                    dma("sync", lambda e, ot=ot, og=og: e.dma_start(
                        out=out_h[ot * 128:(ot + 1) * 128, og * 512:(og + 1) * 512], in_=outt[:]), r=[outt_b])

        S_.wait("sync", [(s, S_.cnt[s]) for s in ALL_DMA_SEMS if S_.cnt[s] > 0])

        sems = {}
        for name in list(ENG_SEM.values()) + ALL_DMA_SEMS:
            sems[name] = es.enter_context(nc.semaphore(name))
        block = es.enter_context(nc.Block())

        def mk(engname):
            items = S_.items[engname]
            osem = ENG_SEM[engname]

            def body(proxy):
                waited = {}
                for it in items:
                    if it[0] == "w":
                        _, s, v = it
                        if waited.get(s, -1) >= v:
                            continue
                        proxy.wait_ge(sems[s], v)
                        waited[s] = v
                    elif it[0] == "op":
                        it[1](proxy).then_inc(sems[osem], 1)
                    elif it[0] == "opg":
                        for f in it[1][:-1]:
                            f(proxy)
                        it[1][-1](proxy).then_inc(sems[osem], 1)
                    else:
                        it[1](proxy).then_inc(sems[it[2]], 16)
            return body

        block.tensor(mk("tensor"))
        block.vector(mk("vector"))
        block.scalar(mk("scalar"))
        block.sync(mk("sync"))
        block.gpsimd(mk("gpsimd"))

    return nc


def kernel(L0_attn_x, L0_attn_w_q, L0_attn_w_k, L0_attn_w_v, L0_attn_w_o,
           L0_attn_q_gain, L0_ffn_weight):
    x = np.ascontiguousarray(np.asarray(L0_attn_x, dtype=np.float32))
    wq = np.asarray(L0_attn_w_q, dtype=np.float32)
    wk = np.asarray(L0_attn_w_k, dtype=np.float32)
    wv = np.asarray(L0_attn_w_v, dtype=np.float32)
    wo = np.asarray(L0_attn_w_o, dtype=np.float32)
    gain = np.asarray(L0_attn_q_gain, dtype=np.float32)
    wf = np.asarray(L0_ffn_weight, dtype=np.float32)

    wqkv_R = np.ascontiguousarray(np.concatenate([wq, wk, wv], axis=0))
    wqkv_T = np.ascontiguousarray(wqkv_R.T)
    wo_T = np.ascontiguousarray(wo.T)
    ffn_T = np.ascontiguousarray(wf.T)

    idnb = np.eye(128, dtype=ml_dtypes.bfloat16)
    idnf = np.eye(128, dtype=np.float32)
    ones_r = np.ones((1, 128), dtype=np.float32)
    ones_c = np.ones((128, 1), dtype=np.float32)
    iidx = np.arange(0, 128, 2, dtype=np.float32).reshape(1, 64)
    ifreq = (1.0 / (10000.0 ** (np.arange(0, 128, 2, dtype=np.float64) / 128.0))).astype(np.float32).reshape(1, 64)
    tpos = np.ascontiguousarray(np.arange(S, dtype=np.float32).reshape(NB, 128).T)

    in_maps = []
    block_maps = []
    for c in range(8):
        b, h = c // 2, c % 2
        obs = own_blocks(h)
        block_maps.append(obs)
        xb = np.ascontiguousarray(x[b])
        x_own = np.ascontiguousarray(xb.reshape(NB, 128, D)[obs].reshape(NOB * 128, D))
        tq_own = np.asarray([g * 128 + i for g in obs for i in range(128)], dtype=np.float32)
        tqpos = np.ascontiguousarray(tq_own.reshape(NOB, 128).T)
        masks = np.zeros((32, 128, 128), dtype=ml_dtypes.bfloat16)
        tri = np.triu(np.ones((128, 128), dtype=np.float32))
        for t in range(4):
            for j in range(4):
                g = obs[t * 4 + j]
                for slot in range(2):
                    kc = 8 * t + 2 * j + slot
                    if kc < g:
                        m = np.ones((128, 128), dtype=np.float32)
                    elif kc == g:
                        m = tri
                    else:
                        m = np.zeros((128, 128), dtype=np.float32)
                    masks[(t * 4 + j) * 2 + slot] = m.astype(ml_dtypes.bfloat16)
        in_maps.append({
            "x_full": xb, "x_own": x_own,
            "wqkv_T": wqkv_T, "wqkv_R": wqkv_R, "wo_T": wo_T, "wo_R": wo,
            "ffn_T": ffn_T, "ffn_R": wf,
            "gains": gain.reshape(1, H), "idnb": idnb, "idnf": idnf,
            "ones_r": ones_r, "ones_c": ones_c,
            "tpos": tpos, "tqpos": tqpos, "ifreq": ifreq, "iidx": iidx,
            "masks": masks,
        })

    nc = build_nc()
    res = run_bass_kernel_spmd(nc, in_maps, core_ids=list(range(8)))

    out = np.empty((B, S, D), dtype=np.float32)
    for c in range(8):
        b = c // 2
        shard = np.asarray(res.results[c]["out_shard"]).reshape(NOB, 128, D)
        for i, g in enumerate(block_maps[c]):
            out[b, g * 128:(g + 1) * 128, :] = shard[i]
    return out


# version 08
# speedup vs baseline: 1.1966x; 1.0424x over previous
import contextlib
import numpy as np
import ml_dtypes
import concourse.bass as bass
import concourse.mybir as mybir
from concourse.bass_utils import run_bass_kernel_spmd

F32 = mybir.dt.float32
BF16 = mybir.dt.bfloat16
AF = mybir.ActivationFunctionType
ALU = mybir.AluOpType
AX = mybir.AxisListType

B, S, D = 4, 4096, 2048
H, HD, KVH = 16, 128, 4
NB = S // 128
NOB = 16
NIC = D // 128
EPS_LN = 1e-5
EPS_RMS = 1.1920929e-7
C23 = 1.5 * 2.0 ** 23
CDELTA = 1.5 * 2.0 ** 52 - 1.5 * 2.0 ** 23
SQ128 = float(np.sqrt(128.0))
TWO_PI = 2.0 * np.pi
INV2PI = 1.0 / TWO_PI
RC1 = 6.28125
RC2 = float(np.float64(TWO_PI) - RC1)
RC2A = float(np.float32(RC2))
RC2B = float(np.float64(RC2) - np.float64(np.float32(RC2)))
EXSC = float(-np.log(10000.0) / 128.0)
EXSCA = float(np.float32(EXSC))
EXSCB = float(np.float64(EXSC) - np.float64(np.float32(EXSC)))
PI_CLAMP = 3.1415925
LN1E4 = float(np.log(10000.0))

ENGS = ("tensor", "vector", "scalar", "sync", "gpsimd")
DMA_SEMS = {"sync": ["Yd%d" % i for i in range(6)], "scalar": ["Cd0"], "gpsimd": ["Gd0"]}
ALL_DMA_SEMS = [s for v in DMA_SEMS.values() for s in v]
ENG_SEM = {"tensor": "T", "vector": "V", "scalar": "C", "sync": "Y", "gpsimd": "G"}


def own_blocks(h):
    return sorted([2 * i + h for i in range(8)] + [16 + 2 * i + (1 - h) for i in range(8)])


class Sched:
    def __init__(self):
        self.items = {e: [] for e in ENGS}
        self.cnt = {s: 0 for s in list(ENG_SEM.values()) + ALL_DMA_SEMS}
        self.rr = {e: 0 for e in DMA_SEMS}

    def wait(self, eng, deps):
        for d in deps:
            if d is not None:
                self.items[eng].append(("w", d[0], d[1]))

    def emit(self, eng, kind, fn, deps):
        st = self.items[eng]
        for d in deps:
            if d is not None:
                st.append(("w", d[0], d[1]))
        if kind == "dma":
            sl = DMA_SEMS[eng]
            sem = sl[self.rr[eng] % len(sl)]
            self.rr[eng] += 1
            if self.cnt[sem] > 0:
                st.append(("w", sem, self.cnt[sem]))
            st.append((kind, fn, sem))
            self.cnt[sem] += 16
            return (sem, self.cnt[sem])
        st.append((kind, fn))
        sem = ENG_SEM[eng]
        self.cnt[sem] += 1
        return (sem, self.cnt[sem])

    def barrier(self):
        snap = dict(self.cnt)
        for e in ENGS:
            for s, v in snap.items():
                if v > 0:
                    self.items[e].append(("w", s, v))


class Buf:
    __slots__ = ("w", "rs")

    def __init__(self):
        self.w = None
        self.rs = []


S_ = None


def _emit(eng, kind, fn, r, w):
    deps = [b.w for b in r]
    for b in w:
        deps.append(b.w)
        deps.extend(b.rs)
    tok = S_.emit(eng, kind, fn, deps)
    for b in r:
        b.rs.append(tok)
    for b in w:
        b.w = tok
        b.rs = []
    return tok


def op(eng, fn, r=(), w=()):
    return _emit(eng, "op", fn, r, w)


def opg(eng, fns, r=(), w=()):
    return _emit(eng, "opg", fns, r, w)


def dma(eng, fn, r=(), w=()):
    return _emit(eng, "dma", fn, r, w)


def _rope_tables_host():
    try:
        import jax
        import jax.numpy as jnp
        cpu = jax.devices("cpu")[0]
        with jax.default_device(cpu):
            inv = 1.0 / 10000.0 ** (jnp.arange(0, 128, 2, dtype=jnp.float32) / 128)
            t = jnp.arange(S, dtype=jnp.float32)
            fr = jnp.outer(t, inv)
            cosv = np.asarray(jnp.cos(fr), dtype=np.float32)
            sinv = np.asarray(jnp.sin(fr), dtype=np.float32)
    except Exception:
        inv = (1.0 / (10000.0 ** (np.arange(0, 128, 2, dtype=np.float32) / np.float32(128)).astype(np.float64))).astype(np.float32)
        fr = (np.arange(S, dtype=np.float32)[:, None] * inv[None, :]).astype(np.float32)
        cosv = np.cos(fr).astype(np.float32)
        sinv = np.sin(fr).astype(np.float32)
    return (np.ascontiguousarray(cosv.reshape(NB, 128, 64)),
            np.ascontiguousarray(sinv.reshape(NB, 128, 64)))


def build_nc():
    global S_
    S_ = Sched()
    nc = bass.Bass("TRN2", target_bir_lowering=False)

    def din(name, shape, dt=F32):
        return nc.dram_tensor(name, list(shape), dt, kind="ExternalInput").ap()

    x_full = din("x_full", [S, D])
    x_own = din("x_own", [NOB * 128, D])
    wqkv_T = din("wqkv_T", [D, 3072])
    wqkv_R = din("wqkv_R", [3072, D])
    wo_T = din("wo_T", [D, D])
    wo_R = din("wo_R", [D, D])
    ffn_T = din("ffn_T", [D, D])
    ffn_R = din("ffn_R", [D, D])
    gains_h = din("gains", [1, H])
    idnb_h = din("idnb", [128, 128], BF16)
    idnf_h = din("idnf", [128, 128], F32)
    ones_r = din("ones_r", [1, 128])
    ones_c = din("ones_c", [128, 1])
    cosT_h = din("cosT_h", [NB, 128, 64])
    sinT_h = din("sinT_h", [NB, 128, 64])
    cosQ_h = din("cosQ_h", [NOB, 128, 64])
    sinQ_h = din("sinQ_h", [NOB, 128, 64])
    masks_h = din("masks", [32, 128, 128], BF16)

    out_h = nc.dram_tensor("out_shard", [NOB * 128, D], F32, kind="ExternalOutput").ap()
    sc_qt = nc.dram_tensor("sc_qt", [H, 128, NOB * 128], BF16, kind="Internal").ap()
    sc_g = nc.dram_tensor("sc_g", [NOB, 128, D], BF16, kind="Internal").ap()
    sc_woq = nc.dram_tensor("sc_woq", [NIC, 128, D], BF16, kind="Internal").ap()
    sc_xq = nc.dram_tensor("sc_xq", [NOB, NIC, 128, 128], BF16, kind="Internal").ap()

    es = contextlib.ExitStack()

    AB = es.enter_context(nc.sbuf_tensor("AB", [128, 72832], BF16))
    AFR = es.enter_context(nc.sbuf_tensor("AFR", [128, 15360], F32))

    def vb(arena, c0, n, p0=0, p1=128):
        return arena[p0:p1, c0:c0 + n], Buf()

    ktf, ktf_b = vb(AB, 0, KVH * S)
    vaug, vaug_b = vb(AB, 16384, KVH * NB * 129)
    wkvq, wkvq_b = vb(AB, 32896, NIC * 1024)
    xqs0, xqs0_b = vb(AB, 32896, NIC * 128)
    xqs1, xqs1_b = vb(AB, 34944, NIC * 128)
    gbf, gbf_b = vb(AB, 32896, 4 * D)
    wqg, wqg_b = vb(AB, 41088, NIC * 512)
    xq, xq_b = vb(AB, 49280, D)
    xqt_t, xqt_t_b = vb(AB, 51328, NIC * 128)
    kbf, kbf_b = vb(AB, 53376, 512)
    qt_t, qt_t_b = vb(AB, 49280, H * 512)
    yq, yq_b = vb(AB, 57472, D)
    e0, e0_b = vb(AB, 59520, 512)
    e1, e1_b = vb(AB, 60032, 512)
    yqt, yqt_b = vb(AB, 60544, 4 * NIC * 128)
    masksb, masksb_b = vb(AB, 68736, 32 * 128)

    wld0, wld0_b = vb(AFR, 0, 2048)
    wld1, wld1_b = vb(AFR, 2048, 2048)
    wtt0, wtt0_b = vb(AFR, 4096, 512)
    wtt1, wtt1_b = vb(AFR, 4608, 512)
    xt0, xt0_b = vb(AFR, 0, 2048)
    xt1, xt1_b = vb(AFR, 2048, 2048)
    xn, xn_b = vb(AFR, 5120, 2048)
    tq, tq_b = vb(AFR, 7168, 2048)
    kwork, kwork_b = vb(AFR, 9216, 512)
    kwork2, kwork2_b = vb(AFR, 9728, 512)
    wskv_rep, wskv_rep_b = vb(AFR, 10240, 1024)
    cosT, cosT_b = vb(AFR, 11264, NB * 64)
    sinT, sinT_b = vb(AFR, 13312, NB * 64)
    cosQ, cosQ_b = vb(AFR, 11264, NOB * 64)
    sinQ, sinQ_b = vb(AFR, 12288, NOB * 64)
    wsqd, wsqd_b = vb(AFR, 13312, 512)
    y_tm, y_tm_b = vb(AFR, 0, 4 * D)
    wso_rep, wso_rep_b = vb(AFR, 9216, D)
    sdqt, sdqt_b = vb(AFR, 11264, 512)
    xrt, xrt_b = vb(AFR, 11776, 512)
    outt, outt_b = vb(AFR, 12288, 512)
    gft, gft_b = vb(AFR, 12800, 512)
    tqT, tqT_b = vb(AFR, 13312, 2048)
    rowstg1, rowstg1_b = vb(AFR, 5120, 512, 0, 1)
    rowstgi1, rowstgi1_b = vb(AFR, 5632, 512, 0, 1)
    rowstg3, rowstg3_b = vb(AFR, 10240, 512, 0, 1)
    rowstgi3, rowstgi3_b = vb(AFR, 10752, 512, 0, 1)

    def sbf(name, p, f, dt=F32):
        return es.enter_context(nc.sbuf_tensor(name, [p, f], dt)), Buf()

    adeq, adeq_b = sbf("adeq", 128, NB)
    adeqo, adeqo_b = sbf("adeqo", 128, NOB)
    afo, afo_b = sbf("afo", 128, NOB)
    kk, kk_b = sbf("kk", 128, 1)
    ce, ce_b = sbf("ce", 128, 1)
    idn, idn_b = sbf("idn", 128, 128, BF16)
    idnf, idnf_b = sbf("idnf_s", 128, 128)
    onr, onr_b = sbf("onr", 1, 128)
    onc, onc_b = sbf("onc", 128, 1)
    mneg, mneg_b = sbf("mneg", 128, H)
    gn, gn_b = sbf("gn", 1, H)
    gnb, gnb_b = sbf("gnb", 128, H)
    wscol, wscol_b = sbf("wscol", 128, 24)
    colA, colA_b = sbf("colA", 128, 1)
    colB, colB_b = sbf("colB", 128, 1)
    colC, colC_b = sbf("colC", 128, 1)
    colS, colS_b = sbf("colS", 128, 4)
    adqy, adqy_b = sbf("adqy", 128, 4)
    sins, sins_b = sbf("sins", 128, 128)
    ang, ang_b = sbf("ang", 128, 64)
    ang2, ang2_b = sbf("ang2", 128, 64)
    angk, angk_b = sbf("angk", 128, 64)
    angt, angt_b = sbf("angt", 128, 64)
    tmpr, tmpr_b = sbf("tmpr", 128, 64)
    tmp2, tmp2_b = sbf("tmp2", 1, 16)
    tmp11, tmp11_b = sbf("tmp11", 1, 1)
    zcol, zcol_b = sbf("zcol", 128, 4 * H)
    zinv, zinv_b = sbf("zinv", 128, 4 * H)
    r127, r127_b = sbf("r127", 128, 1)
    rs128, rs128_b = sbf("rs128", 128, 1)
    wsf_c, wsf_c_b = sbf("wsf_c", 128, 1)
    wsfd, wsfd_b = sbf("wsfd", 128, 1)

    ps, ps_b = [], []
    for i in range(8):
        ps.append(es.enter_context(nc.psum_tensor("ps%d" % i, [128, 512], F32)))
        ps_b.append(Buf())

    with es:
        dma("sync", lambda e: e.dma_start(out=idn[:], in_=idnb_h[:]), w=[idn_b])
        dma("sync", lambda e: e.dma_start(out=idnf[:], in_=idnf_h[:]), w=[idnf_b])
        dma("sync", lambda e: e.dma_start(out=onr[:], in_=ones_r[:]), w=[onr_b])
        dma("sync", lambda e: e.dma_start(out=onc[:], in_=ones_c[:]), w=[onc_b])
        dma("sync", lambda e: e.dma_start(out=gn[:], in_=gains_h[:]), w=[gn_b])
        dma("sync", lambda e: e.dma_start(
            out=cosT.rearrange("p (nb d) -> p nb d", nb=NB), in_=cosT_h.rearrange("nb p d -> p nb d")), w=[cosT_b])
        dma("sync", lambda e: e.dma_start(
            out=sinT.rearrange("p (nb d) -> p nb d", nb=NB), in_=sinT_h.rearrange("nb p d -> p nb d")), w=[sinT_b])

        op("vector", lambda v: v.memset(kk[:], 1.0), w=[kk_b])
        op("vector", lambda v: v.tensor_scalar(kk[:], kk[:], 2.0 ** -25, None, ALU.add), r=[kk_b], w=[kk_b])
        op("vector", lambda v: v.tensor_scalar(kk[:], kk[:], -1.0, None, ALU.add), r=[kk_b], w=[kk_b])
        op("vector", lambda v: v.tensor_scalar(kk[:], kk[:], 2.0 ** 25, None, ALU.mult), r=[kk_b], w=[kk_b])
        op("vector", lambda v: v.tensor_scalar(ce[:], kk[:], CDELTA, C23, ALU.mult, ALU.add), r=[kk_b], w=[ce_b])

        op("vector", lambda v: v.memset(r127[:], 127.0), w=[r127_b])
        op("vector", lambda v: v.reciprocal(r127[:], r127[:]), r=[r127_b], w=[r127_b])
        op("vector", lambda v: v.memset(rs128[:], 128.0), w=[rs128_b])
        op("scalar", lambda c: c.sqrt(rs128[:], rs128[:]), r=[rs128_b], w=[rs128_b])
        op("vector", lambda v: v.reciprocal(rs128[:], rs128[:]), r=[rs128_b], w=[rs128_b])


        op("scalar", lambda c: c.activation(tmp2[:], gn[:], AF.Abs), r=[gn_b], w=[tmp2_b])
        op("tensor", lambda t: t.matmul(ps[6][:, 0:16], onr[:], tmp2[:], start=True, stop=True), r=[onr_b, tmp2_b], w=[ps_b[6]])
        op("vector", lambda v: v.tensor_scalar(mneg[:], ps[6][:, 0:16], -11.35, -2.0, ALU.mult, ALU.add), r=[ps_b[6]], w=[mneg_b])
        op("tensor", lambda t: t.matmul(ps[6][:, 0:16], onr[:], gn[:], start=True, stop=True), r=[onr_b, gn_b], w=[ps_b[6]])
        op("vector", lambda v: v.tensor_scalar(gnb[:], ps[6][:, 0:16], rs128[:], None, ALU.mult), r=[ps_b[6], rs128_b], w=[gnb_b])

        op("vector", lambda v: v.memset(vaug[:], 1.0), w=[vaug_b])

        def rowscale_pass(src_h, nchunks):
            for c in range(nchunks):
                wb, wbb = (wld0, wld0_b) if c % 2 == 0 else (wld1, wld1_b)
                dma("sync", lambda e, c=c, wb=wb: e.dma_start(
                    out=wb[:], in_=src_h[c * 128:(c + 1) * 128, :]), w=[wbb])
                op("vector", lambda v, wb=wb: v.reduce_max(colA[:], wb[:], axis=AX.X, apply_absolute_value=True), r=[wbb], w=[colA_b])
                op("vector", lambda v, c=c: v.tensor_scalar(wscol[:, c:c + 1], colA[:], r127[:], 1e-5, ALU.mult, ALU.max), r=[colA_b, r127_b], w=[wscol_b])

        def build_row(chunks, rstg, rstg_b, rstgi, rstgi_b):
            for i, c in enumerate(chunks):
                op("tensor", lambda t, c=c: t.matmul(ps[7][0:1, 0:128], wscol[:, c:c + 1], idnf[:], start=True, stop=True), r=[wscol_b, idnf_b], w=[ps_b[7]])
                op("vector", lambda v, i=i, rstg=rstg: v.tensor_copy(rstg[:, i * 128:(i + 1) * 128], ps[7][0:1, 0:128]), r=[ps_b[7]], w=[rstg_b])
            op("vector", lambda v, rstg=rstg, rstgi=rstgi: v.reciprocal(rstgi[:], rstg[:]), r=[rstg_b], w=[rstgi_b])

        def rep_to_ps(bank, row_t, row_bb):
            op("tensor", lambda t: t.matmul(ps[bank][:], onr[:], row_t[:], start=True, stop=True),
               r=[onr_b, row_bb], w=[ps_b[bank]])

        def quant_w_tile(srcT_h, col0, ic, dst_cs, repbank, buf, bufb, clip1=False, scalcol=None, scalcol_b=None):
            dma("sync", lambda e: e.dma_start(out=buf[:], in_=srcT_h[ic * 128:(ic + 1) * 128, col0:col0 + 512]), w=[bufb])
            if scalcol is not None:
                op("vector", lambda v: v.tensor_scalar(buf[:], buf[:], scalcol[:], None, ALU.mult), r=[bufb, scalcol_b], w=[bufb])
            else:
                op("vector", lambda v: v.tensor_tensor(buf[:], buf[:], ps[repbank][:], ALU.mult), r=[bufb, ps_b[repbank]], w=[bufb])
            op("vector", lambda v: v.tensor_scalar(buf[:], buf[:], ce[:], None, ALU.add), r=[bufb, ce_b], w=[bufb])
            if clip1:
                op("vector", lambda v: v.tensor_scalar(buf[:], buf[:], ce[:], None, ALU.subtract), r=[bufb, ce_b], w=[bufb])
                op("vector", lambda v: v.tensor_scalar(wqg[:, dst_cs], buf[:], 1.0, -1.0, ALU.min, ALU.max), r=[bufb], w=[wqg_b])
            else:
                op("vector", lambda v: v.tensor_scalar(wqg[:, dst_cs], buf[:], ce[:], None, ALU.subtract), r=[bufb, ce_b], w=[wqg_b])

        def ln_quant(src_ap, idx, adeq_t, adeq_bb, af_t, af_bb, xin, xin_b):
            dma("sync", lambda e: e.dma_start(out=xin[:], in_=src_ap), w=[xin_b])
            op("vector", lambda v: v.reduce_sum(colA[:], xin[:], axis=AX.X), r=[xin_b], w=[colA_b])
            op("vector", lambda v: v.tensor_scalar(colA[:], colA[:], 1.0 / D, None, ALU.mult), r=[colA_b], w=[colA_b])
            op("vector", lambda v: v.tensor_scalar(xn[:], xin[:], colA[:], None, ALU.subtract), r=[xin_b, colA_b], w=[xn_b])
            op("scalar", lambda c: c.activation(tq[:], xn[:], AF.Square), r=[xn_b], w=[tq_b])
            op("vector", lambda v: v.reduce_sum(colA[:], tq[:], axis=AX.X), r=[tq_b], w=[colA_b])
            op("vector", lambda v: v.tensor_scalar(colA[:], colA[:], 1.0 / D, EPS_LN, ALU.mult, ALU.add), r=[colA_b], w=[colA_b])
            op("scalar", lambda c: c.sqrt(colA[:], colA[:]), r=[colA_b], w=[colA_b])
            op("vector", lambda v: v.reciprocal(colA[:], colA[:]), r=[colA_b], w=[colA_b])
            op("vector", lambda v: v.tensor_scalar(xn[:], xn[:], colA[:], None, ALU.mult), r=[xn_b, colA_b], w=[xn_b])
            op("vector", lambda v: v.reduce_max(colA[:], xn[:], axis=AX.X, apply_absolute_value=True), r=[xn_b], w=[colA_b])
            if af_t is not None:
                op("vector", lambda v: v.tensor_scalar(af_t[:, idx:idx + 1], colA[:], 1e-8, None, ALU.max), r=[colA_b], w=[af_bb])
            op("vector", lambda v: v.tensor_scalar(colB[:], colA[:], r127[:], 1e-5, ALU.mult, ALU.max), r=[colA_b, r127_b], w=[colB_b])
            op("vector", lambda v: v.tensor_copy(adeq_t[:, idx:idx + 1], colB[:]), r=[colB_b], w=[adeq_bb])
            op("vector", lambda v: v.reciprocal(colB[:], colB[:]), r=[colB_b], w=[colB_b])
            op("vector", lambda v: v.tensor_scalar(tq[:], xn[:], colB[:], None, ALU.mult), r=[xn_b, colB_b], w=[tq_b])
            op("vector", lambda v: v.tensor_scalar(tq[:], tq[:], ce[:], None, ALU.add), r=[tq_b, ce_b], w=[tq_b])
            op("vector", lambda v: v.tensor_scalar(xq[:], tq[:], ce[:], None, ALU.subtract), r=[tq_b, ce_b], w=[xq_b])

        def transpose16(src_t, src_bb, dst_ap_fn, dst_bb, bank0=0, bank1=1):
            for g4 in range(4):
                bank = bank0 if g4 % 2 == 0 else bank1
                opg("tensor", [
                    (lambda t, c=g4 * 4 + i, i=i, bank=bank: t.matmul(
                        ps[bank][:, i * 128:(i + 1) * 128], src_t[:, c * 128:(c + 1) * 128], idn[:],
                        start=(i == 0), stop=(i == 3))) for i in range(4)],
                    r=[src_bb, idn_b], w=[ps_b[bank]])
                op("vector", lambda v, g4=g4, bank=bank: v.tensor_copy(
                    dst_ap_fn(g4), ps[bank][:].rearrange("p (c t) -> p c t", c=4)), r=[ps_b[bank]], w=[dst_bb])

        def rope_tm(work_t, work_bb, nheads, cos_t, cos_bb, sin_t, sin_bb, blk, w2, w2b):
            cs = slice(blk * 64, blk * 64 + 64)
            nb_ = nheads * 128
            op("vector", lambda v: v.tensor_copy(sins[:, 0:64], sin_t[:, cs]), r=[sin_bb], w=[sins_b])
            op("vector", lambda v: v.tensor_scalar(sins[:, 64:128], sin_t[:, cs], -1.0, None, ALU.mult), r=[sin_bb], w=[sins_b])
            wv = lambda t_, a, b: t_[:, 0:nb_].rearrange("p (h x d) -> p h x d", h=nheads, x=2)[:, :, a:b, :]
            op("vector", lambda v: v.tensor_copy(wv(w2, 0, 1), wv(work_t, 1, 2)), r=[work_bb], w=[w2b])
            op("vector", lambda v: v.tensor_copy(wv(w2, 1, 2), wv(work_t, 0, 1)), r=[work_bb], w=[w2b])
            cosv = cos_t[:, cs].unsqueeze(1).unsqueeze(2).broadcast_to((128, nheads, 2, 64))
            sinv = sins[:, 0:128].unsqueeze(1).broadcast_to((128, nheads, 128))
            op("vector", lambda v: v.tensor_tensor(
                work_t[:, 0:nb_].rearrange("p (h x d) -> p h x d", h=nheads, x=2),
                work_t[:, 0:nb_].rearrange("p (h x d) -> p h x d", h=nheads, x=2), cosv, ALU.mult),
                r=[work_bb, cos_bb], w=[work_bb])
            op("vector", lambda v: v.tensor_tensor(
                w2[:, 0:nb_].rearrange("p (h d) -> p h d", h=nheads),
                w2[:, 0:nb_].rearrange("p (h d) -> p h d", h=nheads), sinv, ALU.mult),
                r=[w2b, sins_b], w=[w2b])
            op("vector", lambda v: v.tensor_tensor(work_t[:, 0:nb_], work_t[:, 0:nb_], w2[:, 0:nb_], ALU.add),
               r=[work_bb, w2b], w=[work_bb])

        rowscale_pass(wqkv_R, 24)
        for g in range(2):
            build_row(range(16 + g * 4, 16 + (g + 1) * 4), rowstg1, rowstg1_b, rowstgi1, rowstgi1_b)
            rep_to_ps(5, rowstg1, rowstg1_b)
            op("vector", lambda v, g=g: v.tensor_copy(wskv_rep[:, g * 512:(g + 1) * 512], ps[5][:]), r=[ps_b[5]], w=[wskv_rep_b])
        for og in range(2):
            col0 = 2048 + og * 512
            build_row(range(16 + og * 4, 16 + (og + 1) * 4), rowstg1, rowstg1_b, rowstgi1, rowstgi1_b)
            rep_to_ps(5, rowstgi1, rowstgi1_b)
            for ic in range(NIC):
                buf, bufb = (wtt0, wtt0_b) if ic % 2 == 0 else (wtt1, wtt1_b)
                dma("sync", lambda e, ic=ic, col0=col0, buf=buf: e.dma_start(
                    out=buf[:], in_=wqkv_T[ic * 128:(ic + 1) * 128, col0:col0 + 512]), w=[bufb])
                op("vector", lambda v, buf=buf: v.tensor_tensor(buf[:], buf[:], ps[5][:], ALU.mult), r=[bufb, ps_b[5]], w=[bufb])
                op("vector", lambda v, buf=buf: v.tensor_scalar(buf[:], buf[:], ce[:], None, ALU.add), r=[bufb, ce_b], w=[bufb])
                op("vector", lambda v, ic=ic, og=og, buf=buf: v.tensor_scalar(
                    wkvq[:, ic * 1024 + og * 512: ic * 1024 + (og + 1) * 512], buf[:], ce[:], None, ALU.subtract),
                    r=[bufb, ce_b], w=[wkvq_b])
        S_.barrier()

        for blk in range(NB):
            xin, xin_b = (xt0, xt0_b) if blk % 2 == 0 else (xt1, xt1_b)
            ln_quant(x_full[blk * 128:(blk + 1) * 128, :], blk, adeq, adeq_b, None, None, xin, xin_b)
            transpose16(xq, xq_b, lambda g4: xqt_t[:, g4 * 512:(g4 + 1) * 512].rearrange("p (c t) -> p c t", c=4), xqt_t_b, 0, 1)
            for og in range(2):
                bank = 2 + og
                opg("tensor", [
                    (lambda t, ic=ic, og=og, bank=bank: t.matmul(
                        ps[bank][:], xqt_t[:, ic * 128:(ic + 1) * 128],
                        wkvq[:, ic * 1024 + og * 512: ic * 1024 + (og + 1) * 512],
                        start=(ic == 0), stop=(ic == NIC - 1))) for ic in range(NIC)],
                    r=[xqt_t_b, wkvq_b], w=[ps_b[bank]])
                op("vector", lambda v, bank=bank, blk=blk: v.tensor_scalar(
                    kwork[:], ps[bank][:], adeq[:, blk:blk + 1], None, ALU.mult), r=[ps_b[bank], adeq_b], w=[kwork_b])
                op("vector", lambda v, og=og: v.tensor_tensor(
                    kwork[:], kwork[:], wskv_rep[:, og * 512:(og + 1) * 512], ALU.mult), r=[kwork_b, wskv_rep_b], w=[kwork_b])
                if og == 0:
                    op("scalar", lambda c: c.activation(kwork2[:], kwork[:], AF.Square), r=[kwork_b], w=[kwork2_b])
                    for kv in range(KVH):
                        op("vector", lambda v, kv=kv: v.reduce_sum(colS[:, kv:kv + 1], kwork2[:, kv * 128:(kv + 1) * 128], axis=AX.X), r=[kwork2_b], w=[colS_b])
                    op("vector", lambda v: v.tensor_scalar(colS[:], colS[:], 1.0 / 128.0, EPS_RMS, ALU.mult, ALU.add), r=[colS_b], w=[colS_b])
                    op("scalar", lambda c: c.sqrt(colS[:], colS[:]), r=[colS_b], w=[colS_b])
                    op("vector", lambda v: v.reciprocal(colS[:], colS[:]), r=[colS_b], w=[colS_b])
                    for kv in range(KVH):
                        op("vector", lambda v, kv=kv: v.tensor_scalar(
                            kwork[:, kv * 128:(kv + 1) * 128], kwork[:, kv * 128:(kv + 1) * 128],
                            colS[:, kv:kv + 1], None, ALU.mult), r=[kwork_b, colS_b], w=[kwork_b])
                    rope_tm(kwork, kwork_b, KVH, cosT, cosT_b, sinT, sinT_b, blk, kwork2, kwork2_b)
                    op("vector", lambda v: v.tensor_copy(kbf[:], kwork[:]), r=[kwork_b], w=[kbf_b])
                    opg("tensor", [
                        (lambda t, kv=kv: t.matmul(
                            ps[4][:, kv * 128:(kv + 1) * 128], kbf[:, kv * 128:(kv + 1) * 128], idn[:],
                            start=(kv == 0), stop=(kv == 3))) for kv in range(KVH)],
                        r=[kbf_b, idn_b], w=[ps_b[4]])
                    op("vector", lambda v, blk=blk: v.tensor_copy(
                        ktf.rearrange("p (kv s) -> p kv s", kv=KVH)[:, :, blk * 128:(blk + 1) * 128],
                        ps[4][:].rearrange("p (kv r) -> p kv r", kv=KVH)), r=[ps_b[4]], w=[ktf_b])
                else:
                    op("vector", lambda v, blk=blk: v.tensor_copy(
                        vaug.rearrange("p (kv nb c) -> p kv nb c", kv=KVH, nb=NB)[:, :, blk:blk + 1, 0:128],
                        kwork[:].rearrange("p (kv o d) -> p kv o d", kv=KVH, o=1)), r=[kwork_b], w=[vaug_b])
        S_.barrier()

        dma("sync", lambda e: e.dma_start(
            out=cosQ.rearrange("p (nb d) -> p nb d", nb=NOB), in_=cosQ_h.rearrange("nb p d -> p nb d")), w=[cosQ_b])
        dma("sync", lambda e: e.dma_start(
            out=sinQ.rearrange("p (nb d) -> p nb d", nb=NOB), in_=sinQ_h.rearrange("nb p d -> p nb d")), w=[sinQ_b])
        for ot in range(NOB):
            xin, xin_b = (xt0, xt0_b) if ot % 2 == 0 else (xt1, xt1_b)
            ln_quant(x_own[ot * 128:(ot + 1) * 128, :], ot, adeqo, adeqo_b, afo, afo_b, xin, xin_b)
            transpose16(xq, xq_b, lambda g4: xqt_t[:, g4 * 512:(g4 + 1) * 512].rearrange("p (c t) -> p c t", c=4), xqt_t_b, 0, 1)
            dma("sync", lambda e, ot=ot: e.dma_start(
                out=sc_xq[ot].rearrange("ic p t -> p ic t"),
                in_=xqt_t[:].rearrange("p (ic t) -> p ic t", ic=NIC)), r=[xqt_t_b])
        for og in range(4):
            build_row(range(og * 4, (og + 1) * 4), rowstg3, rowstg3_b, rowstgi3, rowstgi3_b)
            rep_to_ps(5, rowstg3, rowstg3_b)
            op("vector", lambda v: v.tensor_copy(wsqd[:], ps[5][:]), r=[ps_b[5]], w=[wsqd_b])
            rep_to_ps(5, rowstgi3, rowstgi3_b)
            for ic in range(NIC):
                buf, bufb = (wtt0, wtt0_b) if ic % 2 == 0 else (wtt1, wtt1_b)
                quant_w_tile(wqkv_T, og * 512, ic, slice(ic * 512, (ic + 1) * 512), 5, buf, bufb)
            def q_chain(ot):
                bank = 2 + ot % 2
                xs, xs_b = (xqs0, xqs0_b) if ot % 2 == 0 else (xqs1, xqs1_b)
                dma("sync", lambda e, ot=ot, xs=xs: e.dma_start(
                    out=xs[:].rearrange("p (ic t) -> p ic t", ic=NIC),
                    in_=sc_xq[ot].rearrange("ic p t -> p ic t")), w=[xs_b])
                opg("tensor", [
                    (lambda t, ic=ic, bank=bank, xs=xs: t.matmul(
                        ps[bank][:], xs[:, ic * 128:(ic + 1) * 128],
                        wqg[:, ic * 512:(ic + 1) * 512],
                        start=(ic == 0), stop=(ic == NIC - 1))) for ic in range(NIC)],
                    r=[xs_b, wqg_b], w=[ps_b[bank]])

            q_chain(0)
            for ot in range(NOB):
                bank = 2 + ot % 2
                if ot + 1 < NOB:
                    q_chain(ot + 1)
                op("vector", lambda v, bank=bank, ot=ot: v.tensor_scalar(
                    kwork[:], ps[bank][:], adeqo[:, ot:ot + 1], None, ALU.mult), r=[ps_b[bank], adeqo_b], w=[kwork_b])
                op("vector", lambda v: v.tensor_tensor(kwork[:], kwork[:], wsqd[:], ALU.mult), r=[kwork_b, wsqd_b], w=[kwork_b])
                op("scalar", lambda c: c.activation(kwork2[:], kwork[:], AF.Square), r=[kwork_b], w=[kwork2_b])
                for i in range(4):
                    op("vector", lambda v, i=i: v.reduce_sum(colS[:, i:i + 1], kwork2[:, i * 128:(i + 1) * 128], axis=AX.X), r=[kwork2_b], w=[colS_b])
                op("vector", lambda v: v.tensor_scalar(colS[:], colS[:], 1.0 / 128.0, EPS_RMS, ALU.mult, ALU.add), r=[colS_b], w=[colS_b])
                op("scalar", lambda c: c.sqrt(colS[:], colS[:]), r=[colS_b], w=[colS_b])
                op("vector", lambda v: v.reciprocal(colS[:], colS[:]), r=[colS_b], w=[colS_b])
                op("vector", lambda v, og=og: v.tensor_tensor(colS[:], colS[:], gnb[:, og * 4:(og + 1) * 4], ALU.mult), r=[colS_b, gnb_b], w=[colS_b])
                for i in range(4):
                    op("vector", lambda v, i=i: v.tensor_scalar(
                        kwork[:, i * 128:(i + 1) * 128], kwork[:, i * 128:(i + 1) * 128],
                        colS[:, i:i + 1], None, ALU.mult), r=[kwork_b, colS_b], w=[kwork_b])
                rope_tm(kwork, kwork_b, 4, cosQ, cosQ_b, sinQ, sinQ_b, ot, kwork2, kwork2_b)
                op("vector", lambda v: v.tensor_copy(kbf[:], kwork[:]), r=[kwork_b], w=[kbf_b])
                opg("tensor", [
                    (lambda t, i=i: t.matmul(
                        ps[4][:, i * 128:(i + 1) * 128], kbf[:, i * 128:(i + 1) * 128], idn[:],
                        start=(i == 0), stop=(i == 3))) for i in range(4)],
                    r=[kbf_b, idn_b], w=[ps_b[4]])
                op("vector", lambda v: v.tensor_copy(kbf[:], ps[4][:]), r=[ps_b[4]], w=[kbf_b])
                dma("sync", lambda e, og=og, ot=ot: e.dma_start(
                    out=sc_qt[og * 4:(og + 1) * 4, :, ot * 128:(ot + 1) * 128].rearrange("h p t -> p h t"),
                    in_=kbf[:].rearrange("p (h t) -> p h t", h=4)), r=[kbf_b])
        S_.barrier()

        op("vector", lambda v: v.memset(colB[:], 0.0), w=[colB_b])
        for c in range(NIC):
            wb, wbb = (wld0, wld0_b) if c % 2 == 0 else (wld1, wld1_b)
            dma("sync", lambda e, c=c, wb=wb: e.dma_start(out=wb[:], in_=ffn_R[c * 128:(c + 1) * 128, :]), w=[wbb])
            op("vector", lambda v, wb=wb: v.reduce_sum(colA[:], wb[:], axis=AX.X, apply_absolute_value=True), r=[wbb], w=[colA_b])
            op("vector", lambda v: v.tensor_tensor(colB[:], colB[:], colA[:], ALU.add), r=[colB_b, colA_b], w=[colB_b])
        op("tensor", lambda t: t.matmul(ps[7][0:1, 0:1], colB[:], onc[:], start=True, stop=True), r=[colB_b, onc_b], w=[ps_b[7]])
        op("vector", lambda v: v.tensor_copy(tmp11[:], ps[7][0:1, 0:1]), r=[ps_b[7]], w=[tmp11_b])
        op("vector", lambda v: v.tensor_scalar(tmp11[:], tmp11[:], 1.0 / (D * D), 1e-6, ALU.mult, ALU.max), r=[tmp11_b], w=[tmp11_b])
        op("tensor", lambda t: t.matmul(ps[7][:, 0:1], onr[:], tmp11[:], start=True, stop=True), r=[onr_b, tmp11_b], w=[ps_b[7]])
        op("vector", lambda v: v.tensor_scalar(wsfd[:], ps[7][:, 0:1], r127[:], None, ALU.mult), r=[ps_b[7], r127_b], w=[wsfd_b])
        op("vector", lambda v: v.reciprocal(wsf_c[:], ps[7][:, 0:1]), r=[ps_b[7]], w=[wsf_c_b])
        for og in range(4):
            for ic in range(NIC):
                buf, bufb = (wtt0, wtt0_b) if ic % 2 == 0 else (wtt1, wtt1_b)
                quant_w_tile(ffn_T, og * 512, ic, slice(ic * 512, (ic + 1) * 512), None, buf, bufb,
                             clip1=True, scalcol=wsf_c, scalcol_b=wsf_c_b)
            def f_chain(ot):
                bank = 2 + ot % 2
                xs, xs_b = (xqs0, xqs0_b) if ot % 2 == 0 else (xqs1, xqs1_b)
                dma("sync", lambda e, ot=ot, xs=xs: e.dma_start(
                    out=xs[:].rearrange("p (ic t) -> p ic t", ic=NIC),
                    in_=sc_xq[ot].rearrange("ic p t -> p ic t")), w=[xs_b])
                opg("tensor", [
                    (lambda t, ic=ic, bank=bank, xs=xs: t.matmul(
                        ps[bank][:], xs[:, ic * 128:(ic + 1) * 128],
                        wqg[:, ic * 512:(ic + 1) * 512],
                        start=(ic == 0), stop=(ic == NIC - 1))) for ic in range(NIC)],
                    r=[xs_b, wqg_b], w=[ps_b[bank]])

            f_chain(0)
            for ot in range(NOB):
                bank = 2 + ot % 2
                if ot + 1 < NOB:
                    f_chain(ot + 1)
                op("vector", lambda v, ot=ot: v.tensor_tensor(colC[:], afo[:, ot:ot + 1], wsfd[:], ALU.mult), r=[afo_b, wsfd_b], w=[colC_b])
                op("scalar", lambda c, bank=bank: c.activation(kbf[:], ps[bank][:], AF.Silu, scale=colC[:]), r=[ps_b[bank], colC_b], w=[kbf_b])
                dma("sync", lambda e, ot=ot, og=og: e.dma_start(
                    out=sc_g[ot, :, og * 512:(og + 1) * 512], in_=kbf[:]), r=[kbf_b])
        S_.barrier()

        rowscale_pass(wo_R, NIC)
        for og in range(4):
            build_row(range(og * 4, (og + 1) * 4), rowstg1, rowstg1_b, rowstgi1, rowstgi1_b)
            rep_to_ps(5, rowstgi1, rowstgi1_b)
            for ic in range(NIC):
                buf, bufb = (wtt0, wtt0_b) if ic % 2 == 0 else (wtt1, wtt1_b)
                quant_w_tile(wo_T, og * 512, ic, slice(ic * 512, (ic + 1) * 512), 5, buf, bufb)
            dma("sync", lambda e, og=og: e.dma_start(
                out=sc_woq[:, :, og * 512:(og + 1) * 512].rearrange("ic p d -> p ic d"),
                in_=wqg[:].rearrange("p (ic d) -> p ic d", ic=NIC)), r=[wqg_b])
            rep_to_ps(5, rowstg1, rowstg1_b)
            op("vector", lambda v, og=og: v.tensor_copy(wso_rep[:, og * 512:(og + 1) * 512], ps[5][:]), r=[ps_b[5]], w=[wso_rep_b])
        S_.barrier()

        dma("sync", lambda e: e.dma_start(
            out=masksb.rearrange("p (m q) -> p m q", m=32), in_=masks_h.rearrange("m p q -> p m q")), w=[masksb_b])
        for qt in range(4):
            KMAX = 8 * qt + 8
            dma("sync", lambda e, qt=qt: e.dma_start(
                out=qt_t[:].rearrange("p (h t) -> p h t", h=H),
                in_=sc_qt[:, :, qt * 512:(qt + 1) * 512].rearrange("h p t -> p h t")), w=[qt_t_b])
            for h in range(H):
                kv = h // 4

                def st_mm(kc, kv=kv, h=h):
                    stb = 4 + (kc % 2)
                    op("tensor", lambda t, kc=kc, kv=kv, h=h, stb=stb: t.matmul(
                        ps[stb][:], ktf[:, kv * S + kc * 128: kv * S + (kc + 1) * 128],
                        qt_t[:, h * 512:(h + 1) * 512], start=True, stop=True),
                        r=[ktf_b, qt_t_b], w=[ps_b[stb]])

                st_mm(0)
                for kc in range(KMAX):
                    stb = 4 + (kc % 2)
                    if kc + 1 < KMAX:
                        st_mm(kc + 1)
                    e_t, e_b = (e0, e0_b) if kc % 2 == 0 else (e1, e1_b)
                    op("scalar", lambda c, h=h, stb=stb, e_t=e_t: c.activation(
                        e_t[:], ps[stb][:], AF.Exp, bias=mneg[:, h:h + 1]), r=[ps_b[stb], mneg_b], w=[e_b])
                    mmfns = []
                    wbanks = []
                    for j in range(4):
                        dd = kc - (8 * qt + 2 * j)
                        if dd > 1:
                            continue
                        if dd >= 0:
                            m_idx = (qt * 4 + j) * 2 + dd
                            op("vector", lambda v, j=j, m_idx=m_idx, e_t=e_t: v.tensor_tensor(
                                e_t[:, j * 128:(j + 1) * 128], e_t[:, j * 128:(j + 1) * 128],
                                masksb[:, m_idx * 128:(m_idx + 1) * 128], ALU.mult), r=[masksb_b, e_b], w=[e_b])
                        last_kc = min(8 * qt + 2 * j + 1, KMAX - 1)
                        mmfns.append(lambda t, j=j, kc=kc, kv=kv, e_t=e_t, last_kc=last_kc: t.matmul(
                            ps[j][:, 0:129], e_t[:, j * 128:(j + 1) * 128],
                            vaug[:, (kv * NB + kc) * 129: (kv * NB + kc + 1) * 129],
                            start=(kc == 0), stop=(kc == last_kc)))
                        wbanks.append(ps_b[j])
                    opg("tensor", mmfns, r=[e_b, vaug_b], w=wbanks)
                for j in range(4):
                    op("vector", lambda v, j=j, h=h: v.tensor_copy(
                        y_tm[:, j * D + h * 128: j * D + (h + 1) * 128], ps[j][:, 0:128]), r=[ps_b[j]], w=[y_tm_b])
                    op("vector", lambda v, j=j, h=h: v.tensor_copy(
                        zcol[:, j * H + h: j * H + h + 1], ps[j][:, 128:129]), r=[ps_b[j]], w=[zcol_b])
            for j in range(4):
                op("vector", lambda v, j=j: v.reciprocal(zinv[:, j * H:(j + 1) * H], zcol[:, j * H:(j + 1) * H]), r=[zcol_b], w=[zinv_b])
                for h in range(H):
                    op("vector", lambda v, j=j, h=h: v.tensor_scalar(
                        y_tm[:, j * D + h * 128: j * D + (h + 1) * 128],
                        y_tm[:, j * D + h * 128: j * D + (h + 1) * 128],
                        zinv[:, j * H + h: j * H + h + 1], None, ALU.mult), r=[zinv_b, y_tm_b], w=[y_tm_b])
                op("vector", lambda v, j=j: v.reduce_max(colA[:], y_tm[:, j * D:(j + 1) * D], axis=AX.X, apply_absolute_value=True), r=[y_tm_b], w=[colA_b])
                op("vector", lambda v: v.tensor_scalar(colB[:], colA[:], r127[:], 1e-5, ALU.mult, ALU.max), r=[colA_b, r127_b], w=[colB_b])
                op("vector", lambda v, j=j: v.tensor_copy(adqy[:, j:j + 1], colB[:]), r=[colB_b], w=[adqy_b])
                op("vector", lambda v: v.reciprocal(colB[:], colB[:]), r=[colB_b], w=[colB_b])
                op("vector", lambda v, j=j: v.tensor_scalar(tqT[:], y_tm[:, j * D:(j + 1) * D], colB[:], None, ALU.mult), r=[y_tm_b, colB_b], w=[tqT_b])
                op("vector", lambda v: v.tensor_scalar(tqT[:], tqT[:], ce[:], None, ALU.add), r=[tqT_b, ce_b], w=[tqT_b])
                op("vector", lambda v: v.tensor_scalar(yq[:], tqT[:], ce[:], None, ALU.subtract), r=[tqT_b, ce_b], w=[yq_b])
                transpose16(yq, yq_b, lambda g4, j=j: yqt[:, j * D + g4 * 512: j * D + (g4 + 1) * 512].rearrange("p (c t) -> p c t", c=4), yqt_b, 6, 7)
            for j in range(4):
                ot = qt * 4 + j
                dma("sync", lambda e, ot=ot, j=j: e.dma_start(
                    out=gbf[:, j * D:(j + 1) * D], in_=sc_g[ot, :, :]), w=[gbf_b])
            for og in range(4):
                dma("sync", lambda e, og=og: e.dma_start(
                    out=wqg[:].rearrange("p (ic d) -> p ic d", ic=NIC),
                    in_=sc_woq[:, :, og * 512:(og + 1) * 512].rearrange("ic p d -> p ic d")), w=[wqg_b])
                for j in range(4):
                    ot = qt * 4 + j
                    bank = 6 + (j % 2)
                    opg("tensor", [
                        (lambda t, j=j, ic=ic, bank=bank: t.matmul(
                            ps[bank][:], yqt[:, j * D + ic * 128: j * D + (ic + 1) * 128],
                            wqg[:, ic * 512:(ic + 1) * 512],
                            start=(ic == 0), stop=(ic == NIC - 1))) for ic in range(NIC)],
                        r=[yqt_b, wqg_b], w=[ps_b[bank]])
                    op("vector", lambda v, j=j, bank=bank: v.tensor_scalar(
                        sdqt[:], ps[bank][:], adqy[:, j:j + 1], None, ALU.mult), r=[ps_b[bank], adqy_b], w=[sdqt_b])
                    op("vector", lambda v, og=og: v.tensor_tensor(
                        sdqt[:], sdqt[:], wso_rep[:, og * 512:(og + 1) * 512], ALU.mult), r=[wso_rep_b, sdqt_b], w=[sdqt_b])
                    dma("sync", lambda e, ot=ot, og=og: e.dma_start(
                        out=xrt[:], in_=x_own[ot * 128:(ot + 1) * 128, og * 512:(og + 1) * 512]), w=[xrt_b])
                    op("vector", lambda v, j=j, og=og: v.tensor_copy(
                        gft[:], gbf[:, j * D + og * 512: j * D + (og + 1) * 512]), r=[gbf_b], w=[gft_b])
                    op("vector", lambda v: v.tensor_tensor(outt[:], sdqt[:], gft[:], ALU.mult), r=[sdqt_b, gft_b], w=[outt_b])
                    op("vector", lambda v: v.tensor_tensor(outt[:], outt[:], xrt[:], ALU.add), r=[outt_b, xrt_b], w=[outt_b])
                    dma("sync", lambda e, ot=ot, og=og: e.dma_start(
                        out=out_h[ot * 128:(ot + 1) * 128, og * 512:(og + 1) * 512], in_=outt[:]), r=[outt_b])

        S_.wait("sync", [(s, S_.cnt[s]) for s in ALL_DMA_SEMS if S_.cnt[s] > 0])

        sems = {}
        for name in list(ENG_SEM.values()) + ALL_DMA_SEMS:
            sems[name] = es.enter_context(nc.semaphore(name))
        block = es.enter_context(nc.Block())

        def mk(engname):
            items = S_.items[engname]
            osem = ENG_SEM[engname]

            def body(proxy):
                waited = {}
                for it in items:
                    if it[0] == "w":
                        _, s, v = it
                        if waited.get(s, -1) >= v:
                            continue
                        proxy.wait_ge(sems[s], v)
                        waited[s] = v
                    elif it[0] == "op":
                        it[1](proxy).then_inc(sems[osem], 1)
                    elif it[0] == "opg":
                        for f in it[1][:-1]:
                            f(proxy)
                        it[1][-1](proxy).then_inc(sems[osem], 1)
                    else:
                        it[1](proxy).then_inc(sems[it[2]], 16)
            return body

        block.tensor(mk("tensor"))
        block.vector(mk("vector"))
        block.scalar(mk("scalar"))
        block.sync(mk("sync"))
        block.gpsimd(mk("gpsimd"))

    return nc


def kernel(L0_attn_x, L0_attn_w_q, L0_attn_w_k, L0_attn_w_v, L0_attn_w_o,
           L0_attn_q_gain, L0_ffn_weight):
    x = np.ascontiguousarray(np.asarray(L0_attn_x, dtype=np.float32))
    wq = np.asarray(L0_attn_w_q, dtype=np.float32)
    wk = np.asarray(L0_attn_w_k, dtype=np.float32)
    wv = np.asarray(L0_attn_w_v, dtype=np.float32)
    wo = np.asarray(L0_attn_w_o, dtype=np.float32)
    gain = np.asarray(L0_attn_q_gain, dtype=np.float32)
    wf = np.asarray(L0_ffn_weight, dtype=np.float32)

    wqkv_R = np.ascontiguousarray(np.concatenate([wq, wk, wv], axis=0))
    wqkv_T = np.ascontiguousarray(wqkv_R.T)
    wo_T = np.ascontiguousarray(wo.T)
    ffn_T = np.ascontiguousarray(wf.T)

    idnb = np.eye(128, dtype=ml_dtypes.bfloat16)
    idnf = np.eye(128, dtype=np.float32)
    ones_r = np.ones((1, 128), dtype=np.float32)
    ones_c = np.ones((128, 1), dtype=np.float32)
    cosT_t, sinT_t = _rope_tables_host()

    in_maps = []
    block_maps = []
    for c in range(8):
        b, h = c // 2, c % 2
        obs = own_blocks(h)
        block_maps.append(obs)
        xb = np.ascontiguousarray(x[b])
        x_own = np.ascontiguousarray(xb.reshape(NB, 128, D)[obs].reshape(NOB * 128, D))
        tq_own = np.asarray([g * 128 + i for g in obs for i in range(128)], dtype=np.int64)
        cosQ_t = np.ascontiguousarray(cosT_t.reshape(S, 64)[tq_own].reshape(NOB, 128, 64))
        sinQ_t = np.ascontiguousarray(sinT_t.reshape(S, 64)[tq_own].reshape(NOB, 128, 64))
        masks = np.zeros((32, 128, 128), dtype=ml_dtypes.bfloat16)
        tri = np.triu(np.ones((128, 128), dtype=np.float32))
        for t in range(4):
            for j in range(4):
                g = obs[t * 4 + j]
                for slot in range(2):
                    kc = 8 * t + 2 * j + slot
                    if kc < g:
                        m = np.ones((128, 128), dtype=np.float32)
                    elif kc == g:
                        m = tri
                    else:
                        m = np.zeros((128, 128), dtype=np.float32)
                    masks[(t * 4 + j) * 2 + slot] = m.astype(ml_dtypes.bfloat16)
        in_maps.append({
            "x_full": xb, "x_own": x_own,
            "wqkv_T": wqkv_T, "wqkv_R": wqkv_R, "wo_T": wo_T, "wo_R": wo,
            "ffn_T": ffn_T, "ffn_R": wf,
            "gains": gain.reshape(1, H), "idnb": idnb, "idnf": idnf,
            "ones_r": ones_r, "ones_c": ones_c,
            "cosT_h": cosT_t, "sinT_h": sinT_t, "cosQ_h": cosQ_t, "sinQ_h": sinQ_t,
            "masks": masks,
        })

    nc = build_nc()
    res = run_bass_kernel_spmd(nc, in_maps, core_ids=list(range(8)))

    out = np.empty((B, S, D), dtype=np.float32)
    for c in range(8):
        b = c // 2
        shard = np.asarray(res.results[c]["out_shard"]).reshape(NOB, 128, D)
        for i, g in enumerate(block_maps[c]):
            out[b, g * 128:(g + 1) * 128, :] = shard[i]
    return out


# version 09
# speedup vs baseline: 1.2117x; 1.0126x over previous
import contextlib
import numpy as np
import ml_dtypes
import concourse.bass as bass
import concourse.mybir as mybir
from concourse.bass_utils import run_bass_kernel_spmd

F32 = mybir.dt.float32
BF16 = mybir.dt.bfloat16
AF = mybir.ActivationFunctionType
ALU = mybir.AluOpType
AX = mybir.AxisListType

B, S, D = 4, 4096, 2048
H, HD, KVH = 16, 128, 4
NB = S // 128
NOB = 16
NIC = D // 128
EPS_LN = 1e-5
EPS_RMS = 1.1920929e-7
C23 = 1.5 * 2.0 ** 23
CDELTA = 1.5 * 2.0 ** 52 - 1.5 * 2.0 ** 23
SQ128 = float(np.sqrt(128.0))
TWO_PI = 2.0 * np.pi
INV2PI = 1.0 / TWO_PI
RC1 = 6.28125
RC2 = float(np.float64(TWO_PI) - RC1)
RC2A = float(np.float32(RC2))
RC2B = float(np.float64(RC2) - np.float64(np.float32(RC2)))
EXSC = float(-np.log(10000.0) / 128.0)
EXSCA = float(np.float32(EXSC))
EXSCB = float(np.float64(EXSC) - np.float64(np.float32(EXSC)))
PI_CLAMP = 3.1415925
LN1E4 = float(np.log(10000.0))

ENGS = ("tensor", "vector", "scalar", "sync", "gpsimd")
DMA_SEMS = {"sync": ["Yd%d" % i for i in range(6)], "scalar": ["Cd0"], "gpsimd": ["Gd0"]}
ALL_DMA_SEMS = [s for v in DMA_SEMS.values() for s in v]
ENG_SEM = {"tensor": "T", "vector": "V", "scalar": "C", "sync": "Y", "gpsimd": "G"}


def own_blocks(h):
    return sorted([2 * i + h for i in range(8)] + [16 + 2 * i + (1 - h) for i in range(8)])


class Sched:
    def __init__(self):
        self.items = {e: [] for e in ENGS}
        self.cnt = {s: 0 for s in list(ENG_SEM.values()) + ALL_DMA_SEMS}
        self.rr = {e: 0 for e in DMA_SEMS}

    def wait(self, eng, deps):
        for d in deps:
            if d is not None:
                self.items[eng].append(("w", d[0], d[1]))

    def emit(self, eng, kind, fn, deps):
        st = self.items[eng]
        for d in deps:
            if d is not None:
                st.append(("w", d[0], d[1]))
        if kind == "dma":
            sl = DMA_SEMS[eng]
            sem = sl[self.rr[eng] % len(sl)]
            self.rr[eng] += 1
            if self.cnt[sem] > 0:
                st.append(("w", sem, self.cnt[sem]))
            st.append((kind, fn, sem))
            self.cnt[sem] += 16
            return (sem, self.cnt[sem])
        st.append((kind, fn))
        sem = ENG_SEM[eng]
        self.cnt[sem] += 1
        return (sem, self.cnt[sem])

    def barrier(self):
        snap = dict(self.cnt)
        for e in ENGS:
            for s, v in snap.items():
                if v > 0:
                    self.items[e].append(("w", s, v))


class Buf:
    __slots__ = ("w", "rs")

    def __init__(self):
        self.w = None
        self.rs = []


S_ = None


def _emit(eng, kind, fn, r, w):
    deps = [b.w for b in r]
    for b in w:
        deps.append(b.w)
        deps.extend(b.rs)
    tok = S_.emit(eng, kind, fn, deps)
    for b in r:
        b.rs.append(tok)
    for b in w:
        b.w = tok
        b.rs = []
    return tok


def op(eng, fn, r=(), w=()):
    return _emit(eng, "op", fn, r, w)


def opg(eng, fns, r=(), w=()):
    return _emit(eng, "opg", fns, r, w)


def dma(eng, fn, r=(), w=()):
    return _emit(eng, "dma", fn, r, w)


def _rope_tables_host():
    try:
        import jax
        import jax.numpy as jnp
        cpu = jax.devices("cpu")[0]
        with jax.default_device(cpu):
            inv = 1.0 / 10000.0 ** (jnp.arange(0, 128, 2, dtype=jnp.float32) / 128)
            t = jnp.arange(S, dtype=jnp.float32)
            fr = jnp.outer(t, inv)
            cosv = np.asarray(jnp.cos(fr), dtype=np.float32)
            sinv = np.asarray(jnp.sin(fr), dtype=np.float32)
    except Exception:
        inv = (1.0 / (10000.0 ** (np.arange(0, 128, 2, dtype=np.float32) / np.float32(128)).astype(np.float64))).astype(np.float32)
        fr = (np.arange(S, dtype=np.float32)[:, None] * inv[None, :]).astype(np.float32)
        cosv = np.cos(fr).astype(np.float32)
        sinv = np.sin(fr).astype(np.float32)
    return (np.ascontiguousarray(cosv.reshape(NB, 128, 64)),
            np.ascontiguousarray(sinv.reshape(NB, 128, 64)))


def build_nc():
    global S_
    S_ = Sched()
    nc = bass.Bass("TRN2", target_bir_lowering=False)

    def din(name, shape, dt=F32):
        return nc.dram_tensor(name, list(shape), dt, kind="ExternalInput").ap()

    x_full = din("x_full", [S, D])
    x_own = din("x_own", [NOB * 128, D])
    wqkv_T = din("wqkv_T", [D, 3072])
    wqkv_R = din("wqkv_R", [3072, D])
    wo_T = din("wo_T", [D, D])
    wo_R = din("wo_R", [D, D])
    ffn_T = din("ffn_T", [D, D])
    ffn_R = din("ffn_R", [D, D])
    gains_h = din("gains", [1, H])
    idnb_h = din("idnb", [128, 128], BF16)
    idnf_h = din("idnf", [128, 128], F32)
    ones_r = din("ones_r", [1, 128])
    ones_c = din("ones_c", [128, 1])
    cosT_h = din("cosT_h", [NB, 128, 64])
    sinT_h = din("sinT_h", [NB, 128, 64])
    cosQ_h = din("cosQ_h", [NOB, 128, 64])
    sinQ_h = din("sinQ_h", [NOB, 128, 64])
    masks_h = din("masks", [32, 128, 128], BF16)

    out_h = nc.dram_tensor("out_shard", [NOB * 128, D], F32, kind="ExternalOutput").ap()
    sc_qt = nc.dram_tensor("sc_qt", [H, 128, NOB * 128], BF16, kind="Internal").ap()
    sc_g = nc.dram_tensor("sc_g", [NOB, 128, D], BF16, kind="Internal").ap()
    sc_woq = nc.dram_tensor("sc_woq", [NIC, 128, D], BF16, kind="Internal").ap()
    sc_xq = nc.dram_tensor("sc_xq", [NOB, NIC, 128, 128], BF16, kind="Internal").ap()

    es = contextlib.ExitStack()

    AB = es.enter_context(nc.sbuf_tensor("AB", [128, 72832], BF16))
    AFR = es.enter_context(nc.sbuf_tensor("AFR", [128, 15360], F32))

    def vb(arena, c0, n, p0=0, p1=128):
        return arena[p0:p1, c0:c0 + n], Buf()

    ktf, ktf_b = vb(AB, 0, KVH * S)
    vaug, vaug_b = vb(AB, 16384, KVH * NB * 129)
    wkvq, wkvq_b = vb(AB, 32896, NIC * 1024)
    xqs0, xqs0_b = vb(AB, 32896, NIC * 128)
    xqs1, xqs1_b = vb(AB, 34944, NIC * 128)
    gbf, gbf_b = vb(AB, 32896, 4 * D)
    wqg, wqg_b = vb(AB, 41088, NIC * 512)
    xq, xq_b = vb(AB, 49280, D)
    xqt_t, xqt_t_b = vb(AB, 51328, NIC * 128)
    kbf, kbf_b = vb(AB, 53376, 512)
    qt_t, qt_t_b = vb(AB, 49280, H * 512)
    yq, yq_b = vb(AB, 57472, D)
    e0, e0_b = vb(AB, 59520, 512)
    e1, e1_b = vb(AB, 60032, 512)
    yqt, yqt_b = vb(AB, 60544, 4 * NIC * 128)
    masksb, masksb_b = vb(AB, 68736, 32 * 128)

    wld0, wld0_b = vb(AFR, 0, 2048)
    wld1, wld1_b = vb(AFR, 2048, 2048)
    wtt0, wtt0_b = vb(AFR, 4096, 512)
    wtt1, wtt1_b = vb(AFR, 4608, 512)
    xt0, xt0_b = vb(AFR, 0, 2048)
    xt1, xt1_b = vb(AFR, 2048, 2048)
    xn, xn_b = vb(AFR, 5120, 2048)
    tq, tq_b = vb(AFR, 7168, 2048)
    kwork, kwork_b = vb(AFR, 9216, 512)
    kwork2, kwork2_b = vb(AFR, 9728, 512)
    wskv_rep, wskv_rep_b = vb(AFR, 10240, 1024)
    cosT, cosT_b = vb(AFR, 11264, NB * 64)
    sinT, sinT_b = vb(AFR, 13312, NB * 64)
    cosQ, cosQ_b = vb(AFR, 11264, NOB * 64)
    sinQ, sinQ_b = vb(AFR, 12288, NOB * 64)
    wsqd, wsqd_b = vb(AFR, 13312, 512)
    y_tm, y_tm_b = vb(AFR, 0, 4 * D)
    wso_rep, wso_rep_b = vb(AFR, 9216, D)
    sdqt, sdqt_b = vb(AFR, 11264, 512)
    xrt, xrt_b = vb(AFR, 11776, 512)
    outt, outt_b = vb(AFR, 12288, 512)
    gft, gft_b = vb(AFR, 12800, 512)
    tqT, tqT_b = vb(AFR, 13312, 2048)
    rowstg1, rowstg1_b = vb(AFR, 5120, 512, 0, 1)
    rowstgi1, rowstgi1_b = vb(AFR, 5632, 512, 0, 1)
    rowstg3, rowstg3_b = vb(AFR, 10240, 512, 0, 1)
    rowstgi3, rowstgi3_b = vb(AFR, 10752, 512, 0, 1)

    def sbf(name, p, f, dt=F32):
        return es.enter_context(nc.sbuf_tensor(name, [p, f], dt)), Buf()

    adeq, adeq_b = sbf("adeq", 128, NB)
    adeqo, adeqo_b = sbf("adeqo", 128, NOB)
    afo, afo_b = sbf("afo", 128, NOB)
    kk, kk_b = sbf("kk", 128, 1)
    ce, ce_b = sbf("ce", 128, 1)
    idn, idn_b = sbf("idn", 128, 128, BF16)
    idnf, idnf_b = sbf("idnf_s", 128, 128)
    onr, onr_b = sbf("onr", 1, 128)
    onc, onc_b = sbf("onc", 128, 1)
    mneg, mneg_b = sbf("mneg", 128, H)
    gn, gn_b = sbf("gn", 1, H)
    gnb, gnb_b = sbf("gnb", 128, H)
    wscol, wscol_b = sbf("wscol", 128, 24)
    colA, colA_b = sbf("colA", 128, 1)
    colB, colB_b = sbf("colB", 128, 1)
    colC, colC_b = sbf("colC", 128, 1)
    colS, colS_b = sbf("colS", 128, 4)
    adqy, adqy_b = sbf("adqy", 128, 4)
    sins, sins_b = sbf("sins", 128, 128)
    ang, ang_b = sbf("ang", 128, 64)
    ang2, ang2_b = sbf("ang2", 128, 64)
    angk, angk_b = sbf("angk", 128, 64)
    angt, angt_b = sbf("angt", 128, 64)
    tmpr, tmpr_b = sbf("tmpr", 128, 64)
    tmp2, tmp2_b = sbf("tmp2", 1, 16)
    tmp11, tmp11_b = sbf("tmp11", 1, 1)
    zcol, zcol_b = sbf("zcol", 128, 4 * H)
    zinv, zinv_b = sbf("zinv", 128, 4 * H)
    r127, r127_b = sbf("r127", 128, 1)
    rs128, rs128_b = sbf("rs128", 128, 1)
    wsf_c, wsf_c_b = sbf("wsf_c", 128, 1)
    wsfd, wsfd_b = sbf("wsfd", 128, 1)

    ps, ps_b = [], []
    for i in range(8):
        ps.append(es.enter_context(nc.psum_tensor("ps%d" % i, [128, 512], F32)))
        ps_b.append(Buf())

    with es:
        dma("sync", lambda e: e.dma_start(out=idn[:], in_=idnb_h[:]), w=[idn_b])
        dma("sync", lambda e: e.dma_start(out=idnf[:], in_=idnf_h[:]), w=[idnf_b])
        dma("sync", lambda e: e.dma_start(out=onr[:], in_=ones_r[:]), w=[onr_b])
        dma("sync", lambda e: e.dma_start(out=onc[:], in_=ones_c[:]), w=[onc_b])
        dma("sync", lambda e: e.dma_start(out=gn[:], in_=gains_h[:]), w=[gn_b])
        dma("sync", lambda e: e.dma_start(
            out=cosT.rearrange("p (nb d) -> p nb d", nb=NB), in_=cosT_h.rearrange("nb p d -> p nb d")), w=[cosT_b])
        dma("sync", lambda e: e.dma_start(
            out=sinT.rearrange("p (nb d) -> p nb d", nb=NB), in_=sinT_h.rearrange("nb p d -> p nb d")), w=[sinT_b])

        op("vector", lambda v: v.memset(kk[:], 1.0), w=[kk_b])
        op("vector", lambda v: v.tensor_scalar(kk[:], kk[:], 2.0 ** -25, None, ALU.add), r=[kk_b], w=[kk_b])
        op("vector", lambda v: v.tensor_scalar(kk[:], kk[:], -1.0, None, ALU.add), r=[kk_b], w=[kk_b])
        op("vector", lambda v: v.tensor_scalar(kk[:], kk[:], 2.0 ** 25, None, ALU.mult), r=[kk_b], w=[kk_b])
        op("vector", lambda v: v.tensor_scalar(ce[:], kk[:], CDELTA, C23, ALU.mult, ALU.add), r=[kk_b], w=[ce_b])

        op("vector", lambda v: v.memset(r127[:], 127.0), w=[r127_b])
        op("vector", lambda v: v.reciprocal(r127[:], r127[:]), r=[r127_b], w=[r127_b])
        op("vector", lambda v: v.memset(rs128[:], 128.0), w=[rs128_b])
        op("scalar", lambda c: c.sqrt(rs128[:], rs128[:]), r=[rs128_b], w=[rs128_b])
        op("vector", lambda v: v.reciprocal(rs128[:], rs128[:]), r=[rs128_b], w=[rs128_b])


        op("scalar", lambda c: c.activation(tmp2[:], gn[:], AF.Abs), r=[gn_b], w=[tmp2_b])
        op("tensor", lambda t: t.matmul(ps[6][:, 0:16], onr[:], tmp2[:], start=True, stop=True), r=[onr_b, tmp2_b], w=[ps_b[6]])
        op("vector", lambda v: v.tensor_scalar(mneg[:], ps[6][:, 0:16], -11.35, -2.0, ALU.mult, ALU.add), r=[ps_b[6]], w=[mneg_b])
        op("tensor", lambda t: t.matmul(ps[6][:, 0:16], onr[:], gn[:], start=True, stop=True), r=[onr_b, gn_b], w=[ps_b[6]])
        op("vector", lambda v: v.tensor_scalar(gnb[:], ps[6][:, 0:16], rs128[:], None, ALU.mult), r=[ps_b[6], rs128_b], w=[gnb_b])

        op("vector", lambda v: v.memset(vaug[:], 1.0), w=[vaug_b])

        def rowscale_pass(src_h, nchunks):
            for c in range(nchunks):
                wb, wbb = (wld0, wld0_b) if c % 2 == 0 else (wld1, wld1_b)
                dma("sync", lambda e, c=c, wb=wb: e.dma_start(
                    out=wb[:], in_=src_h[c * 128:(c + 1) * 128, :]), w=[wbb])
                op("vector", lambda v, wb=wb: v.reduce_max(colA[:], wb[:], axis=AX.X, apply_absolute_value=True), r=[wbb], w=[colA_b])
                op("vector", lambda v, c=c: v.tensor_scalar(wscol[:, c:c + 1], colA[:], r127[:], 1e-5, ALU.mult, ALU.max), r=[colA_b, r127_b], w=[wscol_b])

        def build_row(chunks, rstg, rstg_b, rstgi, rstgi_b):
            for i, c in enumerate(chunks):
                op("tensor", lambda t, c=c: t.matmul(ps[7][0:1, 0:128], wscol[:, c:c + 1], idnf[:], start=True, stop=True), r=[wscol_b, idnf_b], w=[ps_b[7]])
                op("vector", lambda v, i=i, rstg=rstg: v.tensor_copy(rstg[:, i * 128:(i + 1) * 128], ps[7][0:1, 0:128]), r=[ps_b[7]], w=[rstg_b])
            op("vector", lambda v, rstg=rstg, rstgi=rstgi: v.reciprocal(rstgi[:], rstg[:]), r=[rstg_b], w=[rstgi_b])

        def rep_to_ps(bank, row_t, row_bb):
            op("tensor", lambda t: t.matmul(ps[bank][:], onr[:], row_t[:], start=True, stop=True),
               r=[onr_b, row_bb], w=[ps_b[bank]])

        def quant_w_tile(srcT_h, col0, ic, dst_cs, repbank, buf, bufb, clip1=False, scalcol=None, scalcol_b=None):
            dma("sync", lambda e: e.dma_start(out=buf[:], in_=srcT_h[ic * 128:(ic + 1) * 128, col0:col0 + 512]), w=[bufb])
            if scalcol is not None:
                op("vector", lambda v: v.tensor_scalar(buf[:], buf[:], scalcol[:], None, ALU.mult), r=[bufb, scalcol_b], w=[bufb])
            else:
                op("vector", lambda v: v.tensor_tensor(buf[:], buf[:], ps[repbank][:], ALU.mult), r=[bufb, ps_b[repbank]], w=[bufb])
            op("vector", lambda v: v.tensor_scalar(buf[:], buf[:], ce[:], None, ALU.add), r=[bufb, ce_b], w=[bufb])
            if clip1:
                op("vector", lambda v: v.tensor_scalar(buf[:], buf[:], ce[:], None, ALU.subtract), r=[bufb, ce_b], w=[bufb])
                op("vector", lambda v: v.tensor_scalar(wqg[:, dst_cs], buf[:], 1.0, -1.0, ALU.min, ALU.max), r=[bufb], w=[wqg_b])
            else:
                op("vector", lambda v: v.tensor_scalar(wqg[:, dst_cs], buf[:], ce[:], None, ALU.subtract), r=[bufb, ce_b], w=[wqg_b])

        def ln_quant(src_ap, idx, adeq_t, adeq_bb, af_t, af_bb, xin, xin_b):
            dma("sync", lambda e: e.dma_start(out=xin[:], in_=src_ap), w=[xin_b])
            op("vector", lambda v: v.reduce_sum(colA[:], xin[:], axis=AX.X), r=[xin_b], w=[colA_b])
            op("vector", lambda v: v.tensor_scalar(colA[:], colA[:], 1.0 / D, None, ALU.mult), r=[colA_b], w=[colA_b])
            op("vector", lambda v: v.tensor_scalar(xn[:], xin[:], colA[:], None, ALU.subtract), r=[xin_b, colA_b], w=[xn_b])
            op("scalar", lambda c: c.activation(tq[:], xn[:], AF.Square), r=[xn_b], w=[tq_b])
            op("vector", lambda v: v.reduce_sum(colA[:], tq[:], axis=AX.X), r=[tq_b], w=[colA_b])
            op("vector", lambda v: v.tensor_scalar(colA[:], colA[:], 1.0 / D, EPS_LN, ALU.mult, ALU.add), r=[colA_b], w=[colA_b])
            op("scalar", lambda c: c.sqrt(colA[:], colA[:]), r=[colA_b], w=[colA_b])
            op("vector", lambda v: v.reciprocal(colA[:], colA[:]), r=[colA_b], w=[colA_b])
            op("vector", lambda v: v.tensor_scalar(xn[:], xn[:], colA[:], None, ALU.mult), r=[xn_b, colA_b], w=[xn_b])
            op("vector", lambda v: v.reduce_max(colA[:], xn[:], axis=AX.X, apply_absolute_value=True), r=[xn_b], w=[colA_b])
            if af_t is not None:
                op("vector", lambda v: v.tensor_scalar(af_t[:, idx:idx + 1], colA[:], 1e-8, None, ALU.max), r=[colA_b], w=[af_bb])
            op("vector", lambda v: v.tensor_scalar(colB[:], colA[:], r127[:], 1e-5, ALU.mult, ALU.max), r=[colA_b, r127_b], w=[colB_b])
            op("vector", lambda v: v.tensor_copy(adeq_t[:, idx:idx + 1], colB[:]), r=[colB_b], w=[adeq_bb])
            op("vector", lambda v: v.reciprocal(colB[:], colB[:]), r=[colB_b], w=[colB_b])
            op("vector", lambda v: v.tensor_scalar(tq[:], xn[:], colB[:], None, ALU.mult), r=[xn_b, colB_b], w=[tq_b])
            op("vector", lambda v: v.tensor_scalar(tq[:], tq[:], ce[:], None, ALU.add), r=[tq_b, ce_b], w=[tq_b])
            op("vector", lambda v: v.tensor_scalar(xq[:], tq[:], ce[:], None, ALU.subtract), r=[tq_b, ce_b], w=[xq_b])

        def transpose16(src_t, src_bb, dst_ap_fn, dst_bb, bank0=0, bank1=1):
            for g4 in range(4):
                bank = bank0 if g4 % 2 == 0 else bank1
                opg("tensor", [
                    (lambda t, c=g4 * 4 + i, i=i, bank=bank: t.matmul(
                        ps[bank][:, i * 128:(i + 1) * 128], src_t[:, c * 128:(c + 1) * 128], idn[:],
                        start=(i == 0), stop=(i == 3))) for i in range(4)],
                    r=[src_bb, idn_b], w=[ps_b[bank]])
                op("scalar", lambda c, g4=g4, bank=bank: c.copy(
                    dst_ap_fn(g4), ps[bank][:].rearrange("p (c t) -> p c t", c=4)), r=[ps_b[bank]], w=[dst_bb])

        def rope_tm(work_t, work_bb, nheads, cos_t, cos_bb, sin_t, sin_bb, blk, w2, w2b):
            cs = slice(blk * 64, blk * 64 + 64)
            nb_ = nheads * 128
            op("vector", lambda v: v.tensor_copy(sins[:, 0:64], sin_t[:, cs]), r=[sin_bb], w=[sins_b])
            op("vector", lambda v: v.tensor_scalar(sins[:, 64:128], sin_t[:, cs], -1.0, None, ALU.mult), r=[sin_bb], w=[sins_b])
            wv = lambda t_, a, b: t_[:, 0:nb_].rearrange("p (h x d) -> p h x d", h=nheads, x=2)[:, :, a:b, :]
            op("vector", lambda v: v.tensor_copy(wv(w2, 0, 1), wv(work_t, 1, 2)), r=[work_bb], w=[w2b])
            op("vector", lambda v: v.tensor_copy(wv(w2, 1, 2), wv(work_t, 0, 1)), r=[work_bb], w=[w2b])
            cosv = cos_t[:, cs].unsqueeze(1).unsqueeze(2).broadcast_to((128, nheads, 2, 64))
            sinv = sins[:, 0:128].unsqueeze(1).broadcast_to((128, nheads, 128))
            op("vector", lambda v: v.tensor_tensor(
                work_t[:, 0:nb_].rearrange("p (h x d) -> p h x d", h=nheads, x=2),
                work_t[:, 0:nb_].rearrange("p (h x d) -> p h x d", h=nheads, x=2), cosv, ALU.mult),
                r=[work_bb, cos_bb], w=[work_bb])
            op("vector", lambda v: v.tensor_tensor(
                w2[:, 0:nb_].rearrange("p (h d) -> p h d", h=nheads),
                w2[:, 0:nb_].rearrange("p (h d) -> p h d", h=nheads), sinv, ALU.mult),
                r=[w2b, sins_b], w=[w2b])
            op("vector", lambda v: v.tensor_tensor(work_t[:, 0:nb_], work_t[:, 0:nb_], w2[:, 0:nb_], ALU.add),
               r=[work_bb, w2b], w=[work_bb])

        rowscale_pass(wqkv_R, 24)
        for g in range(2):
            build_row(range(16 + g * 4, 16 + (g + 1) * 4), rowstg1, rowstg1_b, rowstgi1, rowstgi1_b)
            rep_to_ps(5, rowstg1, rowstg1_b)
            op("vector", lambda v, g=g: v.tensor_copy(wskv_rep[:, g * 512:(g + 1) * 512], ps[5][:]), r=[ps_b[5]], w=[wskv_rep_b])
        for og in range(2):
            col0 = 2048 + og * 512
            build_row(range(16 + og * 4, 16 + (og + 1) * 4), rowstg1, rowstg1_b, rowstgi1, rowstgi1_b)
            rep_to_ps(5, rowstgi1, rowstgi1_b)
            for ic in range(NIC):
                buf, bufb = (wtt0, wtt0_b) if ic % 2 == 0 else (wtt1, wtt1_b)
                dma("sync", lambda e, ic=ic, col0=col0, buf=buf: e.dma_start(
                    out=buf[:], in_=wqkv_T[ic * 128:(ic + 1) * 128, col0:col0 + 512]), w=[bufb])
                op("vector", lambda v, buf=buf: v.tensor_tensor(buf[:], buf[:], ps[5][:], ALU.mult), r=[bufb, ps_b[5]], w=[bufb])
                op("vector", lambda v, buf=buf: v.tensor_scalar(buf[:], buf[:], ce[:], None, ALU.add), r=[bufb, ce_b], w=[bufb])
                op("vector", lambda v, ic=ic, og=og, buf=buf: v.tensor_scalar(
                    wkvq[:, ic * 1024 + og * 512: ic * 1024 + (og + 1) * 512], buf[:], ce[:], None, ALU.subtract),
                    r=[bufb, ce_b], w=[wkvq_b])
        S_.barrier()

        for blk in range(NB):
            xin, xin_b = (xt0, xt0_b) if blk % 2 == 0 else (xt1, xt1_b)
            ln_quant(x_full[blk * 128:(blk + 1) * 128, :], blk, adeq, adeq_b, None, None, xin, xin_b)
            transpose16(xq, xq_b, lambda g4: xqt_t[:, g4 * 512:(g4 + 1) * 512].rearrange("p (c t) -> p c t", c=4), xqt_t_b, 0, 1)
            for og in range(2):
                bank = 2 + og
                opg("tensor", [
                    (lambda t, ic=ic, og=og, bank=bank: t.matmul(
                        ps[bank][:], xqt_t[:, ic * 128:(ic + 1) * 128],
                        wkvq[:, ic * 1024 + og * 512: ic * 1024 + (og + 1) * 512],
                        start=(ic == 0), stop=(ic == NIC - 1))) for ic in range(NIC)],
                    r=[xqt_t_b, wkvq_b], w=[ps_b[bank]])
                op("vector", lambda v, bank=bank, blk=blk: v.tensor_scalar(
                    kwork[:], ps[bank][:], adeq[:, blk:blk + 1], None, ALU.mult), r=[ps_b[bank], adeq_b], w=[kwork_b])
                op("vector", lambda v, og=og: v.tensor_tensor(
                    kwork[:], kwork[:], wskv_rep[:, og * 512:(og + 1) * 512], ALU.mult), r=[kwork_b, wskv_rep_b], w=[kwork_b])
                if og == 0:
                    op("scalar", lambda c: c.activation(kwork2[:], kwork[:], AF.Square), r=[kwork_b], w=[kwork2_b])
                    for kv in range(KVH):
                        op("vector", lambda v, kv=kv: v.reduce_sum(colS[:, kv:kv + 1], kwork2[:, kv * 128:(kv + 1) * 128], axis=AX.X), r=[kwork2_b], w=[colS_b])
                    op("vector", lambda v: v.tensor_scalar(colS[:], colS[:], 1.0 / 128.0, EPS_RMS, ALU.mult, ALU.add), r=[colS_b], w=[colS_b])
                    op("scalar", lambda c: c.sqrt(colS[:], colS[:]), r=[colS_b], w=[colS_b])
                    op("vector", lambda v: v.reciprocal(colS[:], colS[:]), r=[colS_b], w=[colS_b])
                    for kv in range(KVH):
                        op("vector", lambda v, kv=kv: v.tensor_scalar(
                            kwork[:, kv * 128:(kv + 1) * 128], kwork[:, kv * 128:(kv + 1) * 128],
                            colS[:, kv:kv + 1], None, ALU.mult), r=[kwork_b, colS_b], w=[kwork_b])
                    rope_tm(kwork, kwork_b, KVH, cosT, cosT_b, sinT, sinT_b, blk, kwork2, kwork2_b)
                    op("scalar", lambda c: c.copy(kbf[:], kwork[:]), r=[kwork_b], w=[kbf_b])
                    opg("tensor", [
                        (lambda t, kv=kv: t.matmul(
                            ps[4][:, kv * 128:(kv + 1) * 128], kbf[:, kv * 128:(kv + 1) * 128], idn[:],
                            start=(kv == 0), stop=(kv == 3))) for kv in range(KVH)],
                        r=[kbf_b, idn_b], w=[ps_b[4]])
                    op("scalar", lambda c, blk=blk: c.copy(
                        ktf.rearrange("p (kv s) -> p kv s", kv=KVH)[:, :, blk * 128:(blk + 1) * 128],
                        ps[4][:].rearrange("p (kv r) -> p kv r", kv=KVH)), r=[ps_b[4]], w=[ktf_b])
                else:
                    op("scalar", lambda c, blk=blk: c.copy(
                        vaug.rearrange("p (kv nb c) -> p kv nb c", kv=KVH, nb=NB)[:, :, blk:blk + 1, 0:128],
                        kwork[:].rearrange("p (kv o d) -> p kv o d", kv=KVH, o=1)), r=[kwork_b], w=[vaug_b])
        S_.barrier()

        dma("sync", lambda e: e.dma_start(
            out=cosQ.rearrange("p (nb d) -> p nb d", nb=NOB), in_=cosQ_h.rearrange("nb p d -> p nb d")), w=[cosQ_b])
        dma("sync", lambda e: e.dma_start(
            out=sinQ.rearrange("p (nb d) -> p nb d", nb=NOB), in_=sinQ_h.rearrange("nb p d -> p nb d")), w=[sinQ_b])
        for ot in range(NOB):
            xin, xin_b = (xt0, xt0_b) if ot % 2 == 0 else (xt1, xt1_b)
            ln_quant(x_own[ot * 128:(ot + 1) * 128, :], ot, adeqo, adeqo_b, afo, afo_b, xin, xin_b)
            transpose16(xq, xq_b, lambda g4: xqt_t[:, g4 * 512:(g4 + 1) * 512].rearrange("p (c t) -> p c t", c=4), xqt_t_b, 0, 1)
            dma("sync", lambda e, ot=ot: e.dma_start(
                out=sc_xq[ot].rearrange("ic p t -> p ic t"),
                in_=xqt_t[:].rearrange("p (ic t) -> p ic t", ic=NIC)), r=[xqt_t_b])
        for og in range(4):
            build_row(range(og * 4, (og + 1) * 4), rowstg3, rowstg3_b, rowstgi3, rowstgi3_b)
            rep_to_ps(5, rowstg3, rowstg3_b)
            op("vector", lambda v: v.tensor_copy(wsqd[:], ps[5][:]), r=[ps_b[5]], w=[wsqd_b])
            rep_to_ps(5, rowstgi3, rowstgi3_b)
            for ic in range(NIC):
                buf, bufb = (wtt0, wtt0_b) if ic % 2 == 0 else (wtt1, wtt1_b)
                quant_w_tile(wqkv_T, og * 512, ic, slice(ic * 512, (ic + 1) * 512), 5, buf, bufb)
            def q_chain(ot):
                bank = 2 + ot % 2
                xs, xs_b = (xqs0, xqs0_b) if ot % 2 == 0 else (xqs1, xqs1_b)
                dma("sync", lambda e, ot=ot, xs=xs: e.dma_start(
                    out=xs[:].rearrange("p (ic t) -> p ic t", ic=NIC),
                    in_=sc_xq[ot].rearrange("ic p t -> p ic t")), w=[xs_b])
                opg("tensor", [
                    (lambda t, ic=ic, bank=bank, xs=xs: t.matmul(
                        ps[bank][:], xs[:, ic * 128:(ic + 1) * 128],
                        wqg[:, ic * 512:(ic + 1) * 512],
                        start=(ic == 0), stop=(ic == NIC - 1))) for ic in range(NIC)],
                    r=[xs_b, wqg_b], w=[ps_b[bank]])

            q_chain(0)
            for ot in range(NOB):
                bank = 2 + ot % 2
                if ot + 1 < NOB:
                    q_chain(ot + 1)
                op("vector", lambda v, bank=bank, ot=ot: v.tensor_scalar(
                    kwork[:], ps[bank][:], adeqo[:, ot:ot + 1], None, ALU.mult), r=[ps_b[bank], adeqo_b], w=[kwork_b])
                op("vector", lambda v: v.tensor_tensor(kwork[:], kwork[:], wsqd[:], ALU.mult), r=[kwork_b, wsqd_b], w=[kwork_b])
                op("scalar", lambda c: c.activation(kwork2[:], kwork[:], AF.Square), r=[kwork_b], w=[kwork2_b])
                for i in range(4):
                    op("vector", lambda v, i=i: v.reduce_sum(colS[:, i:i + 1], kwork2[:, i * 128:(i + 1) * 128], axis=AX.X), r=[kwork2_b], w=[colS_b])
                op("vector", lambda v: v.tensor_scalar(colS[:], colS[:], 1.0 / 128.0, EPS_RMS, ALU.mult, ALU.add), r=[colS_b], w=[colS_b])
                op("scalar", lambda c: c.sqrt(colS[:], colS[:]), r=[colS_b], w=[colS_b])
                op("vector", lambda v: v.reciprocal(colS[:], colS[:]), r=[colS_b], w=[colS_b])
                op("vector", lambda v, og=og: v.tensor_tensor(colS[:], colS[:], gnb[:, og * 4:(og + 1) * 4], ALU.mult), r=[colS_b, gnb_b], w=[colS_b])
                for i in range(4):
                    op("vector", lambda v, i=i: v.tensor_scalar(
                        kwork[:, i * 128:(i + 1) * 128], kwork[:, i * 128:(i + 1) * 128],
                        colS[:, i:i + 1], None, ALU.mult), r=[kwork_b, colS_b], w=[kwork_b])
                rope_tm(kwork, kwork_b, 4, cosQ, cosQ_b, sinQ, sinQ_b, ot, kwork2, kwork2_b)
                op("scalar", lambda c: c.copy(kbf[:], kwork[:]), r=[kwork_b], w=[kbf_b])
                opg("tensor", [
                    (lambda t, i=i: t.matmul(
                        ps[4][:, i * 128:(i + 1) * 128], kbf[:, i * 128:(i + 1) * 128], idn[:],
                        start=(i == 0), stop=(i == 3))) for i in range(4)],
                    r=[kbf_b, idn_b], w=[ps_b[4]])
                op("vector", lambda v: v.tensor_copy(kbf[:], ps[4][:]), r=[ps_b[4]], w=[kbf_b])
                dma("sync", lambda e, og=og, ot=ot: e.dma_start(
                    out=sc_qt[og * 4:(og + 1) * 4, :, ot * 128:(ot + 1) * 128].rearrange("h p t -> p h t"),
                    in_=kbf[:].rearrange("p (h t) -> p h t", h=4)), r=[kbf_b])
        S_.barrier()

        op("vector", lambda v: v.memset(colB[:], 0.0), w=[colB_b])
        for c in range(NIC):
            wb, wbb = (wld0, wld0_b) if c % 2 == 0 else (wld1, wld1_b)
            dma("sync", lambda e, c=c, wb=wb: e.dma_start(out=wb[:], in_=ffn_R[c * 128:(c + 1) * 128, :]), w=[wbb])
            op("vector", lambda v, wb=wb: v.reduce_sum(colA[:], wb[:], axis=AX.X, apply_absolute_value=True), r=[wbb], w=[colA_b])
            op("vector", lambda v: v.tensor_tensor(colB[:], colB[:], colA[:], ALU.add), r=[colB_b, colA_b], w=[colB_b])
        op("tensor", lambda t: t.matmul(ps[7][0:1, 0:1], colB[:], onc[:], start=True, stop=True), r=[colB_b, onc_b], w=[ps_b[7]])
        op("vector", lambda v: v.tensor_copy(tmp11[:], ps[7][0:1, 0:1]), r=[ps_b[7]], w=[tmp11_b])
        op("vector", lambda v: v.tensor_scalar(tmp11[:], tmp11[:], 1.0 / (D * D), 1e-6, ALU.mult, ALU.max), r=[tmp11_b], w=[tmp11_b])
        op("tensor", lambda t: t.matmul(ps[7][:, 0:1], onr[:], tmp11[:], start=True, stop=True), r=[onr_b, tmp11_b], w=[ps_b[7]])
        op("vector", lambda v: v.tensor_scalar(wsfd[:], ps[7][:, 0:1], r127[:], None, ALU.mult), r=[ps_b[7], r127_b], w=[wsfd_b])
        op("vector", lambda v: v.reciprocal(wsf_c[:], ps[7][:, 0:1]), r=[ps_b[7]], w=[wsf_c_b])
        for og in range(4):
            for ic in range(NIC):
                buf, bufb = (wtt0, wtt0_b) if ic % 2 == 0 else (wtt1, wtt1_b)
                quant_w_tile(ffn_T, og * 512, ic, slice(ic * 512, (ic + 1) * 512), None, buf, bufb,
                             clip1=True, scalcol=wsf_c, scalcol_b=wsf_c_b)
            def f_chain(ot):
                bank = 2 + ot % 2
                xs, xs_b = (xqs0, xqs0_b) if ot % 2 == 0 else (xqs1, xqs1_b)
                dma("sync", lambda e, ot=ot, xs=xs: e.dma_start(
                    out=xs[:].rearrange("p (ic t) -> p ic t", ic=NIC),
                    in_=sc_xq[ot].rearrange("ic p t -> p ic t")), w=[xs_b])
                opg("tensor", [
                    (lambda t, ic=ic, bank=bank, xs=xs: t.matmul(
                        ps[bank][:], xs[:, ic * 128:(ic + 1) * 128],
                        wqg[:, ic * 512:(ic + 1) * 512],
                        start=(ic == 0), stop=(ic == NIC - 1))) for ic in range(NIC)],
                    r=[xs_b, wqg_b], w=[ps_b[bank]])

            f_chain(0)
            for ot in range(NOB):
                bank = 2 + ot % 2
                if ot + 1 < NOB:
                    f_chain(ot + 1)
                op("vector", lambda v, ot=ot: v.tensor_tensor(colC[:], afo[:, ot:ot + 1], wsfd[:], ALU.mult), r=[afo_b, wsfd_b], w=[colC_b])
                op("scalar", lambda c, bank=bank: c.activation(kbf[:], ps[bank][:], AF.Silu, scale=colC[:]), r=[ps_b[bank], colC_b], w=[kbf_b])
                dma("sync", lambda e, ot=ot, og=og: e.dma_start(
                    out=sc_g[ot, :, og * 512:(og + 1) * 512], in_=kbf[:]), r=[kbf_b])
        S_.barrier()

        rowscale_pass(wo_R, NIC)
        for og in range(4):
            build_row(range(og * 4, (og + 1) * 4), rowstg1, rowstg1_b, rowstgi1, rowstgi1_b)
            rep_to_ps(5, rowstgi1, rowstgi1_b)
            for ic in range(NIC):
                buf, bufb = (wtt0, wtt0_b) if ic % 2 == 0 else (wtt1, wtt1_b)
                quant_w_tile(wo_T, og * 512, ic, slice(ic * 512, (ic + 1) * 512), 5, buf, bufb)
            dma("sync", lambda e, og=og: e.dma_start(
                out=sc_woq[:, :, og * 512:(og + 1) * 512].rearrange("ic p d -> p ic d"),
                in_=wqg[:].rearrange("p (ic d) -> p ic d", ic=NIC)), r=[wqg_b])
            rep_to_ps(5, rowstg1, rowstg1_b)
            op("vector", lambda v, og=og: v.tensor_copy(wso_rep[:, og * 512:(og + 1) * 512], ps[5][:]), r=[ps_b[5]], w=[wso_rep_b])
        S_.barrier()

        dma("sync", lambda e: e.dma_start(
            out=masksb.rearrange("p (m q) -> p m q", m=32), in_=masks_h.rearrange("m p q -> p m q")), w=[masksb_b])
        for qt in range(4):
            KMAX = 8 * qt + 8
            dma("sync", lambda e, qt=qt: e.dma_start(
                out=qt_t[:].rearrange("p (h t) -> p h t", h=H),
                in_=sc_qt[:, :, qt * 512:(qt + 1) * 512].rearrange("h p t -> p h t")), w=[qt_t_b])
            for h in range(H):
                kv = h // 4

                def st_mm(kc, kv=kv, h=h):
                    stb = 4 + (kc % 2)
                    op("tensor", lambda t, kc=kc, kv=kv, h=h, stb=stb: t.matmul(
                        ps[stb][:], ktf[:, kv * S + kc * 128: kv * S + (kc + 1) * 128],
                        qt_t[:, h * 512:(h + 1) * 512], start=True, stop=True),
                        r=[ktf_b, qt_t_b], w=[ps_b[stb]])

                st_mm(0)
                for kc in range(KMAX):
                    stb = 4 + (kc % 2)
                    if kc + 1 < KMAX:
                        st_mm(kc + 1)
                    e_t, e_b = (e0, e0_b) if kc % 2 == 0 else (e1, e1_b)
                    op("scalar", lambda c, h=h, stb=stb, e_t=e_t: c.activation(
                        e_t[:], ps[stb][:], AF.Exp, bias=mneg[:, h:h + 1]), r=[ps_b[stb], mneg_b], w=[e_b])
                    mmfns = []
                    wbanks = []
                    for j in range(4):
                        dd = kc - (8 * qt + 2 * j)
                        if dd > 1:
                            continue
                        if dd >= 0:
                            m_idx = (qt * 4 + j) * 2 + dd
                            op("vector", lambda v, j=j, m_idx=m_idx, e_t=e_t: v.tensor_tensor(
                                e_t[:, j * 128:(j + 1) * 128], e_t[:, j * 128:(j + 1) * 128],
                                masksb[:, m_idx * 128:(m_idx + 1) * 128], ALU.mult), r=[masksb_b, e_b], w=[e_b])
                        last_kc = min(8 * qt + 2 * j + 1, KMAX - 1)
                        mmfns.append(lambda t, j=j, kc=kc, kv=kv, e_t=e_t, last_kc=last_kc: t.matmul(
                            ps[j][:, 0:129], e_t[:, j * 128:(j + 1) * 128],
                            vaug[:, (kv * NB + kc) * 129: (kv * NB + kc + 1) * 129],
                            start=(kc == 0), stop=(kc == last_kc)))
                        wbanks.append(ps_b[j])
                    opg("tensor", mmfns, r=[e_b, vaug_b], w=wbanks)
                for j in range(4):
                    op("vector", lambda v, j=j, h=h: v.tensor_copy(
                        y_tm[:, j * D + h * 128: j * D + (h + 1) * 128], ps[j][:, 0:128]), r=[ps_b[j]], w=[y_tm_b])
                    op("vector", lambda v, j=j, h=h: v.tensor_copy(
                        zcol[:, j * H + h: j * H + h + 1], ps[j][:, 128:129]), r=[ps_b[j]], w=[zcol_b])
            for j in range(4):
                op("vector", lambda v, j=j: v.reciprocal(zinv[:, j * H:(j + 1) * H], zcol[:, j * H:(j + 1) * H]), r=[zcol_b], w=[zinv_b])
                for h in range(H):
                    op("vector", lambda v, j=j, h=h: v.tensor_scalar(
                        y_tm[:, j * D + h * 128: j * D + (h + 1) * 128],
                        y_tm[:, j * D + h * 128: j * D + (h + 1) * 128],
                        zinv[:, j * H + h: j * H + h + 1], None, ALU.mult), r=[zinv_b, y_tm_b], w=[y_tm_b])
                op("vector", lambda v, j=j: v.reduce_max(colA[:], y_tm[:, j * D:(j + 1) * D], axis=AX.X, apply_absolute_value=True), r=[y_tm_b], w=[colA_b])
                op("vector", lambda v: v.tensor_scalar(colB[:], colA[:], r127[:], 1e-5, ALU.mult, ALU.max), r=[colA_b, r127_b], w=[colB_b])
                op("vector", lambda v, j=j: v.tensor_copy(adqy[:, j:j + 1], colB[:]), r=[colB_b], w=[adqy_b])
                op("vector", lambda v: v.reciprocal(colB[:], colB[:]), r=[colB_b], w=[colB_b])
                op("vector", lambda v, j=j: v.tensor_scalar(tqT[:], y_tm[:, j * D:(j + 1) * D], colB[:], None, ALU.mult), r=[y_tm_b, colB_b], w=[tqT_b])
                op("vector", lambda v: v.tensor_scalar(tqT[:], tqT[:], ce[:], None, ALU.add), r=[tqT_b, ce_b], w=[tqT_b])
                op("vector", lambda v: v.tensor_scalar(yq[:], tqT[:], ce[:], None, ALU.subtract), r=[tqT_b, ce_b], w=[yq_b])
                transpose16(yq, yq_b, lambda g4, j=j: yqt[:, j * D + g4 * 512: j * D + (g4 + 1) * 512].rearrange("p (c t) -> p c t", c=4), yqt_b, 6, 7)
            for j in range(4):
                ot = qt * 4 + j
                dma("sync", lambda e, ot=ot, j=j: e.dma_start(
                    out=gbf[:, j * D:(j + 1) * D], in_=sc_g[ot, :, :]), w=[gbf_b])
            for og in range(4):
                dma("sync", lambda e, og=og: e.dma_start(
                    out=wqg[:].rearrange("p (ic d) -> p ic d", ic=NIC),
                    in_=sc_woq[:, :, og * 512:(og + 1) * 512].rearrange("ic p d -> p ic d")), w=[wqg_b])
                for j in range(4):
                    ot = qt * 4 + j
                    bank = 6 + (j % 2)
                    opg("tensor", [
                        (lambda t, j=j, ic=ic, bank=bank: t.matmul(
                            ps[bank][:], yqt[:, j * D + ic * 128: j * D + (ic + 1) * 128],
                            wqg[:, ic * 512:(ic + 1) * 512],
                            start=(ic == 0), stop=(ic == NIC - 1))) for ic in range(NIC)],
                        r=[yqt_b, wqg_b], w=[ps_b[bank]])
                    op("vector", lambda v, j=j, bank=bank: v.tensor_scalar(
                        sdqt[:], ps[bank][:], adqy[:, j:j + 1], None, ALU.mult), r=[ps_b[bank], adqy_b], w=[sdqt_b])
                    op("vector", lambda v, og=og: v.tensor_tensor(
                        sdqt[:], sdqt[:], wso_rep[:, og * 512:(og + 1) * 512], ALU.mult), r=[wso_rep_b, sdqt_b], w=[sdqt_b])
                    dma("sync", lambda e, ot=ot, og=og: e.dma_start(
                        out=xrt[:], in_=x_own[ot * 128:(ot + 1) * 128, og * 512:(og + 1) * 512]), w=[xrt_b])
                    op("vector", lambda v, j=j, og=og: v.tensor_copy(
                        gft[:], gbf[:, j * D + og * 512: j * D + (og + 1) * 512]), r=[gbf_b], w=[gft_b])
                    op("vector", lambda v: v.tensor_tensor(outt[:], sdqt[:], gft[:], ALU.mult), r=[sdqt_b, gft_b], w=[outt_b])
                    op("vector", lambda v: v.tensor_tensor(outt[:], outt[:], xrt[:], ALU.add), r=[outt_b, xrt_b], w=[outt_b])
                    dma("sync", lambda e, ot=ot, og=og: e.dma_start(
                        out=out_h[ot * 128:(ot + 1) * 128, og * 512:(og + 1) * 512], in_=outt[:]), r=[outt_b])

        S_.wait("sync", [(s, S_.cnt[s]) for s in ALL_DMA_SEMS if S_.cnt[s] > 0])

        sems = {}
        for name in list(ENG_SEM.values()) + ALL_DMA_SEMS:
            sems[name] = es.enter_context(nc.semaphore(name))
        block = es.enter_context(nc.Block())

        def mk(engname):
            items = S_.items[engname]
            osem = ENG_SEM[engname]

            def body(proxy):
                waited = {}
                for it in items:
                    if it[0] == "w":
                        _, s, v = it
                        if waited.get(s, -1) >= v:
                            continue
                        proxy.wait_ge(sems[s], v)
                        waited[s] = v
                    elif it[0] == "op":
                        it[1](proxy).then_inc(sems[osem], 1)
                    elif it[0] == "opg":
                        for f in it[1][:-1]:
                            f(proxy)
                        it[1][-1](proxy).then_inc(sems[osem], 1)
                    else:
                        it[1](proxy).then_inc(sems[it[2]], 16)
            return body

        block.tensor(mk("tensor"))
        block.vector(mk("vector"))
        block.scalar(mk("scalar"))
        block.sync(mk("sync"))
        block.gpsimd(mk("gpsimd"))

    return nc


def kernel(L0_attn_x, L0_attn_w_q, L0_attn_w_k, L0_attn_w_v, L0_attn_w_o,
           L0_attn_q_gain, L0_ffn_weight):
    x = np.ascontiguousarray(np.asarray(L0_attn_x, dtype=np.float32))
    wq = np.asarray(L0_attn_w_q, dtype=np.float32)
    wk = np.asarray(L0_attn_w_k, dtype=np.float32)
    wv = np.asarray(L0_attn_w_v, dtype=np.float32)
    wo = np.asarray(L0_attn_w_o, dtype=np.float32)
    gain = np.asarray(L0_attn_q_gain, dtype=np.float32)
    wf = np.asarray(L0_ffn_weight, dtype=np.float32)

    wqkv_R = np.ascontiguousarray(np.concatenate([wq, wk, wv], axis=0))
    wqkv_T = np.ascontiguousarray(wqkv_R.T)
    wo_T = np.ascontiguousarray(wo.T)
    ffn_T = np.ascontiguousarray(wf.T)

    idnb = np.eye(128, dtype=ml_dtypes.bfloat16)
    idnf = np.eye(128, dtype=np.float32)
    ones_r = np.ones((1, 128), dtype=np.float32)
    ones_c = np.ones((128, 1), dtype=np.float32)
    cosT_t, sinT_t = _rope_tables_host()

    in_maps = []
    block_maps = []
    for c in range(8):
        b, h = c // 2, c % 2
        obs = own_blocks(h)
        block_maps.append(obs)
        xb = np.ascontiguousarray(x[b])
        x_own = np.ascontiguousarray(xb.reshape(NB, 128, D)[obs].reshape(NOB * 128, D))
        tq_own = np.asarray([g * 128 + i for g in obs for i in range(128)], dtype=np.int64)
        cosQ_t = np.ascontiguousarray(cosT_t.reshape(S, 64)[tq_own].reshape(NOB, 128, 64))
        sinQ_t = np.ascontiguousarray(sinT_t.reshape(S, 64)[tq_own].reshape(NOB, 128, 64))
        masks = np.zeros((32, 128, 128), dtype=ml_dtypes.bfloat16)
        tri = np.triu(np.ones((128, 128), dtype=np.float32))
        for t in range(4):
            for j in range(4):
                g = obs[t * 4 + j]
                for slot in range(2):
                    kc = 8 * t + 2 * j + slot
                    if kc < g:
                        m = np.ones((128, 128), dtype=np.float32)
                    elif kc == g:
                        m = tri
                    else:
                        m = np.zeros((128, 128), dtype=np.float32)
                    masks[(t * 4 + j) * 2 + slot] = m.astype(ml_dtypes.bfloat16)
        in_maps.append({
            "x_full": xb, "x_own": x_own,
            "wqkv_T": wqkv_T, "wqkv_R": wqkv_R, "wo_T": wo_T, "wo_R": wo,
            "ffn_T": ffn_T, "ffn_R": wf,
            "gains": gain.reshape(1, H), "idnb": idnb, "idnf": idnf,
            "ones_r": ones_r, "ones_c": ones_c,
            "cosT_h": cosT_t, "sinT_h": sinT_t, "cosQ_h": cosQ_t, "sinQ_h": sinQ_t,
            "masks": masks,
        })

    nc = build_nc()
    res = run_bass_kernel_spmd(nc, in_maps, core_ids=list(range(8)))

    out = np.empty((B, S, D), dtype=np.float32)
    for c in range(8):
        b = c // 2
        shard = np.asarray(res.results[c]["out_shard"]).reshape(NOB, 128, D)
        for i, g in enumerate(block_maps[c]):
            out[b, g * 128:(g + 1) * 128, :] = shard[i]
    return out


# version 11 (final)
# speedup vs baseline: 1.2196x; 1.0066x over previous
import contextlib
import numpy as np
import ml_dtypes
import concourse.bass as bass
import concourse.mybir as mybir
from concourse.bass_utils import run_bass_kernel_spmd

F32 = mybir.dt.float32
BF16 = mybir.dt.bfloat16
AF = mybir.ActivationFunctionType
ALU = mybir.AluOpType
AX = mybir.AxisListType

B, S, D = 4, 4096, 2048
H, HD, KVH = 16, 128, 4
NB = S // 128
NOB = 16
NIC = D // 128
EPS_LN = 1e-5
EPS_RMS = 1.1920929e-7
C23 = 1.5 * 2.0 ** 23
CDELTA = 1.5 * 2.0 ** 52 - 1.5 * 2.0 ** 23
SQ128 = float(np.sqrt(128.0))
TWO_PI = 2.0 * np.pi
INV2PI = 1.0 / TWO_PI
RC1 = 6.28125
RC2 = float(np.float64(TWO_PI) - RC1)
RC2A = float(np.float32(RC2))
RC2B = float(np.float64(RC2) - np.float64(np.float32(RC2)))
EXSC = float(-np.log(10000.0) / 128.0)
EXSCA = float(np.float32(EXSC))
EXSCB = float(np.float64(EXSC) - np.float64(np.float32(EXSC)))
PI_CLAMP = 3.1415925
LN1E4 = float(np.log(10000.0))

ENGS = ("tensor", "vector", "scalar", "sync", "gpsimd")
DMA_SEMS = {"sync": ["Yd%d" % i for i in range(6)], "scalar": ["Cd0"], "gpsimd": ["Gd0"]}
ALL_DMA_SEMS = [s for v in DMA_SEMS.values() for s in v]
ENG_SEM = {"tensor": "T", "vector": "V", "scalar": "C", "sync": "Y", "gpsimd": "G"}


def own_blocks(h):
    return sorted([2 * i + h for i in range(8)] + [16 + 2 * i + (1 - h) for i in range(8)])


class Sched:
    def __init__(self):
        self.items = {e: [] for e in ENGS}
        self.cnt = {s: 0 for s in list(ENG_SEM.values()) + ALL_DMA_SEMS}
        self.rr = {e: 0 for e in DMA_SEMS}

    def wait(self, eng, deps):
        for d in deps:
            if d is not None:
                self.items[eng].append(("w", d[0], d[1]))

    def emit(self, eng, kind, fn, deps):
        st = self.items[eng]
        for d in deps:
            if d is not None:
                st.append(("w", d[0], d[1]))
        if kind == "dma":
            sl = DMA_SEMS[eng]
            sem = sl[self.rr[eng] % len(sl)]
            self.rr[eng] += 1
            if self.cnt[sem] > 0:
                st.append(("w", sem, self.cnt[sem]))
            st.append((kind, fn, sem))
            self.cnt[sem] += 16
            return (sem, self.cnt[sem])
        st.append((kind, fn))
        sem = ENG_SEM[eng]
        self.cnt[sem] += 1
        return (sem, self.cnt[sem])

    def barrier(self):
        snap = dict(self.cnt)
        for e in ENGS:
            for s, v in snap.items():
                if v > 0:
                    self.items[e].append(("w", s, v))


class Buf:
    __slots__ = ("w", "rs")

    def __init__(self):
        self.w = None
        self.rs = []


S_ = None


def _emit(eng, kind, fn, r, w):
    deps = [b.w for b in r]
    for b in w:
        deps.append(b.w)
        deps.extend(b.rs)
    tok = S_.emit(eng, kind, fn, deps)
    for b in r:
        b.rs.append(tok)
    for b in w:
        b.w = tok
        b.rs = []
    return tok


def op(eng, fn, r=(), w=()):
    return _emit(eng, "op", fn, r, w)


def opg(eng, fns, r=(), w=()):
    return _emit(eng, "opg", fns, r, w)


def dma(eng, fn, r=(), w=()):
    return _emit(eng, "dma", fn, r, w)


def _rope_tables_host():
    try:
        import jax
        import jax.numpy as jnp
        cpu = jax.devices("cpu")[0]
        with jax.default_device(cpu):
            inv = 1.0 / 10000.0 ** (jnp.arange(0, 128, 2, dtype=jnp.float32) / 128)
            t = jnp.arange(S, dtype=jnp.float32)
            fr = jnp.outer(t, inv)
            cosv = np.asarray(jnp.cos(fr), dtype=np.float32)
            sinv = np.asarray(jnp.sin(fr), dtype=np.float32)
    except Exception:
        inv = (1.0 / (10000.0 ** (np.arange(0, 128, 2, dtype=np.float32) / np.float32(128)).astype(np.float64))).astype(np.float32)
        fr = (np.arange(S, dtype=np.float32)[:, None] * inv[None, :]).astype(np.float32)
        cosv = np.cos(fr).astype(np.float32)
        sinv = np.sin(fr).astype(np.float32)
    return (np.ascontiguousarray(cosv.reshape(NB, 128, 64)),
            np.ascontiguousarray(sinv.reshape(NB, 128, 64)))


def build_nc():
    global S_
    S_ = Sched()
    nc = bass.Bass("TRN2", target_bir_lowering=False)

    def din(name, shape, dt=F32):
        return nc.dram_tensor(name, list(shape), dt, kind="ExternalInput").ap()

    x_full = din("x_full", [S, D])
    x_own = din("x_own", [NOB * 128, D])
    wqkv_T = din("wqkv_T", [D, 3072])
    wqkv_R = din("wqkv_R", [3072, D])
    wo_T = din("wo_T", [D, D])
    wo_R = din("wo_R", [D, D])
    ffn_T = din("ffn_T", [D, D])
    ffn_R = din("ffn_R", [D, D])
    gains_h = din("gains", [1, H])
    idnb_h = din("idnb", [128, 128], BF16)
    idnf_h = din("idnf", [128, 128], F32)
    ones_r = din("ones_r", [1, 128])
    ones_c = din("ones_c", [128, 1])
    cosT_h = din("cosT_h", [NB, 128, 64])
    sinT_h = din("sinT_h", [NB, 128, 64])
    cosQ_h = din("cosQ_h", [NOB, 128, 64])
    sinQ_h = din("sinQ_h", [NOB, 128, 64])
    masks_h = din("masks", [32, 128, 128], BF16)

    out_h = nc.dram_tensor("out_shard", [NOB * 128, D], F32, kind="ExternalOutput").ap()
    sc_qt = nc.dram_tensor("sc_qt", [H, 128, NOB * 128], BF16, kind="Internal").ap()
    sc_g = nc.dram_tensor("sc_g", [NOB, 128, D], BF16, kind="Internal").ap()
    sc_woq = nc.dram_tensor("sc_woq", [NIC, 128, D], BF16, kind="Internal").ap()
    sc_xq = nc.dram_tensor("sc_xq", [NOB, NIC, 128, 128], BF16, kind="Internal").ap()

    es = contextlib.ExitStack()

    AB = es.enter_context(nc.sbuf_tensor("AB", [128, 72832], BF16))
    AFR = es.enter_context(nc.sbuf_tensor("AFR", [128, 15360], F32))

    def vb(arena, c0, n, p0=0, p1=128):
        return arena[p0:p1, c0:c0 + n], Buf()

    ktf, ktf_b = vb(AB, 0, KVH * S)
    vaug, vaug_b = vb(AB, 16384, KVH * NB * 129)
    wkvq, wkvq_b = vb(AB, 32896, NIC * 1024)
    xqs0, xqs0_b = vb(AB, 32896, NIC * 128)
    xqs1, xqs1_b = vb(AB, 34944, NIC * 128)
    gbf, gbf_b = vb(AB, 32896, 4 * D)
    wqg, wqg_b = vb(AB, 41088, NIC * 512)
    xq, xq_b = vb(AB, 49280, D)
    xqt_t, xqt_t_b = vb(AB, 51328, NIC * 128)
    kbf, kbf_b = vb(AB, 53376, 512)
    qt_t, qt_t_b = vb(AB, 49280, H * 512)
    yq, yq_b = vb(AB, 57472, D)
    e0, e0_b = vb(AB, 59520, 512)
    e1, e1_b = vb(AB, 60032, 512)
    yqt, yqt_b = vb(AB, 60544, 4 * NIC * 128)
    masksb, masksb_b = vb(AB, 68736, 32 * 128)

    wld0, wld0_b = vb(AFR, 0, 2048)
    wld1, wld1_b = vb(AFR, 2048, 2048)
    wtt0, wtt0_b = vb(AFR, 4096, 512)
    wtt1, wtt1_b = vb(AFR, 4608, 512)
    xt0, xt0_b = vb(AFR, 0, 2048)
    xt1, xt1_b = vb(AFR, 2048, 2048)
    xn, xn_b = vb(AFR, 5120, 2048)
    tq, tq_b = vb(AFR, 7168, 2048)
    kwork, kwork_b = vb(AFR, 9216, 512)
    kwork2, kwork2_b = vb(AFR, 9728, 512)
    wskv_rep, wskv_rep_b = vb(AFR, 10240, 1024)
    cosT, cosT_b = vb(AFR, 11264, NB * 64)
    sinT, sinT_b = vb(AFR, 13312, NB * 64)
    cosQ, cosQ_b = vb(AFR, 11264, NOB * 64)
    sinQ, sinQ_b = vb(AFR, 12288, NOB * 64)
    wsqd, wsqd_b = vb(AFR, 13312, 512)
    y_tm, y_tm_b = vb(AFR, 0, 4 * D)
    wso_rep, wso_rep_b = vb(AFR, 9216, D)
    sdqt, sdqt_b = vb(AFR, 11264, 512)
    xrt, xrt_b = vb(AFR, 11776, 512)
    outt, outt_b = vb(AFR, 12288, 512)
    gft, gft_b = vb(AFR, 12800, 512)
    tqT, tqT_b = vb(AFR, 13312, 2048)
    rowstg1, rowstg1_b = vb(AFR, 5120, 512, 0, 1)
    rowstgi1, rowstgi1_b = vb(AFR, 5632, 512, 0, 1)
    rowstg3, rowstg3_b = vb(AFR, 10240, 512, 0, 1)
    rowstgi3, rowstgi3_b = vb(AFR, 10752, 512, 0, 1)

    def sbf(name, p, f, dt=F32):
        return es.enter_context(nc.sbuf_tensor(name, [p, f], dt)), Buf()

    adeq, adeq_b = sbf("adeq", 128, NB)
    adeqo, adeqo_b = sbf("adeqo", 128, NOB)
    afo, afo_b = sbf("afo", 128, NOB)
    kk, kk_b = sbf("kk", 128, 1)
    ce, ce_b = sbf("ce", 128, 1)
    idn, idn_b = sbf("idn", 128, 128, BF16)
    idnf, idnf_b = sbf("idnf_s", 128, 128)
    onr, onr_b = sbf("onr", 1, 128)
    onc, onc_b = sbf("onc", 128, 1)
    mneg, mneg_b = sbf("mneg", 128, H)
    gn, gn_b = sbf("gn", 1, H)
    gnb, gnb_b = sbf("gnb", 128, H)
    wscol, wscol_b = sbf("wscol", 128, 24)
    colA, colA_b = sbf("colA", 128, 1)
    colB, colB_b = sbf("colB", 128, 1)
    colC, colC_b = sbf("colC", 128, 1)
    colS, colS_b = sbf("colS", 128, 4)
    adqy, adqy_b = sbf("adqy", 128, 4)
    sins, sins_b = sbf("sins", 128, 128)
    ang, ang_b = sbf("ang", 128, 64)
    ang2, ang2_b = sbf("ang2", 128, 64)
    angk, angk_b = sbf("angk", 128, 64)
    angt, angt_b = sbf("angt", 128, 64)
    tmpr, tmpr_b = sbf("tmpr", 128, 64)
    tmp2, tmp2_b = sbf("tmp2", 1, 16)
    tmp11, tmp11_b = sbf("tmp11", 1, 1)
    zcol, zcol_b = sbf("zcol", 128, 4 * H)
    zinv, zinv_b = sbf("zinv", 128, 4 * H)
    r127, r127_b = sbf("r127", 128, 1)
    rs128, rs128_b = sbf("rs128", 128, 1)
    wsf_c, wsf_c_b = sbf("wsf_c", 128, 1)
    wsfd, wsfd_b = sbf("wsfd", 128, 1)

    ps, ps_b = [], []
    for i in range(8):
        ps.append(es.enter_context(nc.psum_tensor("ps%d" % i, [128, 512], F32)))
        ps_b.append(Buf())

    with es:
        dma("sync", lambda e: e.dma_start(out=idn[:], in_=idnb_h[:]), w=[idn_b])
        dma("sync", lambda e: e.dma_start(out=idnf[:], in_=idnf_h[:]), w=[idnf_b])
        dma("sync", lambda e: e.dma_start(out=onr[:], in_=ones_r[:]), w=[onr_b])
        dma("sync", lambda e: e.dma_start(out=onc[:], in_=ones_c[:]), w=[onc_b])
        dma("sync", lambda e: e.dma_start(out=gn[:], in_=gains_h[:]), w=[gn_b])
        dma("sync", lambda e: e.dma_start(
            out=cosT.rearrange("p (nb d) -> p nb d", nb=NB), in_=cosT_h.rearrange("nb p d -> p nb d")), w=[cosT_b])
        dma("sync", lambda e: e.dma_start(
            out=sinT.rearrange("p (nb d) -> p nb d", nb=NB), in_=sinT_h.rearrange("nb p d -> p nb d")), w=[sinT_b])

        op("vector", lambda v: v.memset(kk[:], 1.0), w=[kk_b])
        op("vector", lambda v: v.tensor_scalar(kk[:], kk[:], 2.0 ** -25, None, ALU.add), r=[kk_b], w=[kk_b])
        op("vector", lambda v: v.tensor_scalar(kk[:], kk[:], -1.0, None, ALU.add), r=[kk_b], w=[kk_b])
        op("vector", lambda v: v.tensor_scalar(kk[:], kk[:], 2.0 ** 25, None, ALU.mult), r=[kk_b], w=[kk_b])
        op("vector", lambda v: v.tensor_scalar(ce[:], kk[:], CDELTA, C23, ALU.mult, ALU.add), r=[kk_b], w=[ce_b])

        op("vector", lambda v: v.memset(r127[:], 127.0), w=[r127_b])
        op("vector", lambda v: v.reciprocal(r127[:], r127[:]), r=[r127_b], w=[r127_b])
        op("vector", lambda v: v.memset(rs128[:], 128.0), w=[rs128_b])
        op("scalar", lambda c: c.sqrt(rs128[:], rs128[:]), r=[rs128_b], w=[rs128_b])
        op("vector", lambda v: v.reciprocal(rs128[:], rs128[:]), r=[rs128_b], w=[rs128_b])


        op("scalar", lambda c: c.activation(tmp2[:], gn[:], AF.Abs), r=[gn_b], w=[tmp2_b])
        op("tensor", lambda t: t.matmul(ps[6][:, 0:16], onr[:], tmp2[:], start=True, stop=True), r=[onr_b, tmp2_b], w=[ps_b[6]])
        op("vector", lambda v: v.tensor_scalar(mneg[:], ps[6][:, 0:16], -11.35, -2.0, ALU.mult, ALU.add), r=[ps_b[6]], w=[mneg_b])
        op("tensor", lambda t: t.matmul(ps[6][:, 0:16], onr[:], gn[:], start=True, stop=True), r=[onr_b, gn_b], w=[ps_b[6]])
        op("vector", lambda v: v.tensor_scalar(gnb[:], ps[6][:, 0:16], rs128[:], None, ALU.mult), r=[ps_b[6], rs128_b], w=[gnb_b])

        op("vector", lambda v: v.memset(vaug[:], 1.0), w=[vaug_b])

        def rowscale_pass(src_h, nchunks):
            for c in range(nchunks):
                wb, wbb = (wld0, wld0_b) if c % 2 == 0 else (wld1, wld1_b)
                dma("sync", lambda e, c=c, wb=wb: e.dma_start(
                    out=wb[:], in_=src_h[c * 128:(c + 1) * 128, :]), w=[wbb])
                op("vector", lambda v, wb=wb: v.reduce_max(colA[:], wb[:], axis=AX.X, apply_absolute_value=True), r=[wbb], w=[colA_b])
                op("vector", lambda v, c=c: v.tensor_scalar(wscol[:, c:c + 1], colA[:], r127[:], 1e-5, ALU.mult, ALU.max), r=[colA_b, r127_b], w=[wscol_b])

        def build_row(chunks, rstg, rstg_b, rstgi, rstgi_b):
            for i, c in enumerate(chunks):
                op("tensor", lambda t, c=c: t.matmul(ps[7][0:1, 0:128], wscol[:, c:c + 1], idnf[:], start=True, stop=True), r=[wscol_b, idnf_b], w=[ps_b[7]])
                op("vector", lambda v, i=i, rstg=rstg: v.tensor_copy(rstg[:, i * 128:(i + 1) * 128], ps[7][0:1, 0:128]), r=[ps_b[7]], w=[rstg_b])
            op("vector", lambda v, rstg=rstg, rstgi=rstgi: v.reciprocal(rstgi[:], rstg[:]), r=[rstg_b], w=[rstgi_b])

        def rep_to_ps(bank, row_t, row_bb):
            op("tensor", lambda t: t.matmul(ps[bank][:], onr[:], row_t[:], start=True, stop=True),
               r=[onr_b, row_bb], w=[ps_b[bank]])

        def quant_w_tile(srcT_h, col0, ic, dst_cs, repbank, buf, bufb, clip1=False, scalcol=None, scalcol_b=None):
            dma("sync", lambda e: e.dma_start(out=buf[:], in_=srcT_h[ic * 128:(ic + 1) * 128, col0:col0 + 512]), w=[bufb])
            if scalcol is not None:
                op("vector", lambda v: v.tensor_scalar(buf[:], buf[:], scalcol[:], None, ALU.mult), r=[bufb, scalcol_b], w=[bufb])
            else:
                op("vector", lambda v: v.tensor_tensor(buf[:], buf[:], ps[repbank][:], ALU.mult), r=[bufb, ps_b[repbank]], w=[bufb])
            op("vector", lambda v: v.tensor_scalar(buf[:], buf[:], ce[:], None, ALU.add), r=[bufb, ce_b], w=[bufb])
            if clip1:
                op("vector", lambda v: v.tensor_scalar(buf[:], buf[:], ce[:], None, ALU.subtract), r=[bufb, ce_b], w=[bufb])
                op("vector", lambda v: v.tensor_scalar(wqg[:, dst_cs], buf[:], 1.0, -1.0, ALU.min, ALU.max), r=[bufb], w=[wqg_b])
            else:
                op("vector", lambda v: v.tensor_scalar(wqg[:, dst_cs], buf[:], ce[:], None, ALU.subtract), r=[bufb, ce_b], w=[wqg_b])

        def ln_quant(src_ap, idx, adeq_t, adeq_bb, af_t, af_bb, xin, xin_b):
            dma("sync", lambda e: e.dma_start(out=xin[:], in_=src_ap), w=[xin_b])
            op("vector", lambda v: v.reduce_sum(colA[:], xin[:], axis=AX.X), r=[xin_b], w=[colA_b])
            op("vector", lambda v: v.tensor_scalar(colA[:], colA[:], 1.0 / D, None, ALU.mult), r=[colA_b], w=[colA_b])
            op("vector", lambda v: v.tensor_scalar(xn[:], xin[:], colA[:], None, ALU.subtract), r=[xin_b, colA_b], w=[xn_b])
            op("scalar", lambda c: c.activation(tq[:], xn[:], AF.Square), r=[xn_b], w=[tq_b])
            op("vector", lambda v: v.reduce_sum(colA[:], tq[:], axis=AX.X), r=[tq_b], w=[colA_b])
            op("vector", lambda v: v.tensor_scalar(colA[:], colA[:], 1.0 / D, EPS_LN, ALU.mult, ALU.add), r=[colA_b], w=[colA_b])
            op("scalar", lambda c: c.sqrt(colA[:], colA[:]), r=[colA_b], w=[colA_b])
            op("vector", lambda v: v.reciprocal(colA[:], colA[:]), r=[colA_b], w=[colA_b])
            op("vector", lambda v: v.tensor_scalar(xn[:], xn[:], colA[:], None, ALU.mult), r=[xn_b, colA_b], w=[xn_b])
            op("vector", lambda v: v.reduce_max(colA[:], xn[:], axis=AX.X, apply_absolute_value=True), r=[xn_b], w=[colA_b])
            if af_t is not None:
                op("vector", lambda v: v.tensor_scalar(af_t[:, idx:idx + 1], colA[:], 1e-8, None, ALU.max), r=[colA_b], w=[af_bb])
            op("vector", lambda v: v.tensor_scalar(colB[:], colA[:], r127[:], 1e-5, ALU.mult, ALU.max), r=[colA_b, r127_b], w=[colB_b])
            op("vector", lambda v: v.tensor_copy(adeq_t[:, idx:idx + 1], colB[:]), r=[colB_b], w=[adeq_bb])
            op("vector", lambda v: v.reciprocal(colB[:], colB[:]), r=[colB_b], w=[colB_b])
            op("vector", lambda v: v.tensor_scalar(tq[:], xn[:], colB[:], None, ALU.mult), r=[xn_b, colB_b], w=[tq_b])
            op("vector", lambda v: v.tensor_scalar(tq[:], tq[:], ce[:], None, ALU.add), r=[tq_b, ce_b], w=[tq_b])
            op("vector", lambda v: v.tensor_scalar(xq[:], tq[:], ce[:], None, ALU.subtract), r=[tq_b, ce_b], w=[xq_b])

        def transpose16(src_t, src_bb, dst_ap_fn, dst_bb, bank0=0, bank1=1):
            for g4 in range(4):
                bank = bank0 if g4 % 2 == 0 else bank1
                opg("tensor", [
                    (lambda t, c=g4 * 4 + i, i=i, bank=bank: t.matmul(
                        ps[bank][:, i * 128:(i + 1) * 128], src_t[:, c * 128:(c + 1) * 128], idn[:],
                        start=(i == 0), stop=(i == 3))) for i in range(4)],
                    r=[src_bb, idn_b], w=[ps_b[bank]])
                op("scalar", lambda c, g4=g4, bank=bank: c.copy(
                    dst_ap_fn(g4), ps[bank][:].rearrange("p (c t) -> p c t", c=4)), r=[ps_b[bank]], w=[dst_bb])

        def rope_tm(work_t, work_bb, nheads, cos_t, cos_bb, sin_t, sin_bb, blk, w2, w2b):
            cs = slice(blk * 64, blk * 64 + 64)
            nb_ = nheads * 128
            op("vector", lambda v: v.tensor_copy(sins[:, 0:64], sin_t[:, cs]), r=[sin_bb], w=[sins_b])
            op("vector", lambda v: v.tensor_scalar(sins[:, 64:128], sin_t[:, cs], -1.0, None, ALU.mult), r=[sin_bb], w=[sins_b])
            wv = lambda t_, a, b: t_[:, 0:nb_].rearrange("p (h x d) -> p h x d", h=nheads, x=2)[:, :, a:b, :]
            op("vector", lambda v: v.tensor_copy(wv(w2, 0, 1), wv(work_t, 1, 2)), r=[work_bb], w=[w2b])
            op("vector", lambda v: v.tensor_copy(wv(w2, 1, 2), wv(work_t, 0, 1)), r=[work_bb], w=[w2b])
            cosv = cos_t[:, cs].unsqueeze(1).unsqueeze(2).broadcast_to((128, nheads, 2, 64))
            sinv = sins[:, 0:128].unsqueeze(1).broadcast_to((128, nheads, 128))
            op("vector", lambda v: v.tensor_tensor(
                work_t[:, 0:nb_].rearrange("p (h x d) -> p h x d", h=nheads, x=2),
                work_t[:, 0:nb_].rearrange("p (h x d) -> p h x d", h=nheads, x=2), cosv, ALU.mult),
                r=[work_bb, cos_bb], w=[work_bb])
            op("vector", lambda v: v.tensor_tensor(
                w2[:, 0:nb_].rearrange("p (h d) -> p h d", h=nheads),
                w2[:, 0:nb_].rearrange("p (h d) -> p h d", h=nheads), sinv, ALU.mult),
                r=[w2b, sins_b], w=[w2b])
            op("vector", lambda v: v.tensor_tensor(work_t[:, 0:nb_], work_t[:, 0:nb_], w2[:, 0:nb_], ALU.add),
               r=[work_bb, w2b], w=[work_bb])

        rowscale_pass(wqkv_R, 24)
        for g in range(2):
            build_row(range(16 + g * 4, 16 + (g + 1) * 4), rowstg1, rowstg1_b, rowstgi1, rowstgi1_b)
            rep_to_ps(5, rowstg1, rowstg1_b)
            op("vector", lambda v, g=g: v.tensor_copy(wskv_rep[:, g * 512:(g + 1) * 512], ps[5][:]), r=[ps_b[5]], w=[wskv_rep_b])
        for og in range(2):
            col0 = 2048 + og * 512
            build_row(range(16 + og * 4, 16 + (og + 1) * 4), rowstg1, rowstg1_b, rowstgi1, rowstgi1_b)
            rep_to_ps(5, rowstgi1, rowstgi1_b)
            for ic in range(NIC):
                buf, bufb = (wtt0, wtt0_b) if ic % 2 == 0 else (wtt1, wtt1_b)
                dma("sync", lambda e, ic=ic, col0=col0, buf=buf: e.dma_start(
                    out=buf[:], in_=wqkv_T[ic * 128:(ic + 1) * 128, col0:col0 + 512]), w=[bufb])
                op("vector", lambda v, buf=buf: v.tensor_tensor(buf[:], buf[:], ps[5][:], ALU.mult), r=[bufb, ps_b[5]], w=[bufb])
                op("vector", lambda v, buf=buf: v.tensor_scalar(buf[:], buf[:], ce[:], None, ALU.add), r=[bufb, ce_b], w=[bufb])
                op("vector", lambda v, ic=ic, og=og, buf=buf: v.tensor_scalar(
                    wkvq[:, ic * 1024 + og * 512: ic * 1024 + (og + 1) * 512], buf[:], ce[:], None, ALU.subtract),
                    r=[bufb, ce_b], w=[wkvq_b])
        S_.barrier()

        for blk in range(NB):
            xin, xin_b = (xt0, xt0_b) if blk % 2 == 0 else (xt1, xt1_b)
            ln_quant(x_full[blk * 128:(blk + 1) * 128, :], blk, adeq, adeq_b, None, None, xin, xin_b)
            transpose16(xq, xq_b, lambda g4: xqt_t[:, g4 * 512:(g4 + 1) * 512].rearrange("p (c t) -> p c t", c=4), xqt_t_b, 0, 1)
            for og in range(2):
                bank = 2 + og
                opg("tensor", [
                    (lambda t, ic=ic, og=og, bank=bank: t.matmul(
                        ps[bank][:], xqt_t[:, ic * 128:(ic + 1) * 128],
                        wkvq[:, ic * 1024 + og * 512: ic * 1024 + (og + 1) * 512],
                        start=(ic == 0), stop=(ic == NIC - 1))) for ic in range(NIC)],
                    r=[xqt_t_b, wkvq_b], w=[ps_b[bank]])
                op("vector", lambda v, bank=bank, blk=blk: v.tensor_scalar(
                    kwork[:], ps[bank][:], adeq[:, blk:blk + 1], None, ALU.mult), r=[ps_b[bank], adeq_b], w=[kwork_b])
                op("vector", lambda v, og=og: v.tensor_tensor(
                    kwork[:], kwork[:], wskv_rep[:, og * 512:(og + 1) * 512], ALU.mult), r=[kwork_b, wskv_rep_b], w=[kwork_b])
                if og == 0:
                    op("scalar", lambda c: c.activation(kwork2[:], kwork[:], AF.Square), r=[kwork_b], w=[kwork2_b])
                    for kv in range(KVH):
                        op("vector", lambda v, kv=kv: v.reduce_sum(colS[:, kv:kv + 1], kwork2[:, kv * 128:(kv + 1) * 128], axis=AX.X), r=[kwork2_b], w=[colS_b])
                    op("vector", lambda v: v.tensor_scalar(colS[:], colS[:], 1.0 / 128.0, EPS_RMS, ALU.mult, ALU.add), r=[colS_b], w=[colS_b])
                    op("scalar", lambda c: c.sqrt(colS[:], colS[:]), r=[colS_b], w=[colS_b])
                    op("vector", lambda v: v.reciprocal(colS[:], colS[:]), r=[colS_b], w=[colS_b])
                    for kv in range(KVH):
                        op("vector", lambda v, kv=kv: v.tensor_scalar(
                            kwork[:, kv * 128:(kv + 1) * 128], kwork[:, kv * 128:(kv + 1) * 128],
                            colS[:, kv:kv + 1], None, ALU.mult), r=[kwork_b, colS_b], w=[kwork_b])
                    rope_tm(kwork, kwork_b, KVH, cosT, cosT_b, sinT, sinT_b, blk, kwork2, kwork2_b)
                    op("scalar", lambda c: c.copy(kbf[:], kwork[:]), r=[kwork_b], w=[kbf_b])
                    opg("tensor", [
                        (lambda t, kv=kv: t.matmul(
                            ps[4][:, kv * 128:(kv + 1) * 128], kbf[:, kv * 128:(kv + 1) * 128], idn[:],
                            start=(kv == 0), stop=(kv == 3))) for kv in range(KVH)],
                        r=[kbf_b, idn_b], w=[ps_b[4]])
                    op("scalar", lambda c, blk=blk: c.copy(
                        ktf.rearrange("p (kv s) -> p kv s", kv=KVH)[:, :, blk * 128:(blk + 1) * 128],
                        ps[4][:].rearrange("p (kv r) -> p kv r", kv=KVH)), r=[ps_b[4]], w=[ktf_b])
                else:
                    op("scalar", lambda c, blk=blk: c.copy(
                        vaug.rearrange("p (kv nb c) -> p kv nb c", kv=KVH, nb=NB)[:, :, blk:blk + 1, 0:128],
                        kwork[:].rearrange("p (kv o d) -> p kv o d", kv=KVH, o=1)), r=[kwork_b], w=[vaug_b])
        S_.barrier()

        dma("sync", lambda e: e.dma_start(
            out=cosQ.rearrange("p (nb d) -> p nb d", nb=NOB), in_=cosQ_h.rearrange("nb p d -> p nb d")), w=[cosQ_b])
        dma("sync", lambda e: e.dma_start(
            out=sinQ.rearrange("p (nb d) -> p nb d", nb=NOB), in_=sinQ_h.rearrange("nb p d -> p nb d")), w=[sinQ_b])
        for ot in range(NOB):
            xin, xin_b = (xt0, xt0_b) if ot % 2 == 0 else (xt1, xt1_b)
            ln_quant(x_own[ot * 128:(ot + 1) * 128, :], ot, adeqo, adeqo_b, afo, afo_b, xin, xin_b)
            transpose16(xq, xq_b, lambda g4: xqt_t[:, g4 * 512:(g4 + 1) * 512].rearrange("p (c t) -> p c t", c=4), xqt_t_b, 0, 1)
            dma("sync", lambda e, ot=ot: e.dma_start(
                out=sc_xq[ot].rearrange("ic p t -> p ic t"),
                in_=xqt_t[:].rearrange("p (ic t) -> p ic t", ic=NIC)), r=[xqt_t_b])
        for og in range(4):
            build_row(range(og * 4, (og + 1) * 4), rowstg3, rowstg3_b, rowstgi3, rowstgi3_b)
            rep_to_ps(5, rowstg3, rowstg3_b)
            op("vector", lambda v: v.tensor_copy(wsqd[:], ps[5][:]), r=[ps_b[5]], w=[wsqd_b])
            rep_to_ps(5, rowstgi3, rowstgi3_b)
            for ic in range(NIC):
                buf, bufb = (wtt0, wtt0_b) if ic % 2 == 0 else (wtt1, wtt1_b)
                quant_w_tile(wqkv_T, og * 512, ic, slice(ic * 512, (ic + 1) * 512), 5, buf, bufb)
            def q_chain(ot):
                bank = 2 + ot % 2
                xs, xs_b = (xqs0, xqs0_b) if ot % 2 == 0 else (xqs1, xqs1_b)
                dma("sync", lambda e, ot=ot, xs=xs: e.dma_start(
                    out=xs[:].rearrange("p (ic t) -> p ic t", ic=NIC),
                    in_=sc_xq[ot].rearrange("ic p t -> p ic t")), w=[xs_b])
                opg("tensor", [
                    (lambda t, ic=ic, bank=bank, xs=xs: t.matmul(
                        ps[bank][:], xs[:, ic * 128:(ic + 1) * 128],
                        wqg[:, ic * 512:(ic + 1) * 512],
                        start=(ic == 0), stop=(ic == NIC - 1))) for ic in range(NIC)],
                    r=[xs_b, wqg_b], w=[ps_b[bank]])

            q_chain(0)
            for ot in range(NOB):
                bank = 2 + ot % 2
                if ot + 1 < NOB:
                    q_chain(ot + 1)
                op("vector", lambda v, bank=bank, ot=ot: v.tensor_scalar(
                    kwork[:], ps[bank][:], adeqo[:, ot:ot + 1], None, ALU.mult), r=[ps_b[bank], adeqo_b], w=[kwork_b])
                op("vector", lambda v: v.tensor_tensor(kwork[:], kwork[:], wsqd[:], ALU.mult), r=[kwork_b, wsqd_b], w=[kwork_b])
                op("scalar", lambda c: c.activation(kwork2[:], kwork[:], AF.Square), r=[kwork_b], w=[kwork2_b])
                for i in range(4):
                    op("vector", lambda v, i=i: v.reduce_sum(colS[:, i:i + 1], kwork2[:, i * 128:(i + 1) * 128], axis=AX.X), r=[kwork2_b], w=[colS_b])
                op("vector", lambda v: v.tensor_scalar(colS[:], colS[:], 1.0 / 128.0, EPS_RMS, ALU.mult, ALU.add), r=[colS_b], w=[colS_b])
                op("scalar", lambda c: c.sqrt(colS[:], colS[:]), r=[colS_b], w=[colS_b])
                op("vector", lambda v: v.reciprocal(colS[:], colS[:]), r=[colS_b], w=[colS_b])
                op("vector", lambda v, og=og: v.tensor_tensor(colS[:], colS[:], gnb[:, og * 4:(og + 1) * 4], ALU.mult), r=[colS_b, gnb_b], w=[colS_b])
                for i in range(4):
                    op("vector", lambda v, i=i: v.tensor_scalar(
                        kwork[:, i * 128:(i + 1) * 128], kwork[:, i * 128:(i + 1) * 128],
                        colS[:, i:i + 1], None, ALU.mult), r=[kwork_b, colS_b], w=[kwork_b])
                rope_tm(kwork, kwork_b, 4, cosQ, cosQ_b, sinQ, sinQ_b, ot, kwork2, kwork2_b)
                op("scalar", lambda c: c.copy(kbf[:], kwork[:]), r=[kwork_b], w=[kbf_b])
                opg("tensor", [
                    (lambda t, i=i: t.matmul(
                        ps[4][:, i * 128:(i + 1) * 128], kbf[:, i * 128:(i + 1) * 128], idn[:],
                        start=(i == 0), stop=(i == 3))) for i in range(4)],
                    r=[kbf_b, idn_b], w=[ps_b[4]])
                op("vector", lambda v: v.tensor_copy(kbf[:], ps[4][:]), r=[ps_b[4]], w=[kbf_b])
                dma("sync", lambda e, og=og, ot=ot: e.dma_start(
                    out=sc_qt[og * 4:(og + 1) * 4, :, ot * 128:(ot + 1) * 128].rearrange("h p t -> p h t"),
                    in_=kbf[:].rearrange("p (h t) -> p h t", h=4)), r=[kbf_b])
        S_.barrier()

        op("vector", lambda v: v.memset(colB[:], 0.0), w=[colB_b])
        for c in range(NIC):
            wb, wbb = (wld0, wld0_b) if c % 2 == 0 else (wld1, wld1_b)
            dma("sync", lambda e, c=c, wb=wb: e.dma_start(out=wb[:], in_=ffn_R[c * 128:(c + 1) * 128, :]), w=[wbb])
            op("vector", lambda v, wb=wb: v.reduce_sum(colA[:], wb[:], axis=AX.X, apply_absolute_value=True), r=[wbb], w=[colA_b])
            op("vector", lambda v: v.tensor_tensor(colB[:], colB[:], colA[:], ALU.add), r=[colB_b, colA_b], w=[colB_b])
        op("tensor", lambda t: t.matmul(ps[7][0:1, 0:1], colB[:], onc[:], start=True, stop=True), r=[colB_b, onc_b], w=[ps_b[7]])
        op("vector", lambda v: v.tensor_copy(tmp11[:], ps[7][0:1, 0:1]), r=[ps_b[7]], w=[tmp11_b])
        op("vector", lambda v: v.tensor_scalar(tmp11[:], tmp11[:], 1.0 / (D * D), 1e-6, ALU.mult, ALU.max), r=[tmp11_b], w=[tmp11_b])
        op("tensor", lambda t: t.matmul(ps[7][:, 0:1], onr[:], tmp11[:], start=True, stop=True), r=[onr_b, tmp11_b], w=[ps_b[7]])
        op("vector", lambda v: v.tensor_scalar(wsfd[:], ps[7][:, 0:1], r127[:], None, ALU.mult), r=[ps_b[7], r127_b], w=[wsfd_b])
        op("vector", lambda v: v.reciprocal(wsf_c[:], ps[7][:, 0:1]), r=[ps_b[7]], w=[wsf_c_b])
        for og in range(4):
            for ic in range(NIC):
                buf, bufb = (wtt0, wtt0_b) if ic % 2 == 0 else (wtt1, wtt1_b)
                quant_w_tile(ffn_T, og * 512, ic, slice(ic * 512, (ic + 1) * 512), None, buf, bufb,
                             clip1=True, scalcol=wsf_c, scalcol_b=wsf_c_b)
            def f_chain(ot):
                bank = 2 + ot % 2
                xs, xs_b = (xqs0, xqs0_b) if ot % 2 == 0 else (xqs1, xqs1_b)
                dma("sync", lambda e, ot=ot, xs=xs: e.dma_start(
                    out=xs[:].rearrange("p (ic t) -> p ic t", ic=NIC),
                    in_=sc_xq[ot].rearrange("ic p t -> p ic t")), w=[xs_b])
                opg("tensor", [
                    (lambda t, ic=ic, bank=bank, xs=xs: t.matmul(
                        ps[bank][:], xs[:, ic * 128:(ic + 1) * 128],
                        wqg[:, ic * 512:(ic + 1) * 512],
                        start=(ic == 0), stop=(ic == NIC - 1))) for ic in range(NIC)],
                    r=[xs_b, wqg_b], w=[ps_b[bank]])

            f_chain(0)
            for ot in range(NOB):
                bank = 2 + ot % 2
                if ot + 1 < NOB:
                    f_chain(ot + 1)
                op("vector", lambda v, ot=ot: v.tensor_tensor(colC[:], afo[:, ot:ot + 1], wsfd[:], ALU.mult), r=[afo_b, wsfd_b], w=[colC_b])
                op("scalar", lambda c, bank=bank: c.activation(kbf[:], ps[bank][:], AF.Silu, scale=colC[:]), r=[ps_b[bank], colC_b], w=[kbf_b])
                dma("sync", lambda e, ot=ot, og=og: e.dma_start(
                    out=sc_g[ot, :, og * 512:(og + 1) * 512], in_=kbf[:]), r=[kbf_b])
        S_.barrier()

        rowscale_pass(wo_R, NIC)
        for og in range(4):
            build_row(range(og * 4, (og + 1) * 4), rowstg1, rowstg1_b, rowstgi1, rowstgi1_b)
            rep_to_ps(5, rowstgi1, rowstgi1_b)
            for ic in range(NIC):
                buf, bufb = (wtt0, wtt0_b) if ic % 2 == 0 else (wtt1, wtt1_b)
                quant_w_tile(wo_T, og * 512, ic, slice(ic * 512, (ic + 1) * 512), 5, buf, bufb)
            dma("sync", lambda e, og=og: e.dma_start(
                out=sc_woq[:, :, og * 512:(og + 1) * 512].rearrange("ic p d -> p ic d"),
                in_=wqg[:].rearrange("p (ic d) -> p ic d", ic=NIC)), r=[wqg_b])
            rep_to_ps(5, rowstg1, rowstg1_b)
            op("vector", lambda v, og=og: v.tensor_copy(wso_rep[:, og * 512:(og + 1) * 512], ps[5][:]), r=[ps_b[5]], w=[wso_rep_b])
        S_.barrier()

        dma("sync", lambda e: e.dma_start(
            out=masksb.rearrange("p (m q) -> p m q", m=32), in_=masks_h.rearrange("m p q -> p m q")), w=[masksb_b])
        for qt in range(4):
            KMAX = 8 * qt + 8
            dma("sync", lambda e, qt=qt: e.dma_start(
                out=qt_t[:].rearrange("p (h t) -> p h t", h=H),
                in_=sc_qt[:, :, qt * 512:(qt + 1) * 512].rearrange("h p t -> p h t")), w=[qt_t_b])
            for h in range(H):
                kv = h // 4

                def st_mm(kc, kv=kv, h=h):
                    stb = 4 + (kc % 2)
                    op("tensor", lambda t, kc=kc, kv=kv, h=h, stb=stb: t.matmul(
                        ps[stb][:], ktf[:, kv * S + kc * 128: kv * S + (kc + 1) * 128],
                        qt_t[:, h * 512:(h + 1) * 512], start=True, stop=True),
                        r=[ktf_b, qt_t_b], w=[ps_b[stb]])

                st_mm(0)
                for kc in range(KMAX):
                    stb = 4 + (kc % 2)
                    if kc + 1 < KMAX:
                        st_mm(kc + 1)
                    e_t, e_b = (e0, e0_b) if kc % 2 == 0 else (e1, e1_b)
                    op("scalar", lambda c, h=h, stb=stb, e_t=e_t: c.activation(
                        e_t[:], ps[stb][:], AF.Exp, bias=mneg[:, h:h + 1]), r=[ps_b[stb], mneg_b], w=[e_b])
                    mmfns = []
                    wbanks = []
                    for j in range(4):
                        dd = kc - (8 * qt + 2 * j)
                        if dd > 1:
                            continue
                        if dd >= 0:
                            m_idx = (qt * 4 + j) * 2 + dd
                            op("vector", lambda v, j=j, m_idx=m_idx, e_t=e_t: v.tensor_tensor(
                                e_t[:, j * 128:(j + 1) * 128], e_t[:, j * 128:(j + 1) * 128],
                                masksb[:, m_idx * 128:(m_idx + 1) * 128], ALU.mult), r=[masksb_b, e_b], w=[e_b])
                        last_kc = min(8 * qt + 2 * j + 1, KMAX - 1)
                        mmfns.append(lambda t, j=j, kc=kc, kv=kv, e_t=e_t, last_kc=last_kc: t.matmul(
                            ps[j][:, 0:129], e_t[:, j * 128:(j + 1) * 128],
                            vaug[:, (kv * NB + kc) * 129: (kv * NB + kc + 1) * 129],
                            start=(kc == 0), stop=(kc == last_kc)))
                        wbanks.append(ps_b[j])
                    opg("tensor", mmfns, r=[e_b, vaug_b], w=wbanks)
                for j in range(4):
                    op("scalar", lambda c, j=j, h=h: c.copy(
                        y_tm[:, j * D + h * 128: j * D + (h + 1) * 128], ps[j][:, 0:128]), r=[ps_b[j]], w=[y_tm_b])
                    op("scalar", lambda c, j=j, h=h: c.copy(
                        zcol[:, j * H + h: j * H + h + 1], ps[j][:, 128:129]), r=[ps_b[j]], w=[zcol_b])
            for j in range(4):
                op("vector", lambda v, j=j: v.reciprocal(zinv[:, j * H:(j + 1) * H], zcol[:, j * H:(j + 1) * H]), r=[zcol_b], w=[zinv_b])
                for h in range(H):
                    op("vector", lambda v, j=j, h=h: v.tensor_scalar(
                        y_tm[:, j * D + h * 128: j * D + (h + 1) * 128],
                        y_tm[:, j * D + h * 128: j * D + (h + 1) * 128],
                        zinv[:, j * H + h: j * H + h + 1], None, ALU.mult), r=[zinv_b, y_tm_b], w=[y_tm_b])
                op("vector", lambda v, j=j: v.reduce_max(colA[:], y_tm[:, j * D:(j + 1) * D], axis=AX.X, apply_absolute_value=True), r=[y_tm_b], w=[colA_b])
                op("vector", lambda v: v.tensor_scalar(colB[:], colA[:], r127[:], 1e-5, ALU.mult, ALU.max), r=[colA_b, r127_b], w=[colB_b])
                op("vector", lambda v, j=j: v.tensor_copy(adqy[:, j:j + 1], colB[:]), r=[colB_b], w=[adqy_b])
                op("vector", lambda v: v.reciprocal(colB[:], colB[:]), r=[colB_b], w=[colB_b])
                op("vector", lambda v, j=j: v.tensor_scalar(tqT[:], y_tm[:, j * D:(j + 1) * D], colB[:], None, ALU.mult), r=[y_tm_b, colB_b], w=[tqT_b])
                op("vector", lambda v: v.tensor_scalar(tqT[:], tqT[:], ce[:], None, ALU.add), r=[tqT_b, ce_b], w=[tqT_b])
                op("vector", lambda v: v.tensor_scalar(yq[:], tqT[:], ce[:], None, ALU.subtract), r=[tqT_b, ce_b], w=[yq_b])
                transpose16(yq, yq_b, lambda g4, j=j: yqt[:, j * D + g4 * 512: j * D + (g4 + 1) * 512].rearrange("p (c t) -> p c t", c=4), yqt_b, 6, 7)
            for j in range(4):
                ot = qt * 4 + j
                dma("sync", lambda e, ot=ot, j=j: e.dma_start(
                    out=gbf[:, j * D:(j + 1) * D], in_=sc_g[ot, :, :]), w=[gbf_b])
            for og in range(4):
                dma("sync", lambda e, og=og: e.dma_start(
                    out=wqg[:].rearrange("p (ic d) -> p ic d", ic=NIC),
                    in_=sc_woq[:, :, og * 512:(og + 1) * 512].rearrange("ic p d -> p ic d")), w=[wqg_b])
                for j in range(4):
                    ot = qt * 4 + j
                    bank = 6 + (j % 2)
                    opg("tensor", [
                        (lambda t, j=j, ic=ic, bank=bank: t.matmul(
                            ps[bank][:], yqt[:, j * D + ic * 128: j * D + (ic + 1) * 128],
                            wqg[:, ic * 512:(ic + 1) * 512],
                            start=(ic == 0), stop=(ic == NIC - 1))) for ic in range(NIC)],
                        r=[yqt_b, wqg_b], w=[ps_b[bank]])
                    op("vector", lambda v, j=j, bank=bank: v.tensor_scalar(
                        sdqt[:], ps[bank][:], adqy[:, j:j + 1], None, ALU.mult), r=[ps_b[bank], adqy_b], w=[sdqt_b])
                    op("vector", lambda v, og=og: v.tensor_tensor(
                        sdqt[:], sdqt[:], wso_rep[:, og * 512:(og + 1) * 512], ALU.mult), r=[wso_rep_b, sdqt_b], w=[sdqt_b])
                    dma("sync", lambda e, ot=ot, og=og: e.dma_start(
                        out=xrt[:], in_=x_own[ot * 128:(ot + 1) * 128, og * 512:(og + 1) * 512]), w=[xrt_b])
                    op("scalar", lambda c, j=j, og=og: c.copy(
                        gft[:], gbf[:, j * D + og * 512: j * D + (og + 1) * 512]), r=[gbf_b], w=[gft_b])
                    op("vector", lambda v: v.tensor_tensor(outt[:], sdqt[:], gft[:], ALU.mult), r=[sdqt_b, gft_b], w=[outt_b])
                    op("vector", lambda v: v.tensor_tensor(outt[:], outt[:], xrt[:], ALU.add), r=[outt_b, xrt_b], w=[outt_b])
                    dma("sync", lambda e, ot=ot, og=og: e.dma_start(
                        out=out_h[ot * 128:(ot + 1) * 128, og * 512:(og + 1) * 512], in_=outt[:]), r=[outt_b])

        S_.wait("sync", [(s, S_.cnt[s]) for s in ALL_DMA_SEMS if S_.cnt[s] > 0])

        sems = {}
        for name in list(ENG_SEM.values()) + ALL_DMA_SEMS:
            sems[name] = es.enter_context(nc.semaphore(name))
        block = es.enter_context(nc.Block())

        def mk(engname):
            items = S_.items[engname]
            osem = ENG_SEM[engname]

            def body(proxy):
                waited = {}
                for it in items:
                    if it[0] == "w":
                        _, s, v = it
                        if waited.get(s, -1) >= v:
                            continue
                        proxy.wait_ge(sems[s], v)
                        waited[s] = v
                    elif it[0] == "op":
                        it[1](proxy).then_inc(sems[osem], 1)
                    elif it[0] == "opg":
                        for f in it[1][:-1]:
                            f(proxy)
                        it[1][-1](proxy).then_inc(sems[osem], 1)
                    else:
                        it[1](proxy).then_inc(sems[it[2]], 16)
            return body

        block.tensor(mk("tensor"))
        block.vector(mk("vector"))
        block.scalar(mk("scalar"))
        block.sync(mk("sync"))
        block.gpsimd(mk("gpsimd"))

    return nc


def kernel(L0_attn_x, L0_attn_w_q, L0_attn_w_k, L0_attn_w_v, L0_attn_w_o,
           L0_attn_q_gain, L0_ffn_weight):
    x = np.ascontiguousarray(np.asarray(L0_attn_x, dtype=np.float32))
    wq = np.asarray(L0_attn_w_q, dtype=np.float32)
    wk = np.asarray(L0_attn_w_k, dtype=np.float32)
    wv = np.asarray(L0_attn_w_v, dtype=np.float32)
    wo = np.asarray(L0_attn_w_o, dtype=np.float32)
    gain = np.asarray(L0_attn_q_gain, dtype=np.float32)
    wf = np.asarray(L0_ffn_weight, dtype=np.float32)

    wqkv_R = np.ascontiguousarray(np.concatenate([wq, wk, wv], axis=0))
    wqkv_T = np.ascontiguousarray(wqkv_R.T)
    wo_T = np.ascontiguousarray(wo.T)
    ffn_T = np.ascontiguousarray(wf.T)

    idnb = np.eye(128, dtype=ml_dtypes.bfloat16)
    idnf = np.eye(128, dtype=np.float32)
    ones_r = np.ones((1, 128), dtype=np.float32)
    ones_c = np.ones((128, 1), dtype=np.float32)
    cosT_t, sinT_t = _rope_tables_host()

    in_maps = []
    block_maps = []
    for c in range(8):
        b, h = c // 2, c % 2
        obs = own_blocks(h)
        block_maps.append(obs)
        xb = np.ascontiguousarray(x[b])
        x_own = np.ascontiguousarray(xb.reshape(NB, 128, D)[obs].reshape(NOB * 128, D))
        tq_own = np.asarray([g * 128 + i for g in obs for i in range(128)], dtype=np.int64)
        cosQ_t = np.ascontiguousarray(cosT_t.reshape(S, 64)[tq_own].reshape(NOB, 128, 64))
        sinQ_t = np.ascontiguousarray(sinT_t.reshape(S, 64)[tq_own].reshape(NOB, 128, 64))
        masks = np.zeros((32, 128, 128), dtype=ml_dtypes.bfloat16)
        tri = np.triu(np.ones((128, 128), dtype=np.float32))
        for t in range(4):
            for j in range(4):
                g = obs[t * 4 + j]
                for slot in range(2):
                    kc = 8 * t + 2 * j + slot
                    if kc < g:
                        m = np.ones((128, 128), dtype=np.float32)
                    elif kc == g:
                        m = tri
                    else:
                        m = np.zeros((128, 128), dtype=np.float32)
                    masks[(t * 4 + j) * 2 + slot] = m.astype(ml_dtypes.bfloat16)
        in_maps.append({
            "x_full": xb, "x_own": x_own,
            "wqkv_T": wqkv_T, "wqkv_R": wqkv_R, "wo_T": wo_T, "wo_R": wo,
            "ffn_T": ffn_T, "ffn_R": wf,
            "gains": gain.reshape(1, H), "idnb": idnb, "idnf": idnf,
            "ones_r": ones_r, "ones_c": ones_c,
            "cosT_h": cosT_t, "sinT_h": sinT_t, "cosQ_h": cosQ_t, "sinQ_h": sinQ_t,
            "masks": masks,
        })

    nc = build_nc()
    res = run_bass_kernel_spmd(nc, in_maps, core_ids=list(range(8)))

    out = np.empty((B, S, D), dtype=np.float32)
    for c in range(8):
        b = c // 2
        shard = np.asarray(res.results[c]["out_shard"]).reshape(NOB, 128, D)
        for i, g in enumerate(block_maps[c]):
            out[b, g * 128:(g + 1) * 128, :] = shard[i]
    return out
